# version 01 (final)
# baseline (speedup 1.0000x reference)
import math
from contextlib import ExitStack

import numpy as np
import ml_dtypes

import concourse.bass as bass
import concourse.mybir as mybir
from concourse.bass_utils import run_bass_kernel_spmd
from concourse.aot_env import allow_ici_all_to_all

BF16 = mybir.dt.bfloat16
F32 = mybir.dt.float32
AF = mybir.ActivationFunctionType
ALU = mybir.AluOpType
bfdt = ml_dtypes.bfloat16

R = 8
T = 8192
DM = 2048
TC = 1024
NCH = 16
INTER = 8192
EPS = 1e-5
SM_SCALE = 1.0 / math.sqrt(128.0)



class Ctx:
    def __init__(self):
        self.semh = {}
        self.semv = {}
        self.hw = {"TE": {}, "VE": {}, "SE": {}, "SY": {}, "GP": {}}
        self.ev = {}
        self.ev_bank = {}
        self.slab_ev = [None, None, None, None]
        self.slab_i = 0
        self.rope_i = 0
        self.rope_ev = {}
        self.dsi = 0

    def next_dsem(self):
        d = f"ds{self.dsi % 4}"
        self.dsi += 1
        return d

    def add_sem(self, name, handle):
        self.semh[name] = handle
        self.semv[name] = 0

    def inc(self, instr, s, n=1):
        self.semv[s] += n
        instr.then_inc(self.semh[s], n)
        return (s, self.semv[s])

    def wait(self, engname, eng, ev):
        if ev is None:
            return
        s, v = ev
        if v <= 0:
            return
        if self.hw[engname].get(s, 0) >= v:
            return
        eng.wait_ge(self.semh[s], v)
        self.hw[engname][s] = v

    def now(self, s):
        return (s, self.semv[s])

    def seal_banks(self, ev):
        for i in range(7):
            self.ev_bank[i] = ev


class BankPool:
    def __init__(self, ctx):
        self.ctx = ctx
        self.i = 0

    def acquire(self, te):
        b = self.i % 7
        self.i += 1
        self.ctx.wait("TE", te, self.ctx.ev_bank.get(b))
        return b

    def release(self, b, ev):
        self.ctx.ev_bank[b] = ev


def build_program(upto=4, debug_wts=False):
    nc = bass.Bass("TRN2", target_bir_lowering=False)
    D = {}

    def din(name, shape, dt):
        D[name] = nc.dram_tensor(name, shape, dt, kind="ExternalInput")

    din("x_fm", [DM, TC], BF16)
    din("Wq", [DM, DM], BF16)
    din("Wk", [DM, 512], BF16)
    din("Wv", [DM, 512], BF16)
    din("Wo", [DM, DM], BF16)
    din("bqr", [1, 2048], BF16)
    din("bkr", [1, 512], BF16)
    din("bvr", [1, 512], BF16)
    din("rpmc", [128, 128], BF16)
    din("wqT_sh", [DM, 256], F32)
    din("wkT_sh", [DM, 64], F32)
    din("wvT_sh", [DM, 64], F32)
    din("woT_sh", [DM, 256], F32)
    din("wupT_sh", [DM, 1024], F32)
    din("wgT_sh", [DM, 1024], F32)
    din("wdT_sh", [INTER, 256], F32)
    din("su_ug2", [128, 16], F32)
    din("su_dp2", [128, 64], F32)
    din("wsup2", [128, 64], F32)
    din("wsg2", [128, 64], F32)
    din("wsdp2", [128, 16], F32)
    din("Hpm", [DM, DM], BF16)
    din("H64c", [64, 64], BF16)
    din("H128c", [128, 128], BF16)
    din("identc", [128, 128], BF16)
    din("onesc", [128, 128], BF16)
    din("maskcc", [128, 896], BF16)
    din("ra_cos", [128, TC], BF16)
    din("ra_sin", [128, TC], BF16)
    din("rb_cos", [128, TC], BF16)
    din("rb_sin", [128, TC], BF16)
    din("diagTc", [64, 128], F32)
    din("qdecc", [128, 128], BF16)
    din("kdecc", [128, 2], F32)
    din("blkvc", [128, 2], F32)
    D["out_x"] = nc.dram_tensor("out_x", [128, NCH * TC], F32, kind="ExternalOutput")
    if debug_wts:
        D["dbg_ug"] = nc.dram_tensor("dbg_ug", [R * DM, 2048], BF16, kind="ExternalOutput")
        D["dbg_wd"] = nc.dram_tensor("dbg_wd", [R * INTER, 256], BF16, kind="ExternalOutput")
        D["dbg_wb"] = nc.dram_tensor("dbg_wb", [R * DM, 640], BF16, kind="ExternalOutput")
        D["dbg_wdi"] = nc.dram_tensor("dbg_wdi", [INTER, 256], BF16, kind="ExternalOutput")
        D["dbg_xh"] = nc.dram_tensor("dbg_xh", [128, NCH * TC], BF16, kind="ExternalOutput")
        D["dbg_z1"] = nc.dram_tensor("dbg_z1", [128, 2 * NCH * TC], BF16, kind="ExternalOutput")

    for name, shape, dt in [
        ("ug_in", [DM, 2048], BF16), ("ug_out", [R * DM, 2048], BF16),
        ("wb_in", [DM, 640], BF16), ("wb_out", [R * DM, 640], BF16),
        ("wdi", [INTER, 256], BF16), ("wd_in", [INTER, 256], BF16),
        ("wd_out", [R * INTER, 256], BF16),
        ("qk_in", [R * 384, TC], BF16), ("qk_out", [R * 384, TC], BF16),
        ("v_in", [R * TC, 128], BF16), ("v_out", [R * TC, 128], BF16),
        ("o_in", [R * 256, TC], BF16), ("o_out", [R * 256, TC], BF16),
        ("qk2_in", [R * 384, TC], BF16), ("qk2_out", [R * 384, TC], BF16),
        ("ktv_in", [R * TC, 256], BF16), ("ktv_out", [R * TC, 256], BF16),
        ("o2_in", [R * 256, TC], BF16), ("o2_out", [R * 256, TC], BF16),
    ]:
        D[name] = nc.dram_tensor(name, shape, dt)

    with ExitStack() as st:
        def sbt(name, shape, dt):
            D[name] = st.enter_context(nc.sbuf_tensor(name, shape, dt))

        sbt("x32", [128, NCH * TC], F32)
        sbt("qbuf", [128, NCH * TC], BF16)
        sbt("u1", [128, 2 * NCH * TC], BF16)
        sbt("wstream", [128, 4 * 512], BF16)
        sbt("t1", [128, 512], F32)
        sbt("t2", [128, 512], F32)
        sbt("tg", [128, 512], F32)
        sbt("ptile", [128, 3 * 512], BF16)
        sbt("mrow", [1, 512], F32)
        sbt("vrow", [1, 512], F32)
        sbt("wrow", [1, 512], F32)
        sbt("mr_row", [1, 2 * 512], BF16)
        sbt("rlrow", [1, 512], F32)
        sbt("epsrow", [1, 1], F32)
        sbt("rlrow_bf", [1, 512], BF16)
        sbt("ropec", [128, TC], BF16)
        sbt("ropes", [128, TC], BF16)
        sbt("maskt", [128, 896], BF16)
        sbt("diagt", [64, 128], F32)
        sbt("qdect", [128, 128], BF16)
        sbt("kdect", [128, 2], F32)
        sbt("blkvt", [128, 2], F32)
        sbt("identt", [128, 128], BF16)
        sbt("onest", [128, 128], BF16)
        sbt("h64t", [64, 64], BF16)
        sbt("h128t", [128, 128], BF16)
        sbt("sut", [128, 16], F32)
        sbt("suwut", [128, 64], F32)
        sbt("wsupt", [128, 64], F32)
        sbt("wsgt", [128, 64], F32)
        sbt("wsdpt", [128, 16], F32)
        sbt("bqrt", [1, 2048], BF16)
        sbt("bkrt", [1, 512], BF16)
        sbt("bvrt", [1, 512], BF16)
        sbt("rpmt", [128, 128], BF16)
        sbt("kvf0", [128, 128], F32)
        sbt("kvf1", [128, 128], F32)
        sbt("kvb0", [128, 2 * 128], BF16)
        sbt("kvb1", [128, 2 * 128], BF16)
        sbt("smt", [128, 2 * 64], BF16)
        sbt("tmq", [128, 2 * 64], BF16)
        sbt("ktmd", [128, 2 * 8 * 128], BF16)
        sbt("fold_a", [64, 2 * 512], BF16)

        pf = [st.enter_context(nc.psum_tensor(f"pf{i}", [128, 512], F32))
              for i in range(7)]
        ptp = st.enter_context(nc.psum_tensor("ptp", [128, 1024], BF16))
        D["pf"] = pf
        D["ptp"] = ptp

        semnames = ["d0", "d1", "d2", "d3", "d4", "d5",
                    "ds0", "ds1", "ds2", "ds3", "fs",
                    "fa0", "fa1", "fa2", "fa3", "sa0", "sa1", "sa2", "sa3",
                    "sg0", "sg1", "g0", "g1", "pe", "ve", "se", "cc"]
        for n in semnames:
            D["sem_" + n] = st.enter_context(nc.semaphore(n))

        block = st.enter_context(nc.Block())
        ctx = Ctx()
        for n in semnames:
            ctx.add_sem(n, D["sem_" + n])

        _emit(ctx, block, D, upto, debug_wts)

    return nc


def _emit(ctx, block, D, upto, debug_wts=False):
    c = ctx
    pf = D["pf"]
    ptp = D["ptp"]
    rot = BankPool(ctx)

    def ap(name):
        return D[name]

    def cap(name):
        return D[name].ap().opt()

    def p0_sync(sy):
        def ld(dst, src):
            return c.inc(sy.dma_start(out=dst, in_=src), "d0", 16)

        ld(ap("identt")[:, :], ap("identc")[:, :])
        ld(ap("onest")[:, :], ap("onesc")[:, :])
        ld(ap("h64t")[:, :], ap("H64c")[:, :])
        ld(ap("h128t")[:, :], ap("H128c")[:, :])
        ld(ap("maskt")[:, :], ap("maskcc")[:, :])
        ld(ap("ropec")[:, :], ap("ra_cos")[:, :])
        ld(ap("ropes")[:, :], ap("ra_sin")[:, :])
        ld(ap("diagt")[:, :], ap("diagTc")[:, :])
        ld(ap("qdect")[:, :], ap("qdecc")[:, :])
        ld(ap("kdect")[:, :], ap("kdecc")[:, :])
        ld(ap("blkvt")[:, :], ap("blkvc")[:, :])
        ld(ap("sut")[:, :], ap("su_ug2")[:, :])
        ld(ap("suwut")[:, :], ap("su_dp2")[:, :])
        ld(ap("wsupt")[:, :], ap("wsup2")[:, :])
        ld(ap("wsgt")[:, :], ap("wsg2")[:, :])
        ld(ap("wsdpt")[:, :], ap("wsdp2")[:, :])
        ld(ap("bqrt")[:, :], ap("bqr")[:, :])
        ld(ap("bkrt")[:, :], ap("bkr")[:, :])
        ld(ap("bvrt")[:, :], ap("bvr")[:, :])
        ld(ap("rpmt")[:, :], ap("rpmc")[:, :])
        for ch in range(NCH):
            ld(ap("u1")[:, 8192 + ch * TC: 8192 + (ch + 1) * TC],
               ap("x_fm")[ch * 128:(ch + 1) * 128, :])
        ctx.ev["p0_loads"] = c.now("d0")

    block.sync(p0_sync)

    def p0_gp(gp):
        for lc in range(8):
            c.inc(gp.dma_start(out=ap("ug_in")[:, lc * 256: lc * 256 + 128],
                               in_=ap("wupT_sh")[:, lc * 128:(lc + 1) * 128]), "g0", 16)
            c.inc(gp.dma_start(out=ap("ug_in")[:, lc * 256 + 128: lc * 256 + 256],
                               in_=ap("wgT_sh")[:, lc * 128:(lc + 1) * 128]), "g0", 16)
        c.inc(gp.dma_start(out=ap("wb_in")[:, 0:256], in_=ap("wqT_sh")[:, :]), "g0", 16)
        c.inc(gp.dma_start(out=ap("wb_in")[:, 256:320], in_=ap("wkT_sh")[:, :]), "g0", 16)
        c.inc(gp.dma_start(out=ap("wb_in")[:, 320:384], in_=ap("wvT_sh")[:, :]), "g0", 16)
        c.inc(gp.dma_start(out=ap("wb_in")[:, 384:640], in_=ap("woT_sh")[:, :]), "g0", 16)
        ctx.ev["ugwb_in"] = c.now("g0")

    block.gpsimd(p0_gp)

    def p0_se(se):
        c.wait("SE", se, ctx.ev["p0_loads"])
        ins = None
        for ch in range(NCH):
            ins = se.copy(ap("x32")[:, ch * TC:(ch + 1) * TC],
                          ap("u1")[:, 8192 + ch * TC: 8192 + (ch + 1) * TC])
        ctx.ev["x32_ready"] = c.inc(ins, "se")

    block.scalar(p0_se)

    def p0_ve(ve):
        c.wait("VE", ve, ctx.ev["p0_loads"])
        ve.tensor_scalar(ap("wsupt")[:, :], ap("wsupt")[:, :],
                         1.0 / math.sqrt(2048.0), None, op0=ALU.mult)
        ve.tensor_scalar(ap("wsgt")[:, :], ap("wsgt")[:, :],
                         1.0 / math.sqrt(2048.0), None, op0=ALU.mult)
        ve.tensor_scalar(ap("wsdpt")[:, :], ap("wsdpt")[:, :],
                         1.0 / math.sqrt(8192.0), None, op0=ALU.mult)
        ins = ve.tensor_tensor(ap("suwut")[:, :], ap("suwut")[:, :],
                               ap("wsupt")[:, :], op=ALU.mult)
        ctx.ev["tabs_ready"] = c.inc(ins, "ve")
        ve.memset(ap("epsrow")[:, :], EPS)
        ve.memset(ap("smt")[:, :], 0)
        ve.memset(ap("ktmd")[:, :], 0)
        ve.memset(ap("kvb0")[:, :], 0)
        ve.memset(ap("kvb1")[:, :], 0)
        ve.memset(ap("kvf0")[:, :], 0)
        ins = ve.memset(ap("kvf1")[:, :], 0)
        ctx.ev["pads_zero"] = c.inc(ins, "ve")

    block.vector(p0_ve)

    def slab_slot():
        s = ctx.slab_i % 4
        ctx.slab_i += 1
        return s, ctx.slab_ev[s]

    def emit_ln(tb, su, xn_dst, pre_ev=None):
        x32 = ap("x32")

        def ve1(ve):
            c.wait("VE", ve, ctx.ev["x32_ready"])
            c.wait("VE", ve, pre_ev)
            ins = None
            for ch in range(NCH):
                xs = x32[:, ch * TC + tb * 512: ch * TC + (tb + 1) * 512]
                ve.tensor_copy(ap("u1")[:, 16384 + ch * 512: 16384 + (ch + 1) * 512], xs)
                ins = ve.tensor_tensor(
                    ap("u1")[:, 24576 + ch * 512: 24576 + (ch + 1) * 512],
                    xs, xs, op=ALU.mult)
            ctx.ev["ln_pre"] = c.inc(ins, "ve")

        def te1(te):
            bs = rot.acquire(te)
            bq_ = rot.acquire(te)
            c.wait("TE", te, ctx.ev["ln_pre"])
            for ch in range(NCH):
                te.matmul(pf[bs][0:1, :], ap("onest")[:, 0:1],
                          ap("u1")[:, 16384 + ch * 512: 16384 + (ch + 1) * 512],
                          start=(ch == 0), stop=(ch == NCH - 1))
            for ch in range(NCH):
                ins = te.matmul(pf[bq_][0:1, :], ap("onest")[:, 0:1],
                                ap("u1")[:, 24576 + ch * 512: 24576 + (ch + 1) * 512],
                                start=(ch == 0), stop=(ch == NCH - 1))
            ctx.ev["ln_sums"] = c.inc(ins, "pe")
            ctx.ev["ln_banks"] = (bs, bq_)

        def ve2(ve):
            bs, bq_ = ctx.ev["ln_banks"]
            c.wait("VE", ve, ctx.ev["ln_sums"])
            ve.tensor_scalar(ap("mrow")[:, :], pf[bs][0:1, :], 1.0 / DM, None,
                             op0=ALU.mult)
            ve.tensor_scalar(ap("vrow")[:, :], pf[bq_][0:1, :], 1.0 / DM, None,
                             op0=ALU.mult)
            ve.tensor_tensor(ap("wrow")[:, :], ap("mrow")[:, :], ap("mrow")[:, :],
                             op=ALU.mult)
            ve.tensor_copy(ap("mr_row")[:, 0:512], ap("mrow")[:, :])
            ins = ve.tensor_tensor(ap("vrow")[:, :], ap("vrow")[:, :],
                                   ap("wrow")[:, :], op=ALU.subtract)
            ctx.ev["ln_var"] = c.inc(ins, "ve")
            rot.release(bs, ctx.ev["ln_var"])
            rot.release(bq_, ctx.ev["ln_var"])

        def se1(se):
            c.wait("SE", se, ctx.ev["ln_var"])
            se.copy(ap("wrow")[:, :], ap("mrow")[:, :])
            se.activation(ap("wrow")[:, 0:256], ap("vrow")[:, 0:256], AF.Ln,
                          bias=ap("epsrow")[:, :])
            se.activation(ap("wrow")[:, 256:512], ap("vrow")[:, 256:512], AF.Ln,
                          bias=ap("epsrow")[:, :])
            se.activation(ap("wrow")[:, 0:256], ap("wrow")[:, 0:256], AF.Exp,
                          scale=-0.5)
            ins = se.activation(ap("wrow")[:, 256:512], ap("wrow")[:, 256:512],
                                AF.Exp, scale=-0.5)
            ctx.ev["ln_rstd"] = c.inc(ins, "se")

        def ve3(ve):
            c.wait("VE", ve, ctx.ev["ln_rstd"])
            ve.tensor_copy(ap("mr_row")[:, 0:512], ap("mrow")[:, :])
            ins = ve.tensor_copy(ap("mr_row")[:, 512:1024], ap("wrow")[:, :])
            ctx.ev["ln_mr"] = c.inc(ins, "ve")

        def te2(te):
            b1 = rot.acquire(te)
            b2 = rot.acquire(te)
            c.wait("TE", te, ctx.ev["ln_mr"])
            te.matmul(pf[b1][:, :], ap("onest")[0:1, :], ap("mr_row")[:, 0:512],
                      start=True, stop=True)
            ins = te.matmul(pf[b2][:, :], ap("onest")[0:1, :], ap("mr_row")[:, 512:1024],
                            start=True, stop=True)
            ctx.ev["ln_bc"] = c.inc(ins, "pe")
            ctx.ev["ln_bcb"] = (b1, b2)

        def ve4(ve):
            b1, b2 = ctx.ev["ln_bcb"]
            c.wait("VE", ve, ctx.ev["ln_bc"])
            ve.tensor_copy(ap("ptile")[:, 0:512], pf[b1][:, :])
            ins = ve.tensor_copy(ap("ptile")[:, 512:1024], pf[b2][:, :])
            ev = c.inc(ins, "ve")
            rot.release(b1, ev)
            rot.release(b2, ev)
            pend = []
            for ch in range(NCH):
                xs = x32[:, ch * TC + tb * 512: ch * TC + (tb + 1) * 512]
                tmp = ap("t1") if ch % 2 == 0 else ap("t2")
                ve.tensor_tensor(tmp[:, :], xs, ap("ptile")[:, 0:512],
                                 op=ALU.subtract)
                if pend:
                    ch0, t0 = pend.pop()
                    ve.scalar_tensor_tensor(
                        xn_dst[:, ch0 * 512:(ch0 + 1) * 512], t0[:, :],
                        (ap("sut")[:, ch0:ch0 + 1] if su else 1.0),
                        ap("ptile")[:, 512:1024], op0=ALU.mult, op1=ALU.mult)
                pend.append((ch, tmp))
            ch0, t0 = pend.pop()
            ins = ve.scalar_tensor_tensor(
                xn_dst[:, ch0 * 512:(ch0 + 1) * 512], t0[:, :],
                (ap("sut")[:, ch0:ch0 + 1] if su else 1.0),
                ap("ptile")[:, 512:1024], op0=ALU.mult, op1=ALU.mult)
            ctx.ev["xn_ready"] = c.inc(ins, "ve")

        block.vector(ve1)
        block.tensor(te1)
        block.vector(ve2)
        block.scalar(se1)
        block.vector(ve3)
        block.tensor(te2)
        block.vector(ve4)
        return ctx.ev["xn_ready"]

    QROT = 24576

    def emit_rope_evac(name, tb, items, tabc, tabs, ve_pre=None):
        ve_base = ctx.semv["ve"]
        se_base = ctx.semv["se"]
        n_rope = sum(1 for it in items if it[2])
        rbanks = {}
        slots = []
        slot_wait = []
        for ri in range(n_rope):
            sl = ctx.rope_i % 16
            ctx.rope_i += 1
            slots.append(sl)
            slot_wait.append(ctx.rope_ev.get(sl))
            ctx.rope_ev[sl] = ("ve", ve_base + ri + 1)

        def seb(se):
            assert ctx.semv["se"] == se_base
            ri = 0
            for oc, dst, do_rope in items:
                bk = ctx.ev[(name, "bank", oc)]
                c.wait("SE", se, ctx.ev[(name, "mm", oc)])
                if do_rope:
                    c.wait("SE", se, slot_wait[ri])
                    sl = slots[ri]
                    ins = se.copy(
                        ap("u1")[:, QROT + sl * 512: QROT + (sl + 1) * 512],
                        pf[bk][:, :])
                    ctx.ev[(name, "qb", ri)] = c.inc(ins, "se")
                    ri += 1
                else:
                    ins = se.copy(dst, pf[bk][:, :])
                    ev = c.inc(ins, "se")
                    ctx.ev[(name, "evdone")] = ev

        def teb(te):
            ri = 0
            for oc, dst, do_rope in items:
                if not do_rope:
                    continue
                rb = rot.acquire(te)
                c.wait("TE", te, ctx.ev[(name, "qb", ri)])
                sl = slots[ri]
                ins = te.matmul(
                    pf[rb][:, :], ap("rpmt")[:, :],
                    ap("u1")[:, QROT + sl * 512: QROT + (sl + 1) * 512],
                    start=True, stop=True)
                c.inc(ins, "pe")
                ctx.ev[(name, "rot", ri)] = c.now("pe")
                rbanks[ri] = rb
                rot.release(rb, ("ve", ve_base + ri + 1))
                ri += 1

        def veb(ve):
            assert ctx.semv["ve"] == ve_base
            c.wait("VE", ve, ve_pre)
            cos = tabc[:, tb * 512:(tb + 1) * 512]
            sin = tabs[:, tb * 512:(tb + 1) * 512]
            ritems = [(oc, dst) for oc, dst, dr in items if dr]
            pend = []
            for ri, (oc, dst) in enumerate(ritems):
                rb = rbanks[ri]
                sl = slots[ri]
                c.wait("VE", ve, ctx.ev[(name, "rot", ri)])
                slap = ap("u1")[:, QROT + sl * 512: QROT + (sl + 1) * 512]
                tmp = ap("t2") if ri % 2 == 0 else ap("tg")
                ve.tensor_tensor(dst, slap, cos, op=ALU.mult)
                ve.tensor_tensor(tmp[:, :], pf[rb][:, :], sin, op=ALU.mult)
                if pend:
                    d0, t0 = pend.pop()
                    c.inc(ve.tensor_tensor(d0, d0, t0[:, :], op=ALU.add), "ve")
                pend.append((dst, tmp))
            d0, t0 = pend.pop()
            c.inc(ve.tensor_tensor(d0, d0, t0[:, :], op=ALU.add), "ve")
            assert ctx.semv["ve"] == ve_base + n_rope
            ctx.ev[(name, "evdone")] = c.now("ve")

        block.scalar(seb)
        block.tensor(teb)
        if n_rope:
            block.vector(veb)
        return ctx.ev[(name, "evdone")]

    def emit_stream_mm(name, dsem, n_oc, grp, slab_fn, slab_w, rhs_fn,
                       sy_pre=None, te_pre=None, nK=NCH, dual_dma=False,
                       bias_fn=None, release_fn=None):
        dsem = ctx.next_dsem()
        ngrp = (n_oc + grp - 1) // grp
        pe_base = ctx.semv["pe"]
        n_slab = ngrp * nK
        plan = []
        si = 0
        for gi in range(ngrp):
            for k in range(nK):
                plan.append((gi, k, si))
                si += 1

        def syb(sy):
            for s_, v_ in ([] if sy_pre is None else sy_pre):
                c.wait("SY", sy, (s_, v_))
            for gi, k, i in plan:
                s, wev = slab_slot()
                c.wait("SY", sy, wev)
                src = slab_fn(gi, k)
                if dual_dma:
                    i1 = sy.dma_start(
                        out=ap("wstream")[:, s * 512: s * 512 + slab_w // 2],
                        in_=src[0])
                    c.inc(i1, dsem, 16)
                    ins = sy.dma_start(
                        out=ap("wstream")[:, s * 512 + slab_w // 2: s * 512 + slab_w],
                        in_=src[1])
                    ev = c.inc(ins, dsem, 16)
                else:
                    ins = sy.dma_start(out=ap("wstream")[:, s * 512: s * 512 + slab_w],
                                       in_=src)
                    ev = c.inc(ins, dsem, 16)
                ctx.ev[(name, "slab", i)] = (ev, s)
                ctx.slab_ev[s] = ("pe", pe_base + i + 1)
            gi, k, i = plan[-1]
            _, s = ctx.ev[(name, "slab", i)]
            src = slab_fn(gi, k)
            if dual_dma:
                src = src[1]
                ins = sy.dma_start(
                    out=ap("wstream")[:, s * 512 + slab_w // 2: s * 512 + slab_w],
                    in_=src)
            else:
                ins = sy.dma_start(out=ap("wstream")[:, s * 512: s * 512 + slab_w],
                                   in_=src)
            ctx.ev[(name, "slab_tail")] = c.inc(ins, dsem, 16)

        def teb(te):
            assert ctx.semv["pe"] == pe_base, (name, ctx.semv["pe"], pe_base)
            if te_pre is not None:
                c.wait("TE", te, te_pre)
            for gi in range(ngrp):
                ocs = [gi * grp + j for j in range(grp) if gi * grp + j < n_oc]
                banks = [rot.acquire(te) for _ in ocs]
                for k in range(nK):
                    i = gi * nK + k
                    ev, s = ctx.ev[(name, "slab", i)]
                    if i + 1 < n_slab:
                        ev = ctx.ev[(name, "slab", i + 1)][0]
                    else:
                        ev = ctx.ev[(name, "slab_tail")]
                    c.wait("TE", te, ev)
                    last = (k == nK - 1) and bias_fn is None
                    for j, oc in enumerate(ocs):
                        ins = te.matmul(
                            pf[banks[j]][:, :],
                            ap("wstream")[:, s * 512 + j * 128: s * 512 + (j + 1) * 128],
                            rhs_fn(k), start=(k == 0), stop=last)
                    if k < nK - 1 or bias_fn is None:
                        c.inc(ins, "pe")
                if bias_fn is not None:
                    for j, oc in enumerate(ocs):
                        ins = te.matmul(pf[banks[j]][:, :], bias_fn(oc),
                                        ap("maskt")[0:1, 384:896],
                                        start=False, stop=True)
                    c.inc(ins, "pe")
                for j, oc in enumerate(ocs):
                    ctx.ev[(name, "mm", oc)] = c.now("pe")
                    ctx.ev[(name, "bank", oc)] = banks[j]
                    if release_fn is not None:
                        rot.release(banks[j], release_fn(oc))
            assert ctx.semv["pe"] == pe_base + n_slab

        block.sync(syb)
        block.tensor(teb)


    def phase1():
        for tb in range(2):
            xn_ev = emit_ln(tb, False, ap("u1")[:, 8192:16384])

            def rhs(k):
                return ap("u1")[:, 8192 + k * 512: 8192 + (k + 1) * 512]

            se_b = ctx.semv["se"]
            emit_stream_mm(f"qp{tb}", "d1", 16, 4,
                           lambda gi, k: ap("Wq")[k * 128:(k + 1) * 128,
                                                  gi * 512:(gi + 1) * 512],
                           512, rhs, te_pre=xn_ev,
                           bias_fn=lambda oc: ap("bqrt")[0:1, oc * 128:(oc + 1) * 128],
                           release_fn=lambda oc: ("se", se_b + oc + 1))
            items = [(h, ap("qbuf")[:, h * TC + tb * 512: h * TC + (tb + 1) * 512],
                      True) for h in range(16)]
            e1 = emit_rope_evac(f"qp{tb}", tb, items, ap("ropec"), ap("ropes"))
            se_b2 = ctx.semv["se"]
            emit_stream_mm(f"kp{tb}", "d1", 4, 4,
                           lambda gi, k: ap("Wk")[k * 128:(k + 1) * 128, 0:512],
                           512, rhs,
                           bias_fn=lambda oc: ap("bkrt")[0:1, oc * 128:(oc + 1) * 128],
                           release_fn=lambda oc: ("se", se_b2 + oc + 1))
            items = [(kv, ap("u1")[:, kv * TC + tb * 512: kv * TC + (tb + 1) * 512],
                      True) for kv in range(4)]
            e2 = emit_rope_evac(f"kp{tb}", tb, items, ap("ropec"), ap("ropes"))
            se_b3 = ctx.semv["se"]
            emit_stream_mm(f"vp{tb}", "d1", 4, 4,
                           lambda gi, k: ap("Wv")[k * 128:(k + 1) * 128, 0:512],
                           512, rhs,
                           bias_fn=lambda oc: ap("bvrt")[0:1, oc * 128:(oc + 1) * 128],
                           release_fn=lambda oc: ("se", se_b3 + oc + 1))
            items = [(kv, ap("u1")[:, 4096 + kv * TC + tb * 512:
                                   4096 + kv * TC + (tb + 1) * 512], False)
                     for kv in range(4)]
            e3 = emit_rope_evac(f"vp{tb}", tb, items, ap("ropec"), ap("ropes"))
            ctx.ev[f"qkv_ev{tb}"] = [e1, e2, e3]

        se_vt0 = ctx.semv["se"]
        d1_vt0 = ctx.semv["d1"]

        def te_vt(te):
            for e in ctx.ev["qkv_ev0"] + ctx.ev["qkv_ev1"]:
                c.wait("TE", te, e)
            for i in range(32):
                kv, ti = divmod(i, 8)
                if i >= 2:
                    c.wait("TE", te, ("se", se_vt0 + i - 1))
                ins = te.transpose(ptp[:, (i % 2) * 128:(i % 2) * 128 + 128],
                                   ap("u1")[:, 4096 + kv * TC + ti * 128:
                                            4096 + kv * TC + (ti + 1) * 128],
                                   ap("identt")[:, :])
                ctx.ev[("vtmm", i)] = c.inc(ins, "pe")

        def se_vt(se):
            assert ctx.semv["se"] == se_vt0
            for i in range(32):
                c.wait("SE", se, ctx.ev[("vtmm", i)])
                if i >= 2:
                    c.wait("SE", se, ("d1", d1_vt0 + 32 * (i - 1)))
                ins = se.copy(ap("u1")[:, 16384 + (i % 2) * 128: 16384 + (i % 2) * 128 + 128],
                              ptp[:, (i % 2) * 128:(i % 2) * 128 + 128])
                ctx.ev[("vtse", i)] = c.inc(ins, "se")

        def sy_vt(sy):
            assert ctx.semv["d1"] == d1_vt0
            for i in range(32):
                kv, ti = divmod(i, 8)
                c.wait("SY", sy, ctx.ev[("vtse", i)])
                for j in (2 * kv, 2 * kv + 1):
                    ins = sy.dma_start(
                        out=ap("v_in")[j * TC + ti * 128: j * TC + (ti + 1) * 128, :],
                        in_=ap("u1")[:, 16384 + (i % 2) * 128: 16384 + (i % 2) * 128 + 128])
                    c.inc(ins, "d1", 16)
            ctx.ev["v_in_done"] = c.now("d1")
            for e in ctx.ev["qkv_ev0"] + ctx.ev["qkv_ev1"]:
                c.wait("SY", sy, e)
            for j in range(R):
                for hl in range(2):
                    h = 2 * j + hl
                    c.inc(sy.dma_start(
                        out=ap("qk_in")[j * 384 + hl * 128: j * 384 + (hl + 1) * 128, :],
                        in_=ap("qbuf")[:, h * TC:(h + 1) * TC]), "d0", 16)
                kv = j // 2
                c.inc(sy.dma_start(
                    out=ap("qk_in")[j * 384 + 256: j * 384 + 384, :],
                    in_=ap("u1")[:, kv * TC:(kv + 1) * TC]), "d0", 16)
            ctx.ev["qk_in_done"] = c.now("d0")

        block.tensor(te_vt)
        block.scalar(se_vt)
        block.sync(sy_vt)

        def gp_cc(gp):
            c.wait("GP", gp, ctx.ev["qk_in_done"])
            with allow_ici_all_to_all():
                ctx.ev["cc_qk"] = c.inc(gp.collective_compute(
                    "AllToAll", ALU.bypass, replica_groups=[list(range(R))],
                    ins=[cap("qk_in")], outs=[cap("qk_out")]), "cc")
                c.wait("GP", gp, ctx.ev["v_in_done"])
                ctx.ev["cc_v"] = c.inc(gp.collective_compute(
                    "AllToAll", ALU.bypass, replica_groups=[list(range(R))],
                    ins=[cap("v_in")], outs=[cap("v_out")]), "cc")

        block.gpsimd(gp_cc)

    def fold_wd():
        pe_base = ctx.semv["pe"]
        se_base = ctx.semv["se"]
        d0_base = ctx.semv["d0"]

        ve_cast0 = [None]
        fsA = [None]

        def sy_lda(sy):
            wdr = ap("wdT_sh").rearrange("(a r) h -> a (r h)", a=64)
            for bg in range(64):
                if bg >= 2:
                    c.wait("SY", sy, ("ve", ve_cast0[0] + bg - 1))
                stage = ap("t2") if bg % 2 == 0 else ap("tg")
                c.inc(sy.dma_start(
                    out=stage[0:64, :],
                    in_=wdr[:, bg * 512:(bg + 1) * 512]), f"fa{bg % 4}", 16)

        def ve_lda(ve):
            assert ctx.semv["ve"] == ve_cast0[0]
            for bg in range(64):
                c.wait("VE", ve, (f"fa{bg % 4}", 16 * (bg // 4 + 1)))
                if bg >= 2:
                    c.wait("VE", ve, ("pe", pe_base + 2 * (bg - 2) + 2))
                stage = ap("t2") if bg % 2 == 0 else ap("tg")
                ins = ve.tensor_copy(
                    ap("fold_a")[:, (bg % 2) * 512:(bg % 2) * 512 + 512],
                    stage[0:64, :])
                ctx.ev[("fa_cast", bg)] = c.inc(ins, "ve")

        ve_cast0[0] = ctx.semv["ve"]
        block.sync(sy_lda)
        block.vector(ve_lda)

        def teA(te):
            assert ctx.semv["pe"] == pe_base
            for b_ in range(2):
                c.wait("TE", te, ctx.ev_bank.get(b_))
            for i in range(128):
                bg = i // 2
                c.wait("TE", te, ctx.ev[("fa_cast", bg)])
                if i >= 2:
                    c.wait("TE", te, ("se", se_base + i - 1))
                ins = te.matmul(pf[i % 2][0:64, 0:256], ap("h64t")[:, :],
                                ap("fold_a")[:, (bg % 2) * 512 + (i % 2) * 256:
                                             (bg % 2) * 512 + (i % 2) * 256 + 256],
                                start=True, stop=True)
                c.inc(ins, "pe")
            assert ctx.semv["pe"] == pe_base + 128

        def seA(se):
            assert ctx.semv["se"] == se_base
            for i in range(128):
                c.wait("SE", se, ("pe", pe_base + i + 1))
                if i >= 2:
                    c.wait("SE", se, (f"sa{(i - 2) % 4}", ((i - 2) // 4 + 1) * 16))
                ins = se.copy(ap("ptile")[0:64, 1024 + (i % 2) * 256: 1024 + (i % 2) * 256 + 256],
                              pf[i % 2][0:64, 0:256])
                c.inc(ins, "se")
            assert ctx.semv["se"] == se_base + 128

        def stA(gp):
            wdiv = ap("wdi").rearrange("(a b) h -> b a h", a=64, b=128)
            for i in range(128):
                c.wait("GP", gp, ("se", se_base + i + 1))
                c.inc(gp.dma_start(out=wdiv[i],
                                   in_=ap("ptile")[0:64, 1024 + (i % 2) * 256: 1024 + (i % 2) * 256 + 256]),
                      f"sa{i % 4}", 16)
            ctx.ev["wdi_done"] = [(f"sa{j}", ctx.semv[f"sa{j}"]) for j in range(4)]

        block.tensor(teA)
        block.scalar(seA)
        block.gpsimd(stA)

        pe_b = ctx.semv["pe"]
        ve_b = ctx.semv["ve"]
        d0_b = ctx.semv["d0"]

        def syB(sy):
            assert ctx.semv["d0"] == d0_b
            for e in ctx.ev["wdi_done"]:
                c.wait("SY", sy, e)
            c.wait("SY", sy, ctx.ev["qk_in_done"])
            for a in range(64):
                if a >= 2:
                    c.wait("SY", sy, ("pe", pe_b + a - 1))
                c.inc(sy.dma_start(
                    out=ap("qbuf")[:, (a % 2) * 256:(a % 2) * 256 + 256],
                    in_=ap("wdi")[a * 128:(a + 1) * 128, :]), "d0", 16)
            ctx.ev["foldb_ld"] = c.now("d0")

        def teB(te):
            assert ctx.semv["pe"] == pe_b
            for b_ in (2, 3):
                c.wait("TE", te, ctx.ev_bank.get(b_))
            for a in range(64):
                c.wait("TE", te, ("d0", d0_b + 16 * (a + 1)))
                if a >= 2:
                    c.wait("TE", te, ("ve", ve_b + a - 1))
                ins = te.matmul(pf[2 + a % 2][:, 0:256], ap("h128t")[:, :],
                                ap("qbuf")[:, (a % 2) * 256:(a % 2) * 256 + 256],
                                start=True, stop=True)
                c.inc(ins, "pe")
            assert ctx.semv["pe"] == pe_b + 64

        g0B = [None]

        def veB(ve):
            assert ctx.semv["ve"] == ve_b
            c.wait("VE", ve, ctx.ev["tabs_ready"])
            for a in range(64):
                c.wait("VE", ve, ("pe", pe_b + a + 1))
                if a >= 2:
                    c.wait("VE", ve, ("fs", g0B[0] + (a - 1) * 16))
                ins = ve.tensor_scalar(
                    ap("ptile")[:, 1024 + (a % 2) * 256: 1024 + (a % 2) * 256 + 256],
                    pf[2 + a % 2][:, 0:256],
                    ap("suwut")[:, a:a + 1], None, op0=ALU.mult)
                c.inc(ins, "ve")
            assert ctx.semv["ve"] == ve_b + 64

        def stB(gp):
            assert ctx.semv["fs"] == g0B[0]
            for a in range(64):
                c.wait("GP", gp, ("ve", ve_b + a + 1))
                c.inc(gp.dma_start(
                    out=ap("wd_in")[a * 128:(a + 1) * 128, :],
                    in_=ap("ptile")[:, 1024 + (a % 2) * 256: 1024 + (a % 2) * 256 + 256]),
                    "fs", 16)
            ctx.ev["wd_in_done"] = c.now("fs")

        block.sync(syB)
        block.tensor(teB)
        ctx.ev["foldB_pe_end"] = c.now("pe")
        g0B[0] = ctx.semv["fs"]
        block.vector(veB)
        block.gpsimd(stB)
        ctx.seal_banks(c.now("ve"))

        def gp_cc(gp):
            c.wait("GP", gp, ctx.ev["ugwb_in"])
            with allow_ici_all_to_all():
                ctx.ev["cc_ug"] = c.inc(gp.collective_compute(
                    "AllGather", ALU.bypass, replica_groups=[list(range(R))],
                    ins=[cap("ug_in")], outs=[cap("ug_out")]), "cc")
                ctx.ev["cc_wb"] = c.inc(gp.collective_compute(
                    "AllGather", ALU.bypass, replica_groups=[list(range(R))],
                    ins=[cap("wb_in")], outs=[cap("wb_out")]), "cc")
                c.wait("GP", gp, ctx.ev["wd_in_done"])
                ctx.ev["cc_wd"] = c.inc(gp.collective_compute(
                    "AllGather", ALU.bypass, replica_groups=[list(range(R))],
                    ins=[cap("wd_in")], outs=[cap("wd_out")]), "cc")

        block.gpsimd(gp_cc)

    def phase2():
        def sy_ld(sy):
            c.wait("SY", sy, ctx.ev["cc_qk"])
            c.wait("SY", sy, ctx.ev["foldB_pe_end"])
            for hl in range(2):
                for r in range(R):
                    c.inc(sy.dma_start(
                        out=ap("qbuf")[:, hl * T + r * TC: hl * T + (r + 1) * TC],
                        in_=ap("qk_out")[r * 384 + hl * 128:
                                         r * 384 + (hl + 1) * 128, :]), "d1", 16)
            for r in range(R):
                c.inc(sy.dma_start(
                    out=ap("u1")[:, r * TC:(r + 1) * TC],
                    in_=ap("qk_out")[r * 384 + 256: r * 384 + 384, :]), "d1", 16)
            c.wait("SY", sy, ctx.ev["cc_v"])
            for jt in range(64):
                c.inc(sy.dma_start(
                    out=ap("u1")[:, 8192 + jt * 128: 8192 + (jt + 1) * 128],
                    in_=ap("v_out")[jt * 128:(jt + 1) * 128, :]), "d1", 16)
            ctx.ev["att_ld"] = c.now("d1")

        block.sync(sy_ld)

        pe0 = ctx.semv["pe"]
        se0 = ctx.semv["se"]
        ve0 = ctx.semv["ve"]
        s_val, pv_val, exp_val, p_val = {}, {}, {}, {}
        lo_val, bc_val, rl_val, oev_val = {}, {}, {}, {}
        pe_n, se_n, ve_n = pe0, se0, ve0
        si_of = {}
        si = 0
        for hl in range(2):
            for ib in range(16):
                ext = 4 * ib + 4
                pend = []
                for jc in range(ext):
                    si_of[(hl, ib, jc)] = si
                    pe_n += 1
                    s_val[si] = pe_n
                    pend.append(si)
                    si += 1
                    if len(pend) >= 3:
                        s0 = pend.pop(0)
                        pe_n += 1
                        pv_val[s0] = pe_n
                while pend:
                    s0 = pend.pop(0)
                    pe_n += 1
                    pv_val[s0] = pe_n
                lo_val[(hl, ib)] = pe_n
                pe_n += 1
                bc_val[(hl, ib)] = pe_n
        for hl in range(2):
            for ib in range(16):
                for jc in range(4 * ib + 4):
                    s0 = si_of[(hl, ib, jc)]
                    se_n += 1
                    exp_val[s0] = se_n
        for hl in range(2):
            for ib in range(16):
                for jc in range(4 * ib + 4):
                    s0 = si_of[(hl, ib, jc)]
                    if jc >= 4 * ib:
                        ve_n += 1
                        p_val[s0] = ("ve", ve_n)
                    else:
                        p_val[s0] = ("se", exp_val[s0])
                ve_n += 1
                rl_val[(hl, ib)] = ve_n
                ve_n += 1
                oev_val[(hl, ib)] = ve_n
        n_si = si

        def se_att(se):
            assert ctx.semv["se"] == se0
            c.wait("SE", se, ctx.ev["wd_in_done"])
            for hl in range(2):
                for ib in range(16):
                    for jc in range(4 * ib + 4):
                        s0 = si_of[(hl, ib, jc)]
                        c.wait("SE", se, ("pe", s_val[s0]))
                        if s0 >= 3:
                            c.wait("SE", se, ("pe", pv_val[s0 - 3]))
                        ins = se.activation(
                            ap("ptile")[:, (s0 % 3) * 512:(s0 % 3 + 1) * 512],
                            pf[s0 % 3][:, :], AF.Exp, scale=SM_SCALE)
                        c.inc(ins, "se")
            assert ctx.semv["se"] == se_n

        def ve_att(ve):
            assert ctx.semv["ve"] == ve0
            for hl in range(2):
                for ib in range(16):
                    for jc in range(4 * ib, 4 * ib + 4):
                        s0 = si_of[(hl, ib, jc)]
                        tau = jc - 4 * ib
                        c.wait("VE", ve, ("se", exp_val[s0]))
                        ins = ve.tensor_tensor(
                            ap("ptile")[:, (s0 % 3) * 512:(s0 % 3 + 1) * 512],
                            ap("ptile")[:, (s0 % 3) * 512:(s0 % 3 + 1) * 512],
                            ap("maskt")[:, 384 - tau * 128: 896 - tau * 128],
                            op=ALU.mult)
                        c.inc(ins, "ve")
                    c.wait("VE", ve, ("pe", lo_val[(hl, ib)]))
                    ve.reciprocal(ap("rlrow")[:, :], pf[4][0:1, :])
                    ins = ve.tensor_copy(ap("rlrow_bf")[:, :], ap("rlrow")[:, :])
                    c.inc(ins, "ve")
                    c.wait("VE", ve, ("pe", bc_val[(hl, ib)]))
                    ve.tensor_copy(ap("tg")[:, :], pf[5][:, :])
                    ins = ve.tensor_tensor(
                        ap("u1")[:, 16384 + hl * T + ib * 512:
                                 16384 + hl * T + (ib + 1) * 512],
                        pf[3][:, :], ap("tg")[:, :], op=ALU.mult)
                    c.inc(ins, "ve")
            assert ctx.semv["ve"] == ve_n
            ctx.ev["att_done_ve"] = c.now("ve")

        def te_att(te):
            assert ctx.semv["pe"] == pe0
            c.wait("TE", te, ctx.ev["att_ld"])
            for i in range(7):
                c.wait("TE", te, ctx.ev_bank.get(i))
            for hl in range(2):
                for ib in range(16):
                    ext = 4 * ib + 4
                    if ib > 0 or hl > 0:
                        prev = (hl, ib - 1) if ib > 0 else (hl - 1, 15)
                        c.wait("TE", te, ("ve", oev_val[prev]))
                    pend = []
                    for jc in range(ext):
                        s0 = si_of[(hl, ib, jc)]
                        if s0 >= 3:
                            c.wait("TE", te, p_val[s0 - 3])
                        ins = te.matmul(
                            pf[s0 % 3][:, :],
                            ap("u1")[:, jc * 128:(jc + 1) * 128],
                            ap("qbuf")[:, hl * T + ib * 512: hl * T + (ib + 1) * 512],
                            start=True, stop=True)
                        c.inc(ins, "pe")
                        assert ctx.semv["pe"] == s_val[s0]
                        pend.append((jc, s0))
                        if len(pend) >= 3:
                            _lpv(te, hl, ib, pend.pop(0))
                    while pend:
                        _lpv(te, hl, ib, pend.pop(0))
                    c.wait("TE", te, ("ve", rl_val[(hl, ib)]))
                    ins = te.matmul(pf[5][:, :], ap("onest")[0:1, :],
                                    ap("rlrow_bf")[0:1, :], start=True, stop=True)
                    c.inc(ins, "pe")
                    assert ctx.semv["pe"] == bc_val[(hl, ib)]
            ctx.seal_banks(ctx.ev["att_done_ve"])

        def _lpv(te, hl, ib, item):
            jc, s0 = item
            first = (jc == 0)
            last = (jc == 4 * ib + 3)
            c.wait("TE", te, p_val[s0])
            te.matmul(pf[4][0:1, :], ap("onest")[:, 0:1],
                      ap("ptile")[:, (s0 % 3) * 512:(s0 % 3 + 1) * 512],
                      start=first, stop=last, skip_group_check=True)
            ins = te.matmul(pf[3][:, :],
                            ap("u1")[:, 8192 + jc * 128: 8192 + (jc + 1) * 128],
                            ap("ptile")[:, (s0 % 3) * 512:(s0 % 3 + 1) * 512],
                            start=first, stop=last, skip_group_check=True)
            c.inc(ins, "pe")
            assert ctx.semv["pe"] == pv_val[s0]

        block.scalar(se_att)
        block.vector(ve_att)
        block.tensor(te_att)

        def sy_oc(sy):
            c.wait("SY", sy, ctx.ev["att_done_ve"])
            for j in range(R):
                for hl in range(2):
                    c.inc(sy.dma_start(
                        out=ap("o_in")[j * 256 + hl * 128: j * 256 + (hl + 1) * 128, :],
                        in_=ap("u1")[:, 16384 + hl * T + j * TC:
                                     16384 + hl * T + (j + 1) * TC]), "d1", 16)
            ctx.ev["o_in_done"] = c.now("d1")

        block.sync(sy_oc)

        def gp_cc(gp):
            c.wait("GP", gp, ctx.ev["o_in_done"])
            with allow_ici_all_to_all():
                ctx.ev["cc_o"] = c.inc(gp.collective_compute(
                    "AllToAll", ALU.bypass, replica_groups=[list(range(R))],
                    ins=[cap("o_in")], outs=[cap("o_out")]), "cc")

        block.gpsimd(gp_cc)

        def sy_ld2(sy):
            c.wait("SY", sy, ctx.ev["cc_o"])
            for h in range(16):
                c.inc(sy.dma_start(
                    out=ap("qbuf")[:, h * TC:(h + 1) * TC],
                    in_=ap("o_out")[(h // 2) * 256 + (h % 2) * 128:
                                    (h // 2) * 256 + (h % 2) * 128 + 128, :]),
                    "d1", 16)
            ctx.ev["ao_ld"] = c.now("d1")

        block.sync(sy_ld2)

        for tb in range(2):
            def rhs(k, _tb=tb):
                return ap("qbuf")[:, k * TC + _tb * 512: k * TC + (_tb + 1) * 512]

            ve_bw = ctx.semv["ve"]
            emit_stream_mm(f"wo{tb}", "d1", 16, 4,
                           lambda gi, k: ap("Wo")[k * 128:(k + 1) * 128,
                                                  gi * 512:(gi + 1) * 512],
                           512, rhs, sy_pre=[ctx.ev["ao_ld"]],
                           te_pre=ctx.ev["ao_ld"],
                           release_fn=lambda oc, b=ve_bw: ("ve", b + oc + 1))

            def ve_ev(ve, _tb=tb, _vb=ve_bw):
                assert ctx.semv["ve"] == _vb
                ev = None
                for oc in range(16):
                    bk = ctx.ev[(f"wo{_tb}", "bank", oc)]
                    c.wait("VE", ve, ctx.ev[(f"wo{_tb}", "mm", oc)])
                    xs = ap("x32")[:, oc * TC + _tb * 512: oc * TC + (_tb + 1) * 512]
                    ins = ve.tensor_tensor(xs, xs, pf[bk][:, :], op=ALU.add)
                    ev = c.inc(ins, "ve")
                ctx.ev[f"res1_{_tb}"] = ev

            block.vector(ve_ev)

    def phase_ffn(call, dsem, pre_ev):
        for tb in range(2):
            tag = f"f{call}_{tb}"
            xn_ev = emit_ln(tb, True, ap("qbuf")[:, 8192:16384],
                            pre_ev=(pre_ev if tb == 0 else
                                    ctx.ev.get("dbg_ffn")))

            def rhs_xn(k):
                return ap("qbuf")[:, 8192 + k * 512: 8192 + (k + 1) * 512]

            se_bx = ctx.semv["se"]
            emit_stream_mm(f"xh{tag}", dsem, 16, 4,
                           lambda gi, k: ap("Hpm")[k * 128:(k + 1) * 128,
                                                   gi * 512:(gi + 1) * 512],
                           512, rhs_xn, te_pre=xn_ev,
                           release_fn=lambda oc: ("se", se_bx + oc + 1))

            def se_xh(se):
                assert ctx.semv["se"] == se_bx
                ev = None
                for oc in range(16):
                    bk = ctx.ev[(f"xh{tag}", "bank", oc)]
                    c.wait("SE", se, ctx.ev[(f"xh{tag}", "mm", oc)])
                    ins = se.copy(ap("qbuf")[:, oc * 512:(oc + 1) * 512], pf[bk][:, :])
                    ev = c.inc(ins, "se")
                ctx.ev[f"xh_ev{tag}"] = ev

            block.scalar(se_xh)

            def rhs_xh(k):
                return ap("qbuf")[:, k * 512:(k + 1) * 512]

            def ug_slab(gi, k):
                ci = 2 * gi
                r_ = ci // 8
                lc = ci % 8
                return ap("ug_out")[r_ * DM + k * 128: r_ * DM + (k + 1) * 128,
                                    lc * 256: lc * 256 + 512]

            se_bu = ctx.semv["se"]
            ve_bu = ctx.semv["ve"]

            def ug_rel(oc):
                ioc = oc // 2
                if oc % 2:
                    return ("se", se_bu + 2 * ioc + 2)
                return ("ve", ve_bu + ioc + 1)

            emit_stream_mm(f"ug{tag}", dsem, 128, 4, ug_slab, 512, rhs_xh,
                           sy_pre=[ctx.ev["cc_ug"]],
                           te_pre=ctx.ev[f"xh_ev{tag}"], release_fn=ug_rel)

            def se_ug(se):
                assert ctx.semv["se"] == se_bu
                for ioc in range(64):
                    bg = ctx.ev[(f"ug{tag}", "bank", 2 * ioc + 1)]
                    c.wait("SE", se, ctx.ev[(f"ug{tag}", "mm", 2 * ioc + 1)])
                    if ioc >= 2:
                        c.wait("SE", se, ("ve", ve_bu + ioc - 1))
                    tgb = ap("tg") if ioc % 2 == 0 else ap("t2")
                    ins = se.activation(tgb[:, 0:256], pf[bg][:, 0:256],
                                        AF.Silu,
                                        scale=ap("wsgt")[:, ioc:ioc + 1])
                    c.inc(ins, "se")
                    ins = se.activation(tgb[:, 256:512], pf[bg][:, 256:512],
                                        AF.Silu,
                                        scale=ap("wsgt")[:, ioc:ioc + 1])
                    ctx.ev[(tag, "tg", ioc)] = c.inc(ins, "se")

            def ve_ug(ve):
                assert ctx.semv["ve"] == ve_bu
                for ioc in range(64):
                    bu = ctx.ev[(f"ug{tag}", "bank", 2 * ioc)]
                    c.wait("VE", ve, ctx.ev[(tag, "tg", ioc)])
                    tgb = ap("tg") if ioc % 2 == 0 else ap("t2")
                    ins = ve.tensor_tensor(
                        ap("u1")[:, ioc * 512:(ioc + 1) * 512],
                        pf[bu][:, :], tgb[:, :], op=ALU.mult)
                    ev = c.inc(ins, "ve")
                    ctx.ev[(tag, "z1", ioc)] = ev
                    if ioc == 63:
                        ins = ve.tensor_copy(ap("t1")[:, 0:8], ap("t1")[:, 0:8])
                        c.inc(ins, "ve")

            block.scalar(se_ug)
            block.vector(ve_ug)

            if debug_wts and call == 1 and tb == 0:
                def sy_dbgf(sy):
                    c.wait("SY", sy, ctx.ev[(tag, "z1", 63)])
                    c.inc(sy.dma_start(out=ap("dbg_xh")[:, :],
                                       in_=ap("qbuf")[:, :]), "d5", 16)
                    c.inc(sy.dma_start(out=ap("dbg_z1")[:, :],
                                       in_=ap("u1")[:, :]), "d5", 16)
                    ctx.ev["dbg_ffn"] = c.now("d5")
                block.sync(sy_dbgf)

            pe_base = ctx.semv["pe"]

            dnsem = ctx.next_dsem()

            def sy_dn(sy):
                c.wait("SY", sy, ctx.ev["cc_wd"])
                i = 0
                for hcp in range(8):
                    for ioc in range(64):
                        s, wev = slab_slot()
                        c.wait("SY", sy, wev)
                        ins = sy.dma_start(
                            out=ap("wstream")[:, s * 512: s * 512 + 256],
                            in_=ap("wd_out")[hcp * INTER + ioc * 128:
                                             hcp * INTER + (ioc + 1) * 128, :])
                        ctx.ev[(tag, "dnslab", i)] = (c.inc(ins, dnsem, 16), s)
                        ctx.slab_ev[s] = ("pe", pe_base + i + 1)
                        i += 1
                ins = sy.dma_start(
                    out=ap("wstream")[:, s * 512: s * 512 + 256],
                    in_=ap("wd_out")[7 * INTER + 63 * 128: 8 * INTER, :])
                ctx.ev[(tag, "dntail")] = c.inc(ins, dnsem, 16)

            ve_bd = [None]

            def te_dn(te):
                assert ctx.semv["pe"] == pe_base
                ve_b = ve_bd[0]
                i = 0
                for hcp in range(8):
                    b0 = rot.acquire(te)
                    b1 = rot.acquire(te)
                    rot.release(b0, ("ve", ve_b + 2 * hcp + 1))
                    rot.release(b1, ("ve", ve_b + 2 * hcp + 2))
                    for ioc in range(64):
                        ev, s = ctx.ev[(tag, "dnslab", i)]
                        ev = (ctx.ev[(tag, "dnslab", i + 1)][0] if i + 1 < 512
                              else ctx.ev[(tag, "dntail")])
                        c.wait("TE", te, ev)
                        zev = ctx.ev[(tag, "z1", ioc)]
                        c.wait("TE", te, (zev[0], zev[1] + 1))
                        te.matmul(pf[b0][:, :],
                                  ap("wstream")[:, s * 512: s * 512 + 128],
                                  ap("u1")[:, ioc * 512:(ioc + 1) * 512],
                                  start=(ioc == 0), stop=(ioc == 63))
                        ins = te.matmul(pf[b1][:, :],
                                        ap("wstream")[:, s * 512 + 128: s * 512 + 256],
                                        ap("u1")[:, ioc * 512:(ioc + 1) * 512],
                                        start=(ioc == 0), stop=(ioc == 63))
                        c.inc(ins, "pe")
                        i += 1
                    ctx.ev[(tag, "dnmm", hcp)] = c.now("pe")
                    ctx.ev[(tag, "dnbank", hcp)] = (b0, b1)

            def ve_dn(ve):
                assert ctx.semv["ve"] == ve_bd[0]
                ev = None
                for hcp in range(8):
                    b0, b1 = ctx.ev[(tag, "dnbank", hcp)]
                    c.wait("VE", ve, ctx.ev[(tag, "dnmm", hcp)])
                    for s_, bk in ((0, b0), (1, b1)):
                        hc = 2 * hcp + s_
                        xs = ap("x32")[:, hc * TC + tb * 512: hc * TC + (tb + 1) * 512]
                        ins = ve.scalar_tensor_tensor(
                            xs, pf[bk][:, :], ap("wsdpt")[:, hc:hc + 1], xs,
                            op0=ALU.mult, op1=ALU.add)
                        ev = c.inc(ins, "ve")
                ctx.ev[f"ffn_done{tag}"] = ev

            block.sync(sy_dn)
            ve_bd[0] = ctx.semv["ve"]
            block.tensor(te_dn)
            block.vector(ve_dn)
        return ctx.ev[f"ffn_done{tag}"]

    def phase4(pre_ev, sub=99):
        dsem = "d3"

        def sy_rt(sy):
            c.inc(sy.dma_start(out=ap("ropec")[:, :], in_=ap("rb_cos")[:, :]),
                  "d1", 16)
            c.inc(sy.dma_start(out=ap("ropes")[:, :], in_=ap("rb_sin")[:, :]),
                  "d1", 16)
            ctx.ev["rope_rl"] = c.now("d1")

        block.sync(sy_rt)
        for tb in range(2):
            xn_ev = emit_ln(tb, False, ap("u1")[:, 8192:16384],
                            pre_ev=(pre_ev if tb == 0 else None))

            def rhs(k):
                return ap("u1")[:, 8192 + k * 512: 8192 + (k + 1) * 512]

            def q_slab(gi, k):
                return ap("wb_out")[gi * DM + k * 128: gi * DM + (k + 1) * 128, 0:256]

            se_b4 = ctx.semv["se"]
            emit_stream_mm(f"bq{tb}", dsem, 16, 2, q_slab, 256, rhs,
                           sy_pre=[ctx.ev["cc_wb"]], te_pre=xn_ev,
                           release_fn=lambda oc: ("se", se_b4 + oc + 1))
            items = [(h, ap("qbuf")[:, h * TC + tb * 512: h * TC + (tb + 1) * 512],
                      True) for h in range(16)]
            e1 = emit_rope_evac(f"bq{tb}", tb, items, ap("ropec"), ap("ropes"),
                                ve_pre=ctx.ev["rope_rl"])

            def kv_slab(colbase):
                def f(gi, k):
                    return (ap("wb_out")[(2 * gi) * DM + k * 128:
                                         (2 * gi) * DM + (k + 1) * 128,
                                         colbase: colbase + 64],
                            ap("wb_out")[(2 * gi + 1) * DM + k * 128:
                                         (2 * gi + 1) * DM + (k + 1) * 128,
                                         colbase: colbase + 64])
                return f

            se_b5 = ctx.semv["se"]
            emit_stream_mm(f"bk{tb}", dsem, 4, 1, kv_slab(256), 128, rhs,
                           dual_dma=True,
                           release_fn=lambda oc: ("se", se_b5 + oc + 1))
            items = [(kv, ap("u1")[:, kv * TC + tb * 512: kv * TC + (tb + 1) * 512],
                      True) for kv in range(4)]
            e2 = emit_rope_evac(f"bk{tb}", tb, items, ap("ropec"), ap("ropes"),
                                ve_pre=ctx.ev["rope_rl"])
            se_b6 = ctx.semv["se"]
            emit_stream_mm(f"bv{tb}", dsem, 4, 1, kv_slab(320), 128, rhs,
                           dual_dma=True,
                           release_fn=lambda oc: ("se", se_b6 + oc + 1))
            items = [(kv, ap("u1")[:, 4096 + kv * TC + tb * 512:
                                   4096 + kv * TC + (tb + 1) * 512], False)
                     for kv in range(4)]
            e3 = emit_rope_evac(f"bv{tb}", tb, items, ap("ropec"), ap("ropes"))
            ctx.ev[f"bqkv_ev{tb}"] = [e1, e2, e3]

        if sub < 2:
            return None

        ve_t0 = [None]

        def te_t(te):
            for e in ctx.ev["bqkv_ev0"] + ctx.ev["bqkv_ev1"]:
                c.wait("TE", te, e)
            i = 0
            for src_off in (0, 4096):
                for kv in range(4):
                    for ch2 in range(8):
                        if i >= 2:
                            c.wait("TE", te, ("ve", ve_t0[0] + i - 1))
                        ins = te.transpose(
                            ptp[:, (i % 2) * 128:(i % 2) * 128 + 128],
                            ap("u1")[:, src_off + kv * TC + ch2 * 128:
                                     src_off + kv * TC + (ch2 + 1) * 128],
                            ap("identt")[:, :])
                        ctx.ev[("bt_mm", i)] = c.inc(ins, "pe")
                        i += 1

        def ve_t(ve):
            assert ctx.semv["ve"] == ve_t0[0]
            i = 0
            for dst_off in (16384, 24576):
                for kv in range(4):
                    for ch2 in range(8):
                        c.wait("VE", ve, ctx.ev[("bt_mm", i)])
                        ins = ve.tensor_copy(
                            ap("u1")[:,
                                     dst_off + kv * 1024 + ch2 * 128:
                                     dst_off + kv * 1024 + (ch2 + 1) * 128],
                            ptp[:, (i % 2) * 128:(i % 2) * 128 + 128])
                        ctx.ev[("bt_ve", i)] = c.inc(ins, "ve")
                        i += 1
            ctx.ev["bt_all"] = c.now("ve")

        ve_t0[0] = ctx.semv["ve"]
        block.tensor(te_t)
        block.vector(ve_t)

        if sub < 3:
            return None

        def sy_ct(sy):
            c.wait("SY", sy, ctx.ev["bt_all"])
            for j in range(R):
                for hl in range(2):
                    h = 2 * j + hl
                    c.inc(sy.dma_start(
                        out=ap("qk2_in")[j * 384 + hl * 128: j * 384 + (hl + 1) * 128, :],
                        in_=ap("qbuf")[:, h * TC:(h + 1) * TC]), dsem, 16)
                kv = j // 2
                c.inc(sy.dma_start(
                    out=ap("qk2_in")[j * 384 + 256: j * 384 + 384, :],
                    in_=ap("u1")[:, kv * TC:(kv + 1) * TC]), dsem, 16)
                for ch2 in range(8):
                    c.inc(sy.dma_start(
                        out=ap("ktv_in")[j * TC + ch2 * 128: j * TC + (ch2 + 1) * 128, 0:128],
                        in_=ap("u1")[:, 16384 + kv * 1024 + ch2 * 128:
                                     16384 + kv * 1024 + (ch2 + 1) * 128]), dsem, 16)
                    c.inc(sy.dma_start(
                        out=ap("ktv_in")[j * TC + ch2 * 128: j * TC + (ch2 + 1) * 128, 128:256],
                        in_=ap("u1")[:, 24576 + kv * 1024 + ch2 * 128:
                                     24576 + kv * 1024 + (ch2 + 1) * 128]), dsem, 16)
            ctx.ev["qk2_in_done"] = c.now(dsem)

        block.sync(sy_ct)

        def gp_cc(gp):
            c.wait("GP", gp, ctx.ev["qk2_in_done"])
            with allow_ici_all_to_all():
                ctx.ev["cc_qk2"] = c.inc(gp.collective_compute(
                    "AllToAll", ALU.bypass, replica_groups=[list(range(R))],
                    ins=[cap("qk2_in")], outs=[cap("qk2_out")]), "cc")
                ctx.ev["cc_ktv"] = c.inc(gp.collective_compute(
                    "AllToAll", ALU.bypass, replica_groups=[list(range(R))],
                    ins=[cap("ktv_in")], outs=[cap("ktv_out")]), "cc")

        block.gpsimd(gp_cc)

        def sy_l2(sy):
            c.wait("SY", sy, ctx.ev["cc_qk2"])
            for hl in range(2):
                for r in range(R):
                    c.inc(sy.dma_start(
                        out=ap("qbuf")[:, hl * T + r * TC: hl * T + (r + 1) * TC],
                        in_=ap("qk2_out")[r * 384 + hl * 128:
                                          r * 384 + (hl + 1) * 128, :]), dsem, 16)
            for r in range(R):
                c.inc(sy.dma_start(
                    out=ap("u1")[:, r * TC:(r + 1) * TC],
                    in_=ap("qk2_out")[r * 384 + 256: r * 384 + 384, :]), dsem, 16)
            ctx.ev["b_ld"] = c.now(dsem)

        block.sync(sy_l2)

        if sub < 4:
            return None

        pe0 = ctx.semv["pe"]
        ve0 = ctx.semv["ve"]

        def ve_of_seg(sg):
            return ve0 + sg * 70

        def sy_seg(sy):
            c.wait("SY", sy, ctx.ev["cc_ktv"])
            for sg in range(8):
                if sg >= 2:
                    c.wait("SY", sy, ("pe", pe0 + 48 * (sg - 1)))
                    c.wait("SY", sy, ("ve", ve_of_seg(sg - 2) + 36))
                base = 8192 + (sg % 2) * 4096
                for ci2 in range(8):
                    c.inc(sy.dma_start(
                        out=ap("u1")[0:64, base + (2 * ci2) * 128:
                                     base + (2 * ci2 + 2) * 128]
                        .rearrange("p (a b) -> p a b", a=2),
                        in_=ap("ktv_out")[sg * TC + ci2 * 128:
                                          sg * TC + (ci2 + 1) * 128, 0:128]
                        .rearrange("(a j) d -> j a d", a=2)), f"sg{sg % 2}", 16)
                    c.inc(sy.dma_start(
                        out=ap("u1")[0:64, base + 2048 + (2 * ci2) * 128:
                                     base + 2048 + (2 * ci2 + 2) * 128]
                        .rearrange("p (a b) -> p a b", a=2),
                        in_=ap("ktv_out")[sg * TC + ci2 * 128:
                                          sg * TC + (ci2 + 1) * 128, 128:256]
                        .rearrange("(a j) d -> j a d", a=2)), f"sg{sg % 2}", 16)
                ctx.ev[("seg_ld", sg)] = c.now(f"sg{sg % 2}")

        block.sync(sy_seg)

        def ve_scan(ve):
            assert ctx.semv["ve"] == ve0
            c.wait("VE", ve, ctx.ev["pads_zero"])
            c.wait("VE", ve, ctx.ev["b_ld"])
            for sg in range(8):
                base = 8192 + (sg % 2) * 4096
                for half in range(2):
                    gh = sg * 2 + half
                    c.wait("VE", ve, ctx.ev[("seg_ld", sg)])
                    if gh >= 1:
                        c.wait("VE", ve, ("pe", pe0 + 3 * 8 * gh))
                    ins = None
                    for hl in range(2):
                        for cl in range(8):
                            ci = half * 8 + cl
                            ins = ve.tensor_scalar(
                                ap("ktmd")[0:64, (hl * 8 + cl) * 128:
                                           (hl * 8 + cl + 1) * 128],
                                ap("u1")[0:64, base + ci * 128: base + (ci + 1) * 128],
                                ap("kdect")[0:64, hl:hl + 1], None, op0=ALU.mult)
                    c.inc(ins, "ve")
                    for cl in range(8):
                        ci = half * 8 + cl
                        gc = sg * 16 + ci
                        for hl in range(2):
                            c.wait("VE", ve, ("pe", pe0 + 3 * gc + 2))
                            ve.tensor_tensor(
                                ap("smt")[0:64, hl * 64:(hl + 1) * 64],
                                pf[2 + gc % 2][0:64, hl * 64:(hl + 1) * 64],
                                ap("diagt")[:, hl * 64:(hl + 1) * 64], op=ALU.mult)
                            ins = ve.tensor_tensor(
                                ap("tmq")[:, hl * 64:(hl + 1) * 64],
                                ap("qbuf")[:, hl * T + gc * 64: hl * T + (gc + 1) * 64],
                                ap("qdect")[:, hl * 64:(hl + 1) * 64], op=ALU.mult)
                            c.inc(ins, "ve")
                        for hl in range(2):
                            c.wait("VE", ve, ("pe", pe0 + 3 * gc + 1))
                            kvf = ap("kvf0") if hl == 0 else ap("kvf1")
                            kvb = ap("kvb0") if hl == 0 else ap("kvb1")
                            ve.scalar_tensor_tensor(
                                kvf[:, :], kvf[:, :], ap("blkvt")[:, hl:hl + 1],
                                pf[gc % 2][:, hl * 128:(hl + 1) * 128],
                                op0=ALU.mult, op1=ALU.add)
                            ins = ve.tensor_copy(
                                kvb[:, ((gc + 1) % 2) * 128:((gc + 1) % 2) * 128 + 128],
                                kvf[:, :])
                            c.inc(ins, "ve")
                        if gc % 8 == 7:
                            for hl in range(2):
                                c.wait("VE", ve, ("pe", pe0 + 3 * gc + 3))
                                ins = ve.tensor_copy(
                                    ap("u1")[:, 16384 + hl * T + (gc - 7) * 64:
                                             16384 + hl * T + (gc + 1) * 64],
                                    pf[4 + hl][:, :])
                                c.inc(ins, "ve")
                assert ctx.semv["ve"] == ve_of_seg(sg + 1)
            ctx.ev["scan_done"] = c.now("ve")

        def te_scan(te):
            assert ctx.semv["pe"] == pe0
            c.wait("TE", te, ctx.ev["b_ld"])
            for i in range(7):
                c.wait("TE", te, ctx.ev_bank.get(i))
            for sg in range(8):
                base = 8192 + (sg % 2) * 4096
                for ci in range(16):
                    gc = sg * 16 + ci
                    half = ci // 8
                    gh = sg * 2 + half
                    cl = ci % 8
                    decv = ve_of_seg(sg) + half * 35 + 1
                    c.wait("TE", te, ctx.ev[("seg_ld", sg)])
                    c.wait("TE", te, ("ve", decv))
                    if gc >= 2:
                        c.wait("TE", te, ("ve", _kvb_val(sg, ci, ve0) - 0))
                    for hl in range(2):
                        ins = te.matmul(
                            pf[gc % 2][:, hl * 128:(hl + 1) * 128],
                            ap("ktmd")[:, (hl * 8 + cl) * 128:(hl * 8 + cl + 1) * 128],
                            ap("u1")[:, base + 2048 + ci * 128: base + (ci + 1) * 128 + 2048],
                            start=True, stop=True, skip_group_check=True)
                    c.inc(ins, "pe")
                    for hl in range(2):
                        ins = te.matmul(
                            pf[2 + gc % 2][0:64, hl * 64:(hl + 1) * 64],
                            ap("u1")[:, gc * 64:(gc + 1) * 64],
                            ap("qbuf")[:, hl * T + gc * 64: hl * T + (gc + 1) * 64],
                            start=True, stop=True, skip_group_check=True)
                    c.inc(ins, "pe")
                    slot = gc % 8
                    for hl in range(2):
                        c.wait("TE", te, ("ve", _smq_val(sg, ci, ve0)))
                        if gc > 0:
                            c.wait("TE", te, ("ve", _kvb_val2(sg, ci, ve0)))
                        if slot == 0 and gc >= 8:
                            c.wait("TE", te, ("ve", _oev_val(gc // 8 - 1, ve0)))
                        kvb = ap("kvb0") if hl == 0 else ap("kvb1")
                        te.matmul(pf[4 + hl][:, slot * 64:(slot + 1) * 64],
                                  ap("u1")[:, base + 2048 + ci * 128:
                                           base + 2048 + (ci + 1) * 128],
                                  ap("smt")[:, hl * 64:(hl + 1) * 64],
                                  start=True, stop=False, skip_group_check=True)
                        ins = te.matmul(pf[4 + hl][:, slot * 64:(slot + 1) * 64],
                                        kvb[:, (gc % 2) * 128:(gc % 2) * 128 + 128],
                                        ap("tmq")[:, hl * 64:(hl + 1) * 64],
                                        start=False, stop=True, skip_group_check=True)
                    c.inc(ins, "pe")
                    assert ctx.semv["pe"] == pe0 + 3 * gc + 3
            ctx.seal_banks(ctx.ev["scan_done"])

        def _smq_val(sg, ci, ve0_):
            half = ci // 8
            cl = ci % 8
            return ve_of_seg(sg) + half * 35 + 1 + cl * 4 + 2

        def _kvb_val(sg, ci, ve0_):
            gc = sg * 16 + ci - 2
            sg2, ci2 = divmod(gc, 16)
            half = ci2 // 8
            cl = ci2 % 8
            return ve_of_seg(sg2) + half * 35 + 1 + cl * 4 + 4

        def _kvb_val2(sg, ci, ve0_):
            gc = sg * 16 + ci - 1
            sg2, ci2 = divmod(gc, 16)
            half = ci2 // 8
            cl = ci2 % 8
            return ve_of_seg(sg2) + half * 35 + 1 + cl * 4 + 4

        def _oev_val(slotidx, ve0_):
            gc = slotidx * 8 + 7
            sg2, ci2 = divmod(gc, 16)
            half = ci2 // 8
            return ve_of_seg(sg2) + half * 35 + 1 + 8 * 4 + 2

        block.vector(ve_scan)
        block.tensor(te_scan)

        if sub < 5:
            return None

        def sy_o2(sy):
            c.wait("SY", sy, ctx.ev["scan_done"])
            for j in range(R):
                for hl in range(2):
                    c.inc(sy.dma_start(
                        out=ap("o2_in")[j * 256 + hl * 128: j * 256 + (hl + 1) * 128, :],
                        in_=ap("u1")[:, 16384 + hl * T + j * TC:
                                     16384 + hl * T + (j + 1) * TC]), dsem, 16)
            ctx.ev["o2_in_done"] = c.now(dsem)

        block.sync(sy_o2)

        def gp_cc2(gp):
            c.wait("GP", gp, ctx.ev["o2_in_done"])
            with allow_ici_all_to_all():
                ctx.ev["cc_o2"] = c.inc(gp.collective_compute(
                    "AllToAll", ALU.bypass, replica_groups=[list(range(R))],
                    ins=[cap("o2_in")], outs=[cap("o2_out")]), "cc")

        block.gpsimd(gp_cc2)

        def sy_l3(sy):
            c.wait("SY", sy, ctx.ev["cc_o2"])
            for h in range(16):
                c.inc(sy.dma_start(
                    out=ap("qbuf")[:, h * TC:(h + 1) * TC],
                    in_=ap("o2_out")[(h // 2) * 256 + (h % 2) * 128:
                                     (h // 2) * 256 + (h % 2) * 128 + 128, :]),
                    dsem, 16)
            ctx.ev["bo_ld"] = c.now(dsem)

        block.sync(sy_l3)

        for tb in range(2):
            def rhs(k, _tb=tb):
                return ap("qbuf")[:, k * TC + _tb * 512: k * TC + (_tb + 1) * 512]

            def wo_slab(gi, k):
                return ap("wb_out")[gi * DM + k * 128: gi * DM + (k + 1) * 128,
                                    384:640]

            ve_bw2 = ctx.semv["ve"]
            emit_stream_mm(f"bwo{tb}", dsem, 16, 2, wo_slab, 256, rhs,
                           sy_pre=[ctx.ev["bo_ld"]], te_pre=ctx.ev["bo_ld"],
                           release_fn=lambda oc, b=ve_bw2: ("ve", b + oc + 1))

            def ve_ev(ve, _tb=tb, _vb=ve_bw2):
                assert ctx.semv["ve"] == _vb
                ev = None
                for oc in range(16):
                    bk = ctx.ev[(f"bwo{_tb}", "bank", oc)]
                    c.wait("VE", ve, ctx.ev[(f"bwo{_tb}", "mm", oc)])
                    xs = ap("x32")[:, oc * TC + _tb * 512: oc * TC + (_tb + 1) * 512]
                    ins = ve.tensor_tensor(xs, xs, pf[bk][:, :], op=ALU.add)
                    ev = c.inc(ins, "ve")
                ctx.ev[f"res3_{_tb}"] = ev

            block.vector(ve_ev)

    def phase_store():
        def syb(sy):
            for s_ in list(ctx.semv):
                c.wait("SY", sy, (s_, ctx.semv[s_]))
            for ch in range(NCH):
                c.inc(sy.dma_start(
                    out=ap("out_x")[:, ch * TC:(ch + 1) * TC],
                    in_=ap("x32")[:, ch * TC:(ch + 1) * TC]), "d5", 16)
            c.wait("SY", sy, c.now("d5"))

        block.sync(syb)

    def phase_dbg():
        def gpd(gp):
            c.wait("GP", gp, ctx.ev["cc_ug"])
            c.wait("GP", gp, ctx.ev["cc_wd"])
            c.wait("GP", gp, ctx.ev["cc_wb"])
            c.inc(gp.dma_start(out=ap("dbg_ug")[:, :], in_=ap("ug_out")[:, :]), "g1", 16)
            c.inc(gp.dma_start(out=ap("dbg_wd")[:, :], in_=ap("wd_out")[:, :]), "g1", 16)
            c.inc(gp.dma_start(out=ap("dbg_wb")[:, :], in_=ap("wb_out")[:, :]), "g1", 16)
            c.inc(gp.dma_start(out=ap("dbg_wdi")[:, :], in_=ap("wdi")[:, :]), "g1", 16)
        block.gpsimd(gpd)

    phase1()
    fold_wd()
    phase2()
    if upto >= 2:
        phase_ffn(1, "d2", ctx.ev["o_in_done"])
    if 2 < upto < 3:
        phase4(None, sub=int(round((upto - 2) * 10)))
    if upto >= 3:
        phase4(None)
    if upto >= 4:
        phase_ffn(2, "d4", ctx.ev["o2_in_done"])
    if debug_wts:
        phase_dbg()
    phase_store()


def _hadamard(n):
    h = np.array([[1.0]], dtype=np.float64)
    while h.shape[0] < n:
        h = np.block([[h, h], [h, -h]])
    return h


def _rope_tables(pos, base):
    half = 64
    inv = base ** -(np.arange(half, dtype=np.float64) / half)
    ang = np.outer(inv, pos.astype(np.float64))
    cos = np.cos(ang)
    sin = np.sin(ang)
    return (np.concatenate([cos, cos], axis=0),
            np.concatenate([sin, sin], axis=0))


_NC_CACHE = {}


def _get_nc(upto, debug_wts=False):
    key = (upto, debug_wts)
    if key not in _NC_CACHE:
        _NC_CACHE[key] = build_program(upto, debug_wts)
    return _NC_CACHE[key]


def build_in_maps(L0_attn_a_x, L0_attn_a_Wq, L0_attn_a_bq, L0_attn_a_Wk,
                  L0_attn_a_bk, L0_attn_a_Wv, L0_attn_a_bv, L0_attn_a_Wo,
                  L0_attn_b_wq, L0_attn_b_wk, L0_attn_b_wv, L0_attn_b_wo,
                  L0_ffn_SU_ug, L0_ffn_SU_dp, L0_ffn_Wscale_ug,
                  L0_ffn_Wscale_dp, L0_ffn_W_up, L0_ffn_W_gate, L0_ffn_W_down):
    f32 = np.float32
    asn = np.ascontiguousarray
    x = np.asarray(L0_attn_a_x)
    Wq = np.asarray(L0_attn_a_Wq)
    Wk = np.asarray(L0_attn_a_Wk)
    Wv = np.asarray(L0_attn_a_Wv)
    Wo = np.asarray(L0_attn_a_Wo)
    bq = np.asarray(L0_attn_a_bq)
    bk = np.asarray(L0_attn_a_bk)
    bv = np.asarray(L0_attn_a_bv)
    wqT = asn(np.asarray(L0_attn_b_wq, f32).T)
    wkT = asn(np.asarray(L0_attn_b_wk, f32).T)
    wvT = asn(np.asarray(L0_attn_b_wv, f32).T)
    woT = asn(np.asarray(L0_attn_b_wo, f32).T)
    wupT = asn(np.asarray(L0_ffn_W_up, f32).T)
    wgT = asn(np.asarray(L0_ffn_W_gate, f32).T)
    wdT = asn(np.asarray(L0_ffn_W_down, f32).T)
    su_ug = np.asarray(L0_ffn_SU_ug, f32)
    su_dp = np.asarray(L0_ffn_SU_dp, f32)
    ws_ug = np.asarray(L0_ffn_Wscale_ug, f32)
    ws_dp = np.asarray(L0_ffn_Wscale_dp, f32)

    Hpm = _hadamard(2048).astype(bfdt)
    H64 = _hadamard(64).astype(bfdt)
    H128 = _hadamard(128).astype(bfdt)
    ident = np.eye(128, dtype=bfdt)
    ones = np.ones((128, 128), dtype=bfdt)
    Rm = np.zeros((128, 128), np.float64)
    for d in range(64):
        Rm[d, d + 64] = -1.0
        Rm[d + 64, d] = 1.0
    Rpm = np.ascontiguousarray(Rm.T).astype(bfdt)
    mask = (np.arange(128)[:, None] <= (np.arange(896)[None, :] - 384)).astype(bfdt)
    lam = np.exp2(-8.0 * np.arange(1, 17, dtype=np.float64) / 16.0)

    in_maps = []
    for cid in range(R):
        sl = slice(cid * TC, (cid + 1) * TC)
        pos = np.arange(cid * TC, (cid + 1) * TC)
        ca, sa = _rope_tables(pos, 1e6)
        cb, sb2 = _rope_tables(pos, 1e4)
        heads = [2 * cid, 2 * cid + 1]
        diagT = np.zeros((64, 128), dtype=f32)
        qdec = np.zeros((128, 128), dtype=f32)
        kdec = np.zeros((128, 2), dtype=f32)
        blkv = np.zeros((128, 2), dtype=f32)
        rr = np.arange(64, dtype=np.float64)
        for hi_, h in enumerate(heads):
            diff = rr[None, :] - rr[:, None]
            diagT[:, hi_ * 64:(hi_ + 1) * 64] = np.where(
                diff >= 0, np.exp(-lam[h] * diff), 0.0)
            qdec[:, hi_ * 64:(hi_ + 1) * 64] = np.exp(-lam[h] * (rr + 1.0))[None, :]
            kdec[0:64, hi_] = np.exp(-lam[h] * (63.0 - rr))
            kdec[64:128, hi_] = kdec[0:64, hi_]
            blkv[:, hi_] = math.exp(-lam[h] * 64.0)
        in_maps.append({
            "x_fm": asn(x[0, sl, :].T),
            "Wq": Wq, "Wk": Wk, "Wv": Wv, "Wo": Wo,
            "bqr": asn(bq.reshape(1, 2048)),
            "bkr": asn(bk.reshape(1, 512)),
            "bvr": asn(bv.reshape(1, 512)),
            "rpmc": Rpm,
            "wqT_sh": asn(wqT[:, cid * 256:(cid + 1) * 256]),
            "wkT_sh": asn(wkT[:, cid * 64:(cid + 1) * 64]),
            "wvT_sh": asn(wvT[:, cid * 64:(cid + 1) * 64]),
            "woT_sh": asn(woT[:, cid * 256:(cid + 1) * 256]),
            "wupT_sh": asn(wupT[:, cid * 1024:(cid + 1) * 1024]),
            "wgT_sh": asn(wgT[:, cid * 1024:(cid + 1) * 1024]),
            "wdT_sh": asn(wdT[:, cid * 256:(cid + 1) * 256]),
            "su_ug2": asn(su_ug.reshape(16, 128).T),
            "su_dp2": asn(su_dp.reshape(64, 128).T),
            "wsup2": asn(ws_ug[:INTER].reshape(64, 128).T),
            "wsg2": asn(ws_ug[INTER:].reshape(64, 128).T),
            "wsdp2": asn(ws_dp.reshape(16, 128).T),
            "Hpm": Hpm, "H64c": H64, "H128c": H128,
            "identc": ident, "onesc": ones, "maskcc": mask,
            "ra_cos": asn(ca.astype(bfdt)), "ra_sin": asn(sa.astype(bfdt)),
            "rb_cos": asn(cb.astype(bfdt)), "rb_sin": asn(sb2.astype(bfdt)),
            "diagTc": diagT, "qdecc": asn(qdec.astype(bfdt)),
            "kdecc": kdec, "blkvc": blkv,
        })
    return in_maps


def kernel(_upto=4, _debug_wts=False, **inputs):
    in_maps = build_in_maps(**inputs)
    nc = _get_nc(_upto, _debug_wts)
    res = run_bass_kernel_spmd(nc, in_maps, core_ids=list(range(R)))
    if _debug_wts:
        kernel._dbg = {k: np.asarray(res.results[0][k]) for k in ("dbg_ug", "dbg_wd", "dbg_wb", "dbg_wdi", "dbg_xh", "dbg_z1")}
        kernel._dbg_all = [{k: np.asarray(r[k]) for k in ("dbg_ug", "dbg_wd", "dbg_xh", "dbg_z1")} for r in res.results]
    out = np.empty((1, T, DM), np.float32)
    for cid in range(R):
        arr = np.asarray(res.results[cid]["out_x"])
        out[0, cid * TC:(cid + 1) * TC, :] = (
            arr.reshape(128, NCH, TC).transpose(2, 1, 0).reshape(TC, DM))
    return out


# version 02
# speedup vs baseline: 388.5970x; 388.5970x over previous
import math
from contextlib import ExitStack

import numpy as np
import ml_dtypes

import concourse.bass as bass
import concourse.mybir as mybir
from concourse.bass_utils import run_bass_kernel_spmd
from concourse.aot_env import allow_ici_all_to_all

BF16 = mybir.dt.bfloat16
F32 = mybir.dt.float32
AF = mybir.ActivationFunctionType
ALU = mybir.AluOpType
bfdt = ml_dtypes.bfloat16

R = 8
T = 8192
DM = 2048
TC = 1024
NCH = 16
INTER = 8192
EPS = 1e-5
SM_SCALE = 1.0 / math.sqrt(128.0)



class Ctx:
    def __init__(self):
        self.semh = {}
        self.semv = {}
        self.hw = {"TE": {}, "VE": {}, "SE": {}, "SY": {}, "GP": {}}
        self.ev = {}
        self.ev_bank = {}
        self.slab_ev = [None, None, None, None]
        self.slab_i = 0
        self.rope_i = 0
        self.rope_ev = {}
        self.dsi = 0

    def next_dsem(self):
        d = f"ds{self.dsi % 4}"
        self.dsi += 1
        return d

    def add_sem(self, name, handle):
        self.semh[name] = handle
        self.semv[name] = 0

    def inc(self, instr, s, n=1):
        self.semv[s] += n
        instr.then_inc(self.semh[s], n)
        return (s, self.semv[s])

    def wait(self, engname, eng, ev):
        if ev is None:
            return
        s, v = ev
        if v <= 0:
            return
        if self.hw[engname].get(s, 0) >= v:
            return
        eng.wait_ge(self.semh[s], v)
        self.hw[engname][s] = v

    def now(self, s):
        return (s, self.semv[s])

    def seal_banks(self, ev):
        for i in range(7):
            self.ev_bank[i] = ev


class BankPool:
    def __init__(self, ctx):
        self.ctx = ctx
        self.i = 0

    def acquire(self, te):
        b = self.i % 7
        self.i += 1
        self.ctx.wait("TE", te, self.ctx.ev_bank.get(b))
        return b

    def release(self, b, ev):
        self.ctx.ev_bank[b] = ev


def build_program(upto=4, debug_wts=False):
    nc = bass.Bass("TRN2", target_bir_lowering=False)
    D = {}

    def din(name, shape, dt):
        D[name] = nc.dram_tensor(name, shape, dt, kind="ExternalInput")

    din("x_fm", [DM, TC], BF16)
    din("Wq", [DM, DM], BF16)
    din("Wk", [DM, 512], BF16)
    din("Wv", [DM, 512], BF16)
    din("Wo", [DM, DM], BF16)
    din("bqr", [1, 2048], BF16)
    din("bkr", [1, 512], BF16)
    din("bvr", [1, 512], BF16)
    din("rpmc", [128, 128], BF16)
    din("wqT_sh", [DM, 256], F32)
    din("wkT_sh", [DM, 64], F32)
    din("wvT_sh", [DM, 64], F32)
    din("woT_sh", [DM, 256], F32)
    din("wupT_sh", [DM, 1024], F32)
    din("wgT_sh", [DM, 1024], F32)
    din("wdT_sh", [INTER, 256], F32)
    din("su_ug2", [128, 16], F32)
    din("su_dp2", [128, 64], F32)
    din("wsup2", [128, 64], F32)
    din("wsg2", [128, 64], F32)
    din("wsdp2", [128, 16], F32)
    din("Hpm", [DM, DM], BF16)
    din("H64c", [64, 64], BF16)
    din("H128c", [128, 128], BF16)
    din("identc", [128, 128], BF16)
    din("onesc", [128, 128], BF16)
    din("maskcc", [128, 896], BF16)
    din("ra_cos", [128, TC], BF16)
    din("ra_sin", [128, TC], BF16)
    din("rb_cos", [128, TC], BF16)
    din("rb_sin", [128, TC], BF16)
    din("diagTc", [64, 128], F32)
    din("qdecc", [128, 128], BF16)
    din("kdecc", [128, 2], F32)
    din("blkvc", [128, 2], F32)
    D["out_x"] = nc.dram_tensor("out_x", [128, NCH * TC], F32, kind="ExternalOutput")
    if debug_wts:
        D["dbg_ug"] = nc.dram_tensor("dbg_ug", [R * DM, 2048], BF16, kind="ExternalOutput")
        D["dbg_wd"] = nc.dram_tensor("dbg_wd", [R * INTER, 256], BF16, kind="ExternalOutput")
        D["dbg_wb"] = nc.dram_tensor("dbg_wb", [R * DM, 640], BF16, kind="ExternalOutput")
        D["dbg_wdi"] = nc.dram_tensor("dbg_wdi", [INTER, 256], BF16, kind="ExternalOutput")
        D["dbg_xh"] = nc.dram_tensor("dbg_xh", [128, NCH * TC], BF16, kind="ExternalOutput")
        D["dbg_z1"] = nc.dram_tensor("dbg_z1", [128, 2 * NCH * TC], BF16, kind="ExternalOutput")

    for name, shape, dt in [
        ("ug_in", [DM, 2048], BF16), ("ug_out", [R * DM, 2048], BF16),
        ("wb_in", [DM, 640], BF16), ("wb_out", [R * DM, 640], BF16),
        ("wdi", [INTER, 256], BF16), ("wd_in", [INTER, 256], BF16),
        ("wd_out", [R * INTER, 256], BF16),
        ("qk_in", [R * 384, TC], BF16), ("qk_out", [R * 384, TC], BF16),
        ("v_in", [R * TC, 128], BF16), ("v_out", [R * TC, 128], BF16),
        ("o_in", [R * 256, TC], BF16), ("o_out", [R * 256, TC], BF16),
        ("qk2_in", [R * 384, TC], BF16), ("qk2_out", [R * 384, TC], BF16),
        ("ktv_in", [R * TC, 256], BF16), ("ktv_out", [R * TC, 256], BF16),
        ("o2_in", [R * 256, TC], BF16), ("o2_out", [R * 256, TC], BF16),
    ]:
        D[name] = nc.dram_tensor(name, shape, dt)

    with ExitStack() as st:
        def sbt(name, shape, dt):
            D[name] = st.enter_context(nc.sbuf_tensor(name, shape, dt))

        sbt("x32", [128, NCH * TC], F32)
        sbt("qbuf", [128, NCH * TC], BF16)
        sbt("u1", [128, 2 * NCH * TC], BF16)
        sbt("wstream", [128, 4 * 512], BF16)
        sbt("t1", [128, 512], F32)
        sbt("t2", [128, 512], F32)
        sbt("tg", [128, 512], F32)
        sbt("ptile", [128, 3 * 512], BF16)
        sbt("mrow", [1, 512], F32)
        sbt("vrow", [1, 512], F32)
        sbt("wrow", [1, 512], F32)
        sbt("mr_row", [1, 2 * 512], BF16)
        sbt("rlrow", [1, 512], F32)
        sbt("epsrow", [1, 1], F32)
        sbt("rlrow_bf", [1, 512], BF16)
        sbt("ropec", [128, TC], BF16)
        sbt("ropes", [128, TC], BF16)
        sbt("maskt", [128, 896], BF16)
        sbt("diagt", [64, 128], F32)
        sbt("qdect", [128, 128], BF16)
        sbt("kdect", [128, 2], F32)
        sbt("blkvt", [128, 2], F32)
        sbt("identt", [128, 128], BF16)
        sbt("onest", [128, 128], BF16)
        sbt("h64t", [64, 64], BF16)
        sbt("h128t", [128, 128], BF16)
        sbt("sut", [128, 16], F32)
        sbt("suwut", [128, 64], F32)
        sbt("wsupt", [128, 64], F32)
        sbt("wsgt", [128, 64], F32)
        sbt("wsdpt", [128, 16], F32)
        sbt("bqrt", [1, 2048], BF16)
        sbt("bkrt", [1, 512], BF16)
        sbt("bvrt", [1, 512], BF16)
        sbt("rpmt", [128, 128], BF16)
        sbt("kvf0", [128, 128], F32)
        sbt("kvf1", [128, 128], F32)
        sbt("kvb0", [128, 2 * 128], BF16)
        sbt("kvb1", [128, 2 * 128], BF16)
        sbt("smt", [128, 2 * 64], BF16)
        sbt("tmq", [128, 2 * 64], BF16)
        sbt("ktmd", [128, 2 * 8 * 128], BF16)
        sbt("fold_a", [64, 2 * 512], BF16)

        pf = [st.enter_context(nc.psum_tensor(f"pf{i}", [128, 512], F32))
              for i in range(7)]
        ptp = st.enter_context(nc.psum_tensor("ptp", [128, 1024], BF16))
        D["pf"] = pf
        D["ptp"] = ptp

        semnames = ["d0", "d1", "d2", "d3", "d4", "d5",
                    "ds0", "ds1", "ds2", "ds3", "fs",
                    "fa0", "fa1", "fa2", "fa3", "sa0", "sa1", "sa2", "sa3",
                    "sg0", "sg1", "g0", "g1", "pe", "ve", "se", "cc"]
        for n in semnames:
            D["sem_" + n] = st.enter_context(nc.semaphore(n))

        block = st.enter_context(nc.Block())
        ctx = Ctx()
        for n in semnames:
            ctx.add_sem(n, D["sem_" + n])

        _emit(ctx, block, D, upto, debug_wts)

    return nc


def _emit(ctx, block, D, upto, debug_wts=False):
    c = ctx
    pf = D["pf"]
    ptp = D["ptp"]
    rot = BankPool(ctx)

    def ap(name):
        return D[name]

    def cap(name):
        return D[name].ap().opt()

    def p0_sync(sy):
        def ld(dst, src):
            return c.inc(sy.dma_start(out=dst, in_=src), "d0", 16)

        ld(ap("identt")[:, :], ap("identc")[:, :])
        ld(ap("onest")[:, :], ap("onesc")[:, :])
        ld(ap("h64t")[:, :], ap("H64c")[:, :])
        ld(ap("h128t")[:, :], ap("H128c")[:, :])
        ld(ap("maskt")[:, :], ap("maskcc")[:, :])
        ld(ap("ropec")[:, :], ap("ra_cos")[:, :])
        ld(ap("ropes")[:, :], ap("ra_sin")[:, :])
        ld(ap("diagt")[:, :], ap("diagTc")[:, :])
        ld(ap("qdect")[:, :], ap("qdecc")[:, :])
        ld(ap("kdect")[:, :], ap("kdecc")[:, :])
        ld(ap("blkvt")[:, :], ap("blkvc")[:, :])
        ld(ap("sut")[:, :], ap("su_ug2")[:, :])
        ld(ap("suwut")[:, :], ap("su_dp2")[:, :])
        ld(ap("wsupt")[:, :], ap("wsup2")[:, :])
        ld(ap("wsgt")[:, :], ap("wsg2")[:, :])
        ld(ap("wsdpt")[:, :], ap("wsdp2")[:, :])
        ld(ap("bqrt")[:, :], ap("bqr")[:, :])
        ld(ap("bkrt")[:, :], ap("bkr")[:, :])
        ld(ap("bvrt")[:, :], ap("bvr")[:, :])
        ld(ap("rpmt")[:, :], ap("rpmc")[:, :])
        for ch in range(NCH):
            ld(ap("u1")[:, 8192 + ch * TC: 8192 + (ch + 1) * TC],
               ap("x_fm")[ch * 128:(ch + 1) * 128, :])
        ctx.ev["p0_loads"] = c.now("d0")

    block.sync(p0_sync)

    def p0_gp(gp):
        for lc in range(8):
            c.inc(gp.dma_start(out=ap("ug_in")[:, lc * 256: lc * 256 + 128],
                               in_=ap("wupT_sh")[:, lc * 128:(lc + 1) * 128]), "g0", 16)
            c.inc(gp.dma_start(out=ap("ug_in")[:, lc * 256 + 128: lc * 256 + 256],
                               in_=ap("wgT_sh")[:, lc * 128:(lc + 1) * 128]), "g0", 16)
        c.inc(gp.dma_start(out=ap("wb_in")[:, 0:256], in_=ap("wqT_sh")[:, :]), "g0", 16)
        c.inc(gp.dma_start(out=ap("wb_in")[:, 256:320], in_=ap("wkT_sh")[:, :]), "g0", 16)
        c.inc(gp.dma_start(out=ap("wb_in")[:, 320:384], in_=ap("wvT_sh")[:, :]), "g0", 16)
        c.inc(gp.dma_start(out=ap("wb_in")[:, 384:640], in_=ap("woT_sh")[:, :]), "g0", 16)
        ctx.ev["ugwb_in"] = c.now("g0")

    block.gpsimd(p0_gp)

    def p0_se(se):
        c.wait("SE", se, ctx.ev["p0_loads"])
        ins = None
        for ch in range(NCH):
            ins = se.copy(ap("x32")[:, ch * TC:(ch + 1) * TC],
                          ap("u1")[:, 8192 + ch * TC: 8192 + (ch + 1) * TC])
        ctx.ev["x32_ready"] = c.inc(ins, "se")

    block.scalar(p0_se)

    def p0_ve(ve):
        c.wait("VE", ve, ctx.ev["p0_loads"])
        ve.tensor_scalar(ap("wsupt")[:, :], ap("wsupt")[:, :],
                         1.0 / math.sqrt(2048.0), None, op0=ALU.mult)
        ve.tensor_scalar(ap("wsgt")[:, :], ap("wsgt")[:, :],
                         1.0 / math.sqrt(2048.0), None, op0=ALU.mult)
        ve.tensor_scalar(ap("wsdpt")[:, :], ap("wsdpt")[:, :],
                         1.0 / math.sqrt(8192.0), None, op0=ALU.mult)
        ins = ve.tensor_tensor(ap("suwut")[:, :], ap("suwut")[:, :],
                               ap("wsupt")[:, :], op=ALU.mult)
        ctx.ev["tabs_ready"] = c.inc(ins, "ve")
        ve.memset(ap("epsrow")[:, :], EPS)
        ve.memset(ap("smt")[:, :], 0)
        ve.memset(ap("ktmd")[:, :], 0)
        ve.memset(ap("kvb0")[:, :], 0)
        ve.memset(ap("kvb1")[:, :], 0)
        ve.memset(ap("kvf0")[:, :], 0)
        ins = ve.memset(ap("kvf1")[:, :], 0)
        ctx.ev["pads_zero"] = c.inc(ins, "ve")

    block.vector(p0_ve)

    def slab_slot():
        s = ctx.slab_i % 4
        ctx.slab_i += 1
        return s, ctx.slab_ev[s]

    def emit_ln(tb, su, xn_dst, pre_ev=None):
        x32 = ap("x32")

        def ve1(ve):
            c.wait("VE", ve, ctx.ev["x32_ready"])
            c.wait("VE", ve, pre_ev)
            ins = None
            for ch in range(NCH):
                xs = x32[:, ch * TC + tb * 512: ch * TC + (tb + 1) * 512]
                ve.tensor_copy(ap("u1")[:, 16384 + ch * 512: 16384 + (ch + 1) * 512], xs)
                ins = ve.tensor_tensor(
                    ap("u1")[:, 24576 + ch * 512: 24576 + (ch + 1) * 512],
                    xs, xs, op=ALU.mult)
            ctx.ev["ln_pre"] = c.inc(ins, "ve")

        def te1(te):
            bs = rot.acquire(te)
            bq_ = rot.acquire(te)
            c.wait("TE", te, ctx.ev["ln_pre"])
            for ch in range(NCH):
                te.matmul(pf[bs][0:1, :], ap("onest")[:, 0:1],
                          ap("u1")[:, 16384 + ch * 512: 16384 + (ch + 1) * 512],
                          start=(ch == 0), stop=(ch == NCH - 1))
            for ch in range(NCH):
                ins = te.matmul(pf[bq_][0:1, :], ap("onest")[:, 0:1],
                                ap("u1")[:, 24576 + ch * 512: 24576 + (ch + 1) * 512],
                                start=(ch == 0), stop=(ch == NCH - 1))
            ctx.ev["ln_sums"] = c.inc(ins, "pe")
            ctx.ev["ln_banks"] = (bs, bq_)

        def ve2(ve):
            bs, bq_ = ctx.ev["ln_banks"]
            c.wait("VE", ve, ctx.ev["ln_sums"])
            ve.tensor_scalar(ap("mrow")[:, :], pf[bs][0:1, :], 1.0 / DM, None,
                             op0=ALU.mult)
            ve.tensor_scalar(ap("vrow")[:, :], pf[bq_][0:1, :], 1.0 / DM, None,
                             op0=ALU.mult)
            ve.tensor_tensor(ap("wrow")[:, :], ap("mrow")[:, :], ap("mrow")[:, :],
                             op=ALU.mult)
            ve.tensor_copy(ap("mr_row")[:, 0:512], ap("mrow")[:, :])
            ins = ve.tensor_tensor(ap("vrow")[:, :], ap("vrow")[:, :],
                                   ap("wrow")[:, :], op=ALU.subtract)
            ctx.ev["ln_var"] = c.inc(ins, "ve")
            rot.release(bs, ctx.ev["ln_var"])
            rot.release(bq_, ctx.ev["ln_var"])

        def se1(se):
            c.wait("SE", se, ctx.ev["ln_var"])
            se.copy(ap("wrow")[:, :], ap("mrow")[:, :])
            se.activation(ap("wrow")[:, 0:256], ap("vrow")[:, 0:256], AF.Ln,
                          bias=ap("epsrow")[:, :])
            se.activation(ap("wrow")[:, 256:512], ap("vrow")[:, 256:512], AF.Ln,
                          bias=ap("epsrow")[:, :])
            se.activation(ap("wrow")[:, 0:256], ap("wrow")[:, 0:256], AF.Exp,
                          scale=-0.5)
            ins = se.activation(ap("wrow")[:, 256:512], ap("wrow")[:, 256:512],
                                AF.Exp, scale=-0.5)
            ctx.ev["ln_rstd"] = c.inc(ins, "se")

        def ve3(ve):
            c.wait("VE", ve, ctx.ev["ln_rstd"])
            ve.tensor_copy(ap("mr_row")[:, 0:512], ap("mrow")[:, :])
            ins = ve.tensor_copy(ap("mr_row")[:, 512:1024], ap("wrow")[:, :])
            ctx.ev["ln_mr"] = c.inc(ins, "ve")

        def te2(te):
            b1 = rot.acquire(te)
            b2 = rot.acquire(te)
            c.wait("TE", te, ctx.ev["ln_mr"])
            te.matmul(pf[b1][:, :], ap("onest")[0:1, :], ap("mr_row")[:, 0:512],
                      start=True, stop=True)
            ins = te.matmul(pf[b2][:, :], ap("onest")[0:1, :], ap("mr_row")[:, 512:1024],
                            start=True, stop=True)
            ctx.ev["ln_bc"] = c.inc(ins, "pe")
            ctx.ev["ln_bcb"] = (b1, b2)

        def ve4(ve):
            b1, b2 = ctx.ev["ln_bcb"]
            c.wait("VE", ve, ctx.ev["ln_bc"])
            ve.tensor_copy(ap("ptile")[:, 0:512], pf[b1][:, :])
            ins = ve.tensor_copy(ap("ptile")[:, 512:1024], pf[b2][:, :])
            ev = c.inc(ins, "ve")
            rot.release(b1, ev)
            rot.release(b2, ev)
            pend = []
            for ch in range(NCH):
                xs = x32[:, ch * TC + tb * 512: ch * TC + (tb + 1) * 512]
                tmp = ap("t1") if ch % 2 == 0 else ap("t2")
                ve.tensor_tensor(tmp[:, :], xs, ap("ptile")[:, 0:512],
                                 op=ALU.subtract)
                if pend:
                    ch0, t0 = pend.pop()
                    ve.scalar_tensor_tensor(
                        xn_dst[:, ch0 * 512:(ch0 + 1) * 512], t0[:, :],
                        (ap("sut")[:, ch0:ch0 + 1] if su else 1.0),
                        ap("ptile")[:, 512:1024], op0=ALU.mult, op1=ALU.mult)
                pend.append((ch, tmp))
            ch0, t0 = pend.pop()
            ins = ve.scalar_tensor_tensor(
                xn_dst[:, ch0 * 512:(ch0 + 1) * 512], t0[:, :],
                (ap("sut")[:, ch0:ch0 + 1] if su else 1.0),
                ap("ptile")[:, 512:1024], op0=ALU.mult, op1=ALU.mult)
            ctx.ev["xn_ready"] = c.inc(ins, "ve")

        block.vector(ve1)
        block.tensor(te1)
        block.vector(ve2)
        block.scalar(se1)
        block.vector(ve3)
        block.tensor(te2)
        block.vector(ve4)
        return ctx.ev["xn_ready"]

    QROT = 24576

    def emit_rope_evac(name, tb, items, tabc, tabs, ve_pre=None):
        ve_base = ctx.semv["ve"]
        se_base = ctx.semv["se"]
        n_rope = sum(1 for it in items if it[2])
        rbanks = {}
        slots = []
        slot_wait = []
        for ri in range(n_rope):
            sl = ctx.rope_i % 16
            ctx.rope_i += 1
            slots.append(sl)
            slot_wait.append(ctx.rope_ev.get(sl))
            ctx.rope_ev[sl] = ("ve", ve_base + ri + 1)

        def seb(se):
            assert ctx.semv["se"] == se_base
            ri = 0
            for oc, dst, do_rope in items:
                bk = ctx.ev[(name, "bank", oc)]
                c.wait("SE", se, ctx.ev[(name, "mm", oc)])
                if do_rope:
                    c.wait("SE", se, slot_wait[ri])
                    sl = slots[ri]
                    ins = se.copy(
                        ap("u1")[:, QROT + sl * 512: QROT + (sl + 1) * 512],
                        pf[bk][:, :])
                    ctx.ev[(name, "qb", ri)] = c.inc(ins, "se")
                    ri += 1
                else:
                    ins = se.copy(dst, pf[bk][:, :])
                    ev = c.inc(ins, "se")
                    ctx.ev[(name, "evdone")] = ev

        def teb(te):
            ri = 0
            for oc, dst, do_rope in items:
                if not do_rope:
                    continue
                rb = rot.acquire(te)
                c.wait("TE", te, ctx.ev[(name, "qb", ri)])
                sl = slots[ri]
                ins = te.matmul(
                    pf[rb][:, :], ap("rpmt")[:, :],
                    ap("u1")[:, QROT + sl * 512: QROT + (sl + 1) * 512],
                    start=True, stop=True)
                c.inc(ins, "pe")
                ctx.ev[(name, "rot", ri)] = c.now("pe")
                rbanks[ri] = rb
                rot.release(rb, ("ve", ve_base + ri + 1))
                ri += 1

        def veb(ve):
            assert ctx.semv["ve"] == ve_base
            c.wait("VE", ve, ve_pre)
            cos = tabc[:, tb * 512:(tb + 1) * 512]
            sin = tabs[:, tb * 512:(tb + 1) * 512]
            ritems = [(oc, dst) for oc, dst, dr in items if dr]
            pend = []
            for ri, (oc, dst) in enumerate(ritems):
                rb = rbanks[ri]
                sl = slots[ri]
                c.wait("VE", ve, ctx.ev[(name, "rot", ri)])
                slap = ap("u1")[:, QROT + sl * 512: QROT + (sl + 1) * 512]
                tmp = ap("t2") if ri % 2 == 0 else ap("tg")
                ve.tensor_tensor(dst, slap, cos, op=ALU.mult)
                ve.tensor_tensor(tmp[:, :], pf[rb][:, :], sin, op=ALU.mult)
                if pend:
                    d0, t0 = pend.pop()
                    c.inc(ve.tensor_tensor(d0, d0, t0[:, :], op=ALU.add), "ve")
                pend.append((dst, tmp))
            d0, t0 = pend.pop()
            c.inc(ve.tensor_tensor(d0, d0, t0[:, :], op=ALU.add), "ve")
            assert ctx.semv["ve"] == ve_base + n_rope
            ctx.ev[(name, "evdone")] = c.now("ve")

        block.scalar(seb)
        block.tensor(teb)
        if n_rope:
            block.vector(veb)
        return ctx.ev[(name, "evdone")]

    def emit_stream_mm(name, dsem, n_oc, grp, slab_fn, slab_w, rhs_fn,
                       sy_pre=None, te_pre=None, nK=NCH, dual_dma=False,
                       bias_fn=None, release_fn=None):
        dsem = ctx.next_dsem()
        dsem2 = ctx.next_dsem()
        ngrp = (n_oc + grp - 1) // grp
        pe_base = ctx.semv["pe"]
        n_slab = ngrp * nK
        plan = []
        si = 0
        for gi in range(ngrp):
            for k in range(nK):
                plan.append((gi, k, si))
                si += 1

        def loader(eng, engname, parity, qsem):
            for s_, v_ in ([] if sy_pre is None else sy_pre):
                c.wait(engname, eng, (s_, v_))
            last = None
            for gi, k, i in plan:
                if i % 2 != parity:
                    continue
                s = i % 4
                c.wait(engname, eng, ctx.slab_ev[s])
                src = slab_fn(gi, k)
                if dual_dma:
                    i1 = eng.dma_start(
                        out=ap("wstream")[:, s * 512: s * 512 + slab_w // 2],
                        in_=src[0])
                    c.inc(i1, qsem, 16)
                    ins = eng.dma_start(
                        out=ap("wstream")[:, s * 512 + slab_w // 2: s * 512 + slab_w],
                        in_=src[1])
                    ev = c.inc(ins, qsem, 16)
                else:
                    ins = eng.dma_start(
                        out=ap("wstream")[:, s * 512: s * 512 + slab_w],
                        in_=src)
                    ev = c.inc(ins, qsem, 16)
                ctx.ev[(name, "slab", i)] = (ev, s)
                ctx.slab_ev[s] = ("pe", pe_base + i + 1)
                last = (gi, k, s)
            gi, k, s = last
            src = slab_fn(gi, k)
            if dual_dma:
                src = src[1]
                ins = eng.dma_start(
                    out=ap("wstream")[:, s * 512 + slab_w // 2: s * 512 + slab_w],
                    in_=src)
            else:
                ins = eng.dma_start(
                    out=ap("wstream")[:, s * 512: s * 512 + slab_w],
                    in_=src)
            ctx.ev[(name, "slab_tail", parity)] = c.inc(ins, qsem, 16)

        def syb(sy):
            loader(sy, "SY", 0, dsem)

        def gpb(gp):
            loader(gp, "GP", 1, dsem2)

        def teb(te):
            assert ctx.semv["pe"] == pe_base, (name, ctx.semv["pe"], pe_base)
            if te_pre is not None:
                c.wait("TE", te, te_pre)
            for gi in range(ngrp):
                ocs = [gi * grp + j for j in range(grp) if gi * grp + j < n_oc]
                banks = [rot.acquire(te) for _ in ocs]
                for k in range(nK):
                    i = gi * nK + k
                    ev, s = ctx.ev[(name, "slab", i)]
                    if i + 2 < n_slab:
                        ev = ctx.ev[(name, "slab", i + 2)][0]
                    else:
                        ev = ctx.ev[(name, "slab_tail", i % 2)]
                    c.wait("TE", te, ev)
                    last = (k == nK - 1) and bias_fn is None
                    for j, oc in enumerate(ocs):
                        ins = te.matmul(
                            pf[banks[j]][:, :],
                            ap("wstream")[:, s * 512 + j * 128: s * 512 + (j + 1) * 128],
                            rhs_fn(k), start=(k == 0), stop=last)
                    if k < nK - 1 or bias_fn is None:
                        c.inc(ins, "pe")
                if bias_fn is not None:
                    for j, oc in enumerate(ocs):
                        ins = te.matmul(pf[banks[j]][:, :], bias_fn(oc),
                                        ap("maskt")[0:1, 384:896],
                                        start=False, stop=True)
                    c.inc(ins, "pe")
                for j, oc in enumerate(ocs):
                    ctx.ev[(name, "mm", oc)] = c.now("pe")
                    ctx.ev[(name, "bank", oc)] = banks[j]
                    if release_fn is not None:
                        rot.release(banks[j], release_fn(oc))
            assert ctx.semv["pe"] == pe_base + n_slab

        block.sync(syb)
        block.gpsimd(gpb)
        block.tensor(teb)


    def phase1():
        for tb in range(2):
            xn_ev = emit_ln(tb, False, ap("u1")[:, 8192:16384])

            def rhs(k):
                return ap("u1")[:, 8192 + k * 512: 8192 + (k + 1) * 512]

            se_b = ctx.semv["se"]
            emit_stream_mm(f"qp{tb}", "d1", 16, 4,
                           lambda gi, k: ap("Wq")[k * 128:(k + 1) * 128,
                                                  gi * 512:(gi + 1) * 512],
                           512, rhs, te_pre=xn_ev,
                           bias_fn=lambda oc: ap("bqrt")[0:1, oc * 128:(oc + 1) * 128],
                           release_fn=lambda oc: ("se", se_b + oc + 1))
            items = [(h, ap("qbuf")[:, h * TC + tb * 512: h * TC + (tb + 1) * 512],
                      True) for h in range(16)]
            e1 = emit_rope_evac(f"qp{tb}", tb, items, ap("ropec"), ap("ropes"))
            se_b2 = ctx.semv["se"]
            emit_stream_mm(f"kp{tb}", "d1", 4, 4,
                           lambda gi, k: ap("Wk")[k * 128:(k + 1) * 128, 0:512],
                           512, rhs,
                           bias_fn=lambda oc: ap("bkrt")[0:1, oc * 128:(oc + 1) * 128],
                           release_fn=lambda oc: ("se", se_b2 + oc + 1))
            items = [(kv, ap("u1")[:, kv * TC + tb * 512: kv * TC + (tb + 1) * 512],
                      True) for kv in range(4)]
            e2 = emit_rope_evac(f"kp{tb}", tb, items, ap("ropec"), ap("ropes"))
            se_b3 = ctx.semv["se"]
            emit_stream_mm(f"vp{tb}", "d1", 4, 4,
                           lambda gi, k: ap("Wv")[k * 128:(k + 1) * 128, 0:512],
                           512, rhs,
                           bias_fn=lambda oc: ap("bvrt")[0:1, oc * 128:(oc + 1) * 128],
                           release_fn=lambda oc: ("se", se_b3 + oc + 1))
            items = [(kv, ap("u1")[:, 4096 + kv * TC + tb * 512:
                                   4096 + kv * TC + (tb + 1) * 512], False)
                     for kv in range(4)]
            e3 = emit_rope_evac(f"vp{tb}", tb, items, ap("ropec"), ap("ropes"))
            ctx.ev[f"qkv_ev{tb}"] = [e1, e2, e3]

        se_vt0 = ctx.semv["se"]
        d1_vt0 = ctx.semv["d1"]

        def te_vt(te):
            for e in ctx.ev["qkv_ev0"] + ctx.ev["qkv_ev1"]:
                c.wait("TE", te, e)
            for i in range(32):
                kv, ti = divmod(i, 8)
                if i >= 2:
                    c.wait("TE", te, ("se", se_vt0 + i - 1))
                ins = te.transpose(ptp[:, (i % 2) * 128:(i % 2) * 128 + 128],
                                   ap("u1")[:, 4096 + kv * TC + ti * 128:
                                            4096 + kv * TC + (ti + 1) * 128],
                                   ap("identt")[:, :])
                ctx.ev[("vtmm", i)] = c.inc(ins, "pe")

        def se_vt(se):
            assert ctx.semv["se"] == se_vt0
            for i in range(32):
                c.wait("SE", se, ctx.ev[("vtmm", i)])
                if i >= 2:
                    c.wait("SE", se, ("d1", d1_vt0 + 32 * (i - 1)))
                ins = se.copy(ap("u1")[:, 16384 + (i % 2) * 128: 16384 + (i % 2) * 128 + 128],
                              ptp[:, (i % 2) * 128:(i % 2) * 128 + 128])
                ctx.ev[("vtse", i)] = c.inc(ins, "se")

        def sy_vt(sy):
            assert ctx.semv["d1"] == d1_vt0
            for i in range(32):
                kv, ti = divmod(i, 8)
                c.wait("SY", sy, ctx.ev[("vtse", i)])
                for j in (2 * kv, 2 * kv + 1):
                    ins = sy.dma_start(
                        out=ap("v_in")[j * TC + ti * 128: j * TC + (ti + 1) * 128, :],
                        in_=ap("u1")[:, 16384 + (i % 2) * 128: 16384 + (i % 2) * 128 + 128])
                    c.inc(ins, "d1", 16)
            ctx.ev["v_in_done"] = c.now("d1")
            for e in ctx.ev["qkv_ev0"] + ctx.ev["qkv_ev1"]:
                c.wait("SY", sy, e)
            for j in range(R):
                for hl in range(2):
                    h = 2 * j + hl
                    c.inc(sy.dma_start(
                        out=ap("qk_in")[j * 384 + hl * 128: j * 384 + (hl + 1) * 128, :],
                        in_=ap("qbuf")[:, h * TC:(h + 1) * TC]), "d0", 16)
                kv = j // 2
                c.inc(sy.dma_start(
                    out=ap("qk_in")[j * 384 + 256: j * 384 + 384, :],
                    in_=ap("u1")[:, kv * TC:(kv + 1) * TC]), "d0", 16)
            ctx.ev["qk_in_done"] = c.now("d0")

        block.tensor(te_vt)
        block.scalar(se_vt)
        block.sync(sy_vt)

        def gp_cc(gp):
            c.wait("GP", gp, ctx.ev["qk_in_done"])
            with allow_ici_all_to_all():
                ctx.ev["cc_qk"] = c.inc(gp.collective_compute(
                    "AllToAll", ALU.bypass, replica_groups=[list(range(R))],
                    ins=[cap("qk_in")], outs=[cap("qk_out")]), "cc")
                c.wait("GP", gp, ctx.ev["v_in_done"])
                ctx.ev["cc_v"] = c.inc(gp.collective_compute(
                    "AllToAll", ALU.bypass, replica_groups=[list(range(R))],
                    ins=[cap("v_in")], outs=[cap("v_out")]), "cc")

        block.gpsimd(gp_cc)

    def fold_wd():
        pe_base = ctx.semv["pe"]
        se_base = ctx.semv["se"]
        d0_base = ctx.semv["d0"]

        ve_cast0 = [None]
        fsA = [None]

        def sy_lda(sy):
            wdr = ap("wdT_sh").rearrange("(a r) h -> a (r h)", a=64)
            for bg in range(64):
                if bg >= 2:
                    c.wait("SY", sy, ("ve", ve_cast0[0] + bg - 1))
                stage = ap("t2") if bg % 2 == 0 else ap("tg")
                c.inc(sy.dma_start(
                    out=stage[0:64, :],
                    in_=wdr[:, bg * 512:(bg + 1) * 512]), f"fa{bg % 4}", 16)

        def ve_lda(ve):
            assert ctx.semv["ve"] == ve_cast0[0]
            for bg in range(64):
                c.wait("VE", ve, (f"fa{bg % 4}", 16 * (bg // 4 + 1)))
                if bg >= 2:
                    c.wait("VE", ve, ("pe", pe_base + 2 * (bg - 2) + 2))
                stage = ap("t2") if bg % 2 == 0 else ap("tg")
                ins = ve.tensor_copy(
                    ap("fold_a")[:, (bg % 2) * 512:(bg % 2) * 512 + 512],
                    stage[0:64, :])
                ctx.ev[("fa_cast", bg)] = c.inc(ins, "ve")

        ve_cast0[0] = ctx.semv["ve"]
        block.sync(sy_lda)
        block.vector(ve_lda)

        def teA(te):
            assert ctx.semv["pe"] == pe_base
            for b_ in range(2):
                c.wait("TE", te, ctx.ev_bank.get(b_))
            for i in range(128):
                bg = i // 2
                c.wait("TE", te, ctx.ev[("fa_cast", bg)])
                if i >= 2:
                    c.wait("TE", te, ("se", se_base + i - 1))
                ins = te.matmul(pf[i % 2][0:64, 0:256], ap("h64t")[:, :],
                                ap("fold_a")[:, (bg % 2) * 512 + (i % 2) * 256:
                                             (bg % 2) * 512 + (i % 2) * 256 + 256],
                                start=True, stop=True)
                c.inc(ins, "pe")
            assert ctx.semv["pe"] == pe_base + 128

        def seA(se):
            assert ctx.semv["se"] == se_base
            for i in range(128):
                c.wait("SE", se, ("pe", pe_base + i + 1))
                if i >= 2:
                    c.wait("SE", se, (f"sa{(i - 2) % 4}", ((i - 2) // 4 + 1) * 16))
                ins = se.copy(ap("ptile")[0:64, 1024 + (i % 2) * 256: 1024 + (i % 2) * 256 + 256],
                              pf[i % 2][0:64, 0:256])
                c.inc(ins, "se")
            assert ctx.semv["se"] == se_base + 128

        def stA(gp):
            wdiv = ap("wdi").rearrange("(a b) h -> b a h", a=64, b=128)
            for i in range(128):
                c.wait("GP", gp, ("se", se_base + i + 1))
                c.inc(gp.dma_start(out=wdiv[i],
                                   in_=ap("ptile")[0:64, 1024 + (i % 2) * 256: 1024 + (i % 2) * 256 + 256]),
                      f"sa{i % 4}", 16)
            ctx.ev["wdi_done"] = [(f"sa{j}", ctx.semv[f"sa{j}"]) for j in range(4)]

        block.tensor(teA)
        block.scalar(seA)
        block.gpsimd(stA)

        pe_b = ctx.semv["pe"]
        ve_b = ctx.semv["ve"]
        d0_b = ctx.semv["d0"]

        def syB(sy):
            assert ctx.semv["d0"] == d0_b
            for e in ctx.ev["wdi_done"]:
                c.wait("SY", sy, e)
            c.wait("SY", sy, ctx.ev["qk_in_done"])
            for a in range(64):
                if a >= 2:
                    c.wait("SY", sy, ("pe", pe_b + a - 1))
                c.inc(sy.dma_start(
                    out=ap("qbuf")[:, (a % 2) * 256:(a % 2) * 256 + 256],
                    in_=ap("wdi")[a * 128:(a + 1) * 128, :]), "d0", 16)
            ctx.ev["foldb_ld"] = c.now("d0")

        def teB(te):
            assert ctx.semv["pe"] == pe_b
            for b_ in (2, 3):
                c.wait("TE", te, ctx.ev_bank.get(b_))
            for a in range(64):
                c.wait("TE", te, ("d0", d0_b + 16 * (a + 1)))
                if a >= 2:
                    c.wait("TE", te, ("ve", ve_b + a - 1))
                ins = te.matmul(pf[2 + a % 2][:, 0:256], ap("h128t")[:, :],
                                ap("qbuf")[:, (a % 2) * 256:(a % 2) * 256 + 256],
                                start=True, stop=True)
                c.inc(ins, "pe")
            assert ctx.semv["pe"] == pe_b + 64

        g0B = [None]

        def veB(ve):
            assert ctx.semv["ve"] == ve_b
            c.wait("VE", ve, ctx.ev["tabs_ready"])
            for a in range(64):
                c.wait("VE", ve, ("pe", pe_b + a + 1))
                if a >= 2:
                    c.wait("VE", ve, ("fs", g0B[0] + (a - 1) * 16))
                ins = ve.tensor_scalar(
                    ap("ptile")[:, 1024 + (a % 2) * 256: 1024 + (a % 2) * 256 + 256],
                    pf[2 + a % 2][:, 0:256],
                    ap("suwut")[:, a:a + 1], None, op0=ALU.mult)
                c.inc(ins, "ve")
            assert ctx.semv["ve"] == ve_b + 64

        def stB(gp):
            assert ctx.semv["fs"] == g0B[0]
            for a in range(64):
                c.wait("GP", gp, ("ve", ve_b + a + 1))
                c.inc(gp.dma_start(
                    out=ap("wd_in")[a * 128:(a + 1) * 128, :],
                    in_=ap("ptile")[:, 1024 + (a % 2) * 256: 1024 + (a % 2) * 256 + 256]),
                    "fs", 16)
            ctx.ev["wd_in_done"] = c.now("fs")

        block.sync(syB)
        block.tensor(teB)
        ctx.ev["foldB_pe_end"] = c.now("pe")
        g0B[0] = ctx.semv["fs"]
        block.vector(veB)
        block.gpsimd(stB)
        ctx.seal_banks(c.now("ve"))

        def gp_cc(gp):
            c.wait("GP", gp, ctx.ev["ugwb_in"])
            with allow_ici_all_to_all():
                ctx.ev["cc_ug"] = c.inc(gp.collective_compute(
                    "AllGather", ALU.bypass, replica_groups=[list(range(R))],
                    ins=[cap("ug_in")], outs=[cap("ug_out")]), "cc")
                ctx.ev["cc_wb"] = c.inc(gp.collective_compute(
                    "AllGather", ALU.bypass, replica_groups=[list(range(R))],
                    ins=[cap("wb_in")], outs=[cap("wb_out")]), "cc")
                c.wait("GP", gp, ctx.ev["wd_in_done"])
                ctx.ev["cc_wd"] = c.inc(gp.collective_compute(
                    "AllGather", ALU.bypass, replica_groups=[list(range(R))],
                    ins=[cap("wd_in")], outs=[cap("wd_out")]), "cc")

        block.gpsimd(gp_cc)

    def phase2():
        def sy_ld(sy):
            c.wait("SY", sy, ctx.ev["cc_qk"])
            c.wait("SY", sy, ctx.ev["foldB_pe_end"])
            for hl in range(2):
                for r in range(R):
                    c.inc(sy.dma_start(
                        out=ap("qbuf")[:, hl * T + r * TC: hl * T + (r + 1) * TC],
                        in_=ap("qk_out")[r * 384 + hl * 128:
                                         r * 384 + (hl + 1) * 128, :]), "d1", 16)
            for r in range(R):
                c.inc(sy.dma_start(
                    out=ap("u1")[:, r * TC:(r + 1) * TC],
                    in_=ap("qk_out")[r * 384 + 256: r * 384 + 384, :]), "d1", 16)
            c.wait("SY", sy, ctx.ev["cc_v"])
            for jt in range(64):
                c.inc(sy.dma_start(
                    out=ap("u1")[:, 8192 + jt * 128: 8192 + (jt + 1) * 128],
                    in_=ap("v_out")[jt * 128:(jt + 1) * 128, :]), "d1", 16)
            ctx.ev["att_ld"] = c.now("d1")

        block.sync(sy_ld)

        pe0 = ctx.semv["pe"]
        se0 = ctx.semv["se"]
        ve0 = ctx.semv["ve"]
        s_val, pv_val, exp_val, p_val = {}, {}, {}, {}
        lo_val, bc_val, rl_val, oev_val = {}, {}, {}, {}
        pe_n, se_n, ve_n = pe0, se0, ve0
        si_of = {}
        si = 0
        for hl in range(2):
            for ib in range(16):
                ext = 4 * ib + 4
                pend = []
                for jc in range(ext):
                    si_of[(hl, ib, jc)] = si
                    pe_n += 1
                    s_val[si] = pe_n
                    pend.append(si)
                    si += 1
                    if len(pend) >= 3:
                        s0 = pend.pop(0)
                        pe_n += 1
                        pv_val[s0] = pe_n
                while pend:
                    s0 = pend.pop(0)
                    pe_n += 1
                    pv_val[s0] = pe_n
                lo_val[(hl, ib)] = pe_n
                pe_n += 1
                bc_val[(hl, ib)] = pe_n
        for hl in range(2):
            for ib in range(16):
                for jc in range(4 * ib + 4):
                    s0 = si_of[(hl, ib, jc)]
                    se_n += 1
                    exp_val[s0] = se_n
        for hl in range(2):
            for ib in range(16):
                for jc in range(4 * ib + 4):
                    s0 = si_of[(hl, ib, jc)]
                    if jc >= 4 * ib:
                        ve_n += 1
                        p_val[s0] = ("ve", ve_n)
                    else:
                        p_val[s0] = ("se", exp_val[s0])
                ve_n += 1
                rl_val[(hl, ib)] = ve_n
                ve_n += 1
                oev_val[(hl, ib)] = ve_n
        n_si = si

        def se_att(se):
            assert ctx.semv["se"] == se0
            c.wait("SE", se, ctx.ev["wd_in_done"])
            for hl in range(2):
                for ib in range(16):
                    for jc in range(4 * ib + 4):
                        s0 = si_of[(hl, ib, jc)]
                        c.wait("SE", se, ("pe", s_val[s0]))
                        if s0 >= 3:
                            c.wait("SE", se, ("pe", pv_val[s0 - 3]))
                        ins = se.activation(
                            ap("ptile")[:, (s0 % 3) * 512:(s0 % 3 + 1) * 512],
                            pf[s0 % 3][:, :], AF.Exp, scale=SM_SCALE)
                        c.inc(ins, "se")
            assert ctx.semv["se"] == se_n

        def ve_att(ve):
            assert ctx.semv["ve"] == ve0
            for hl in range(2):
                for ib in range(16):
                    for jc in range(4 * ib, 4 * ib + 4):
                        s0 = si_of[(hl, ib, jc)]
                        tau = jc - 4 * ib
                        c.wait("VE", ve, ("se", exp_val[s0]))
                        ins = ve.tensor_tensor(
                            ap("ptile")[:, (s0 % 3) * 512:(s0 % 3 + 1) * 512],
                            ap("ptile")[:, (s0 % 3) * 512:(s0 % 3 + 1) * 512],
                            ap("maskt")[:, 384 - tau * 128: 896 - tau * 128],
                            op=ALU.mult)
                        c.inc(ins, "ve")
                    c.wait("VE", ve, ("pe", lo_val[(hl, ib)]))
                    ve.reciprocal(ap("rlrow")[:, :], pf[4][0:1, :])
                    ins = ve.tensor_copy(ap("rlrow_bf")[:, :], ap("rlrow")[:, :])
                    c.inc(ins, "ve")
                    c.wait("VE", ve, ("pe", bc_val[(hl, ib)]))
                    ve.tensor_copy(ap("tg")[:, :], pf[5][:, :])
                    ins = ve.tensor_tensor(
                        ap("u1")[:, 16384 + hl * T + ib * 512:
                                 16384 + hl * T + (ib + 1) * 512],
                        pf[3][:, :], ap("tg")[:, :], op=ALU.mult)
                    c.inc(ins, "ve")
            assert ctx.semv["ve"] == ve_n
            ctx.ev["att_done_ve"] = c.now("ve")

        def te_att(te):
            assert ctx.semv["pe"] == pe0
            c.wait("TE", te, ctx.ev["att_ld"])
            for i in range(7):
                c.wait("TE", te, ctx.ev_bank.get(i))
            for hl in range(2):
                for ib in range(16):
                    ext = 4 * ib + 4
                    if ib > 0 or hl > 0:
                        prev = (hl, ib - 1) if ib > 0 else (hl - 1, 15)
                        c.wait("TE", te, ("ve", oev_val[prev]))
                    pend = []
                    for jc in range(ext):
                        s0 = si_of[(hl, ib, jc)]
                        if s0 >= 3:
                            c.wait("TE", te, p_val[s0 - 3])
                        ins = te.matmul(
                            pf[s0 % 3][:, :],
                            ap("u1")[:, jc * 128:(jc + 1) * 128],
                            ap("qbuf")[:, hl * T + ib * 512: hl * T + (ib + 1) * 512],
                            start=True, stop=True)
                        c.inc(ins, "pe")
                        assert ctx.semv["pe"] == s_val[s0]
                        pend.append((jc, s0))
                        if len(pend) >= 3:
                            _lpv(te, hl, ib, pend.pop(0))
                    while pend:
                        _lpv(te, hl, ib, pend.pop(0))
                    c.wait("TE", te, ("ve", rl_val[(hl, ib)]))
                    ins = te.matmul(pf[5][:, :], ap("onest")[0:1, :],
                                    ap("rlrow_bf")[0:1, :], start=True, stop=True)
                    c.inc(ins, "pe")
                    assert ctx.semv["pe"] == bc_val[(hl, ib)]
            ctx.seal_banks(ctx.ev["att_done_ve"])

        def _lpv(te, hl, ib, item):
            jc, s0 = item
            first = (jc == 0)
            last = (jc == 4 * ib + 3)
            c.wait("TE", te, p_val[s0])
            te.matmul(pf[4][0:1, :], ap("onest")[:, 0:1],
                      ap("ptile")[:, (s0 % 3) * 512:(s0 % 3 + 1) * 512],
                      start=first, stop=last, skip_group_check=True)
            ins = te.matmul(pf[3][:, :],
                            ap("u1")[:, 8192 + jc * 128: 8192 + (jc + 1) * 128],
                            ap("ptile")[:, (s0 % 3) * 512:(s0 % 3 + 1) * 512],
                            start=first, stop=last, skip_group_check=True)
            c.inc(ins, "pe")
            assert ctx.semv["pe"] == pv_val[s0]

        block.scalar(se_att)
        block.vector(ve_att)
        block.tensor(te_att)

        def sy_oc(sy):
            c.wait("SY", sy, ctx.ev["att_done_ve"])
            for j in range(R):
                for hl in range(2):
                    c.inc(sy.dma_start(
                        out=ap("o_in")[j * 256 + hl * 128: j * 256 + (hl + 1) * 128, :],
                        in_=ap("u1")[:, 16384 + hl * T + j * TC:
                                     16384 + hl * T + (j + 1) * TC]), "d1", 16)
            ctx.ev["o_in_done"] = c.now("d1")

        block.sync(sy_oc)

        def gp_cc(gp):
            c.wait("GP", gp, ctx.ev["o_in_done"])
            with allow_ici_all_to_all():
                ctx.ev["cc_o"] = c.inc(gp.collective_compute(
                    "AllToAll", ALU.bypass, replica_groups=[list(range(R))],
                    ins=[cap("o_in")], outs=[cap("o_out")]), "cc")

        block.gpsimd(gp_cc)

        def sy_ld2(sy):
            c.wait("SY", sy, ctx.ev["cc_o"])
            for h in range(16):
                c.inc(sy.dma_start(
                    out=ap("qbuf")[:, h * TC:(h + 1) * TC],
                    in_=ap("o_out")[(h // 2) * 256 + (h % 2) * 128:
                                    (h // 2) * 256 + (h % 2) * 128 + 128, :]),
                    "d1", 16)
            ctx.ev["ao_ld"] = c.now("d1")

        block.sync(sy_ld2)

        for tb in range(2):
            def rhs(k, _tb=tb):
                return ap("qbuf")[:, k * TC + _tb * 512: k * TC + (_tb + 1) * 512]

            ve_bw = ctx.semv["ve"]
            emit_stream_mm(f"wo{tb}", "d1", 16, 4,
                           lambda gi, k: ap("Wo")[k * 128:(k + 1) * 128,
                                                  gi * 512:(gi + 1) * 512],
                           512, rhs, sy_pre=[ctx.ev["ao_ld"]],
                           te_pre=ctx.ev["ao_ld"],
                           release_fn=lambda oc, b=ve_bw: ("ve", b + oc + 1))

            def ve_ev(ve, _tb=tb, _vb=ve_bw):
                assert ctx.semv["ve"] == _vb
                ev = None
                for oc in range(16):
                    bk = ctx.ev[(f"wo{_tb}", "bank", oc)]
                    c.wait("VE", ve, ctx.ev[(f"wo{_tb}", "mm", oc)])
                    xs = ap("x32")[:, oc * TC + _tb * 512: oc * TC + (_tb + 1) * 512]
                    ins = ve.tensor_tensor(xs, xs, pf[bk][:, :], op=ALU.add)
                    ev = c.inc(ins, "ve")
                ctx.ev[f"res1_{_tb}"] = ev

            block.vector(ve_ev)

    def phase_ffn(call, dsem, pre_ev):
        for tb in range(2):
            tag = f"f{call}_{tb}"
            xn_ev = emit_ln(tb, True, ap("qbuf")[:, 8192:16384],
                            pre_ev=(pre_ev if tb == 0 else
                                    ctx.ev.get("dbg_ffn")))

            def rhs_xn(k):
                return ap("qbuf")[:, 8192 + k * 512: 8192 + (k + 1) * 512]

            se_bx = ctx.semv["se"]
            emit_stream_mm(f"xh{tag}", dsem, 16, 4,
                           lambda gi, k: ap("Hpm")[k * 128:(k + 1) * 128,
                                                   gi * 512:(gi + 1) * 512],
                           512, rhs_xn, te_pre=xn_ev,
                           release_fn=lambda oc: ("se", se_bx + oc + 1))

            def se_xh(se):
                assert ctx.semv["se"] == se_bx
                ev = None
                for oc in range(16):
                    bk = ctx.ev[(f"xh{tag}", "bank", oc)]
                    c.wait("SE", se, ctx.ev[(f"xh{tag}", "mm", oc)])
                    ins = se.copy(ap("qbuf")[:, oc * 512:(oc + 1) * 512], pf[bk][:, :])
                    ev = c.inc(ins, "se")
                ctx.ev[f"xh_ev{tag}"] = ev

            block.scalar(se_xh)

            def rhs_xh(k):
                return ap("qbuf")[:, k * 512:(k + 1) * 512]

            def ug_slab(gi, k):
                ci = 2 * gi
                r_ = ci // 8
                lc = ci % 8
                return ap("ug_out")[r_ * DM + k * 128: r_ * DM + (k + 1) * 128,
                                    lc * 256: lc * 256 + 512]

            se_bu = ctx.semv["se"]
            ve_bu = ctx.semv["ve"]

            def ug_rel(oc):
                ioc = oc // 2
                if oc % 2:
                    return ("se", se_bu + 2 * ioc + 2)
                return ("ve", ve_bu + ioc + 1)

            emit_stream_mm(f"ug{tag}", dsem, 128, 4, ug_slab, 512, rhs_xh,
                           sy_pre=[ctx.ev["cc_ug"]],
                           te_pre=ctx.ev[f"xh_ev{tag}"], release_fn=ug_rel)

            def se_ug(se):
                assert ctx.semv["se"] == se_bu
                for ioc in range(64):
                    bg = ctx.ev[(f"ug{tag}", "bank", 2 * ioc + 1)]
                    c.wait("SE", se, ctx.ev[(f"ug{tag}", "mm", 2 * ioc + 1)])
                    if ioc >= 2:
                        c.wait("SE", se, ("ve", ve_bu + ioc - 1))
                    tgb = ap("tg") if ioc % 2 == 0 else ap("t2")
                    ins = se.activation(tgb[:, 0:256], pf[bg][:, 0:256],
                                        AF.Silu,
                                        scale=ap("wsgt")[:, ioc:ioc + 1])
                    c.inc(ins, "se")
                    ins = se.activation(tgb[:, 256:512], pf[bg][:, 256:512],
                                        AF.Silu,
                                        scale=ap("wsgt")[:, ioc:ioc + 1])
                    ctx.ev[(tag, "tg", ioc)] = c.inc(ins, "se")

            def ve_ug(ve):
                assert ctx.semv["ve"] == ve_bu
                for ioc in range(64):
                    bu = ctx.ev[(f"ug{tag}", "bank", 2 * ioc)]
                    c.wait("VE", ve, ctx.ev[(tag, "tg", ioc)])
                    tgb = ap("tg") if ioc % 2 == 0 else ap("t2")
                    ins = ve.tensor_tensor(
                        ap("u1")[:, ioc * 512:(ioc + 1) * 512],
                        pf[bu][:, :], tgb[:, :], op=ALU.mult)
                    ev = c.inc(ins, "ve")
                    ctx.ev[(tag, "z1", ioc)] = ev
                    if ioc == 63:
                        ins = ve.tensor_copy(ap("t1")[:, 0:8], ap("t1")[:, 0:8])
                        c.inc(ins, "ve")

            block.scalar(se_ug)
            block.vector(ve_ug)

            if debug_wts and call == 1 and tb == 0:
                def sy_dbgf(sy):
                    c.wait("SY", sy, ctx.ev[(tag, "z1", 63)])
                    c.inc(sy.dma_start(out=ap("dbg_xh")[:, :],
                                       in_=ap("qbuf")[:, :]), "d5", 16)
                    c.inc(sy.dma_start(out=ap("dbg_z1")[:, :],
                                       in_=ap("u1")[:, :]), "d5", 16)
                    ctx.ev["dbg_ffn"] = c.now("d5")
                block.sync(sy_dbgf)

            pe_base = ctx.semv["pe"]

            dnsem = ctx.next_dsem()

            dnsem2 = ctx.next_dsem()

            def dn_loader(eng, engname, parity, qsem):
                c.wait(engname, eng, ctx.ev["cc_wd"])
                last = None
                for i in range(512):
                    if i % 2 != parity:
                        continue
                    hcp, ioc = divmod(i, 64)
                    s = i % 4
                    c.wait(engname, eng, ctx.slab_ev[s])
                    ins = eng.dma_start(
                        out=ap("wstream")[:, s * 512: s * 512 + 256],
                        in_=ap("wd_out")[hcp * INTER + ioc * 128:
                                         hcp * INTER + (ioc + 1) * 128, :])
                    ctx.ev[(tag, "dnslab", i)] = (c.inc(ins, qsem, 16), s)
                    ctx.slab_ev[s] = ("pe", pe_base + i + 1)
                    last = (hcp, ioc, s)
                hcp, ioc, s = last
                ins = eng.dma_start(
                    out=ap("wstream")[:, s * 512: s * 512 + 256],
                    in_=ap("wd_out")[hcp * INTER + ioc * 128:
                                     hcp * INTER + (ioc + 1) * 128, :])
                ctx.ev[(tag, "dntail", parity)] = c.inc(ins, qsem, 16)

            def sy_dn(sy):
                dn_loader(sy, "SY", 0, dnsem)

            def gp_dn(gp):
                dn_loader(gp, "GP", 1, dnsem2)

            ve_bd = [None]

            def te_dn(te):
                assert ctx.semv["pe"] == pe_base
                ve_b = ve_bd[0]
                i = 0
                for hcp in range(8):
                    b0 = rot.acquire(te)
                    b1 = rot.acquire(te)
                    rot.release(b0, ("ve", ve_b + 2 * hcp + 1))
                    rot.release(b1, ("ve", ve_b + 2 * hcp + 2))
                    for ioc in range(64):
                        ev, s = ctx.ev[(tag, "dnslab", i)]
                        ev = (ctx.ev[(tag, "dnslab", i + 2)][0] if i + 2 < 512
                              else ctx.ev[(tag, "dntail", i % 2)])
                        c.wait("TE", te, ev)
                        zev = ctx.ev[(tag, "z1", ioc)]
                        c.wait("TE", te, (zev[0], zev[1] + 1))
                        te.matmul(pf[b0][:, :],
                                  ap("wstream")[:, s * 512: s * 512 + 128],
                                  ap("u1")[:, ioc * 512:(ioc + 1) * 512],
                                  start=(ioc == 0), stop=(ioc == 63))
                        ins = te.matmul(pf[b1][:, :],
                                        ap("wstream")[:, s * 512 + 128: s * 512 + 256],
                                        ap("u1")[:, ioc * 512:(ioc + 1) * 512],
                                        start=(ioc == 0), stop=(ioc == 63))
                        c.inc(ins, "pe")
                        i += 1
                    ctx.ev[(tag, "dnmm", hcp)] = c.now("pe")
                    ctx.ev[(tag, "dnbank", hcp)] = (b0, b1)

            def ve_dn(ve):
                assert ctx.semv["ve"] == ve_bd[0]
                ev = None
                for hcp in range(8):
                    b0, b1 = ctx.ev[(tag, "dnbank", hcp)]
                    c.wait("VE", ve, ctx.ev[(tag, "dnmm", hcp)])
                    for s_, bk in ((0, b0), (1, b1)):
                        hc = 2 * hcp + s_
                        xs = ap("x32")[:, hc * TC + tb * 512: hc * TC + (tb + 1) * 512]
                        ins = ve.scalar_tensor_tensor(
                            xs, pf[bk][:, :], ap("wsdpt")[:, hc:hc + 1], xs,
                            op0=ALU.mult, op1=ALU.add)
                        ev = c.inc(ins, "ve")
                ctx.ev[f"ffn_done{tag}"] = ev

            block.sync(sy_dn)
            block.gpsimd(gp_dn)
            ve_bd[0] = ctx.semv["ve"]
            block.tensor(te_dn)
            block.vector(ve_dn)
        return ctx.ev[f"ffn_done{tag}"]

    def phase4(pre_ev, sub=99):
        dsem = "d3"

        def sy_rt(sy):
            c.inc(sy.dma_start(out=ap("ropec")[:, :], in_=ap("rb_cos")[:, :]),
                  "d1", 16)
            c.inc(sy.dma_start(out=ap("ropes")[:, :], in_=ap("rb_sin")[:, :]),
                  "d1", 16)
            ctx.ev["rope_rl"] = c.now("d1")

        block.sync(sy_rt)
        for tb in range(2):
            xn_ev = emit_ln(tb, False, ap("u1")[:, 8192:16384],
                            pre_ev=(pre_ev if tb == 0 else None))

            def rhs(k):
                return ap("u1")[:, 8192 + k * 512: 8192 + (k + 1) * 512]

            def q_slab(gi, k):
                return ap("wb_out")[gi * DM + k * 128: gi * DM + (k + 1) * 128, 0:256]

            se_b4 = ctx.semv["se"]
            emit_stream_mm(f"bq{tb}", dsem, 16, 2, q_slab, 256, rhs,
                           sy_pre=[ctx.ev["cc_wb"]], te_pre=xn_ev,
                           release_fn=lambda oc: ("se", se_b4 + oc + 1))
            items = [(h, ap("qbuf")[:, h * TC + tb * 512: h * TC + (tb + 1) * 512],
                      True) for h in range(16)]
            e1 = emit_rope_evac(f"bq{tb}", tb, items, ap("ropec"), ap("ropes"),
                                ve_pre=ctx.ev["rope_rl"])

            def kv_slab(colbase):
                def f(gi, k):
                    return (ap("wb_out")[(2 * gi) * DM + k * 128:
                                         (2 * gi) * DM + (k + 1) * 128,
                                         colbase: colbase + 64],
                            ap("wb_out")[(2 * gi + 1) * DM + k * 128:
                                         (2 * gi + 1) * DM + (k + 1) * 128,
                                         colbase: colbase + 64])
                return f

            se_b5 = ctx.semv["se"]
            emit_stream_mm(f"bk{tb}", dsem, 4, 1, kv_slab(256), 128, rhs,
                           dual_dma=True,
                           release_fn=lambda oc: ("se", se_b5 + oc + 1))
            items = [(kv, ap("u1")[:, kv * TC + tb * 512: kv * TC + (tb + 1) * 512],
                      True) for kv in range(4)]
            e2 = emit_rope_evac(f"bk{tb}", tb, items, ap("ropec"), ap("ropes"),
                                ve_pre=ctx.ev["rope_rl"])
            se_b6 = ctx.semv["se"]
            emit_stream_mm(f"bv{tb}", dsem, 4, 1, kv_slab(320), 128, rhs,
                           dual_dma=True,
                           release_fn=lambda oc: ("se", se_b6 + oc + 1))
            items = [(kv, ap("u1")[:, 4096 + kv * TC + tb * 512:
                                   4096 + kv * TC + (tb + 1) * 512], False)
                     for kv in range(4)]
            e3 = emit_rope_evac(f"bv{tb}", tb, items, ap("ropec"), ap("ropes"))
            ctx.ev[f"bqkv_ev{tb}"] = [e1, e2, e3]

        if sub < 2:
            return None

        ve_t0 = [None]

        def te_t(te):
            for e in ctx.ev["bqkv_ev0"] + ctx.ev["bqkv_ev1"]:
                c.wait("TE", te, e)
            i = 0
            for src_off in (0, 4096):
                for kv in range(4):
                    for ch2 in range(8):
                        if i >= 2:
                            c.wait("TE", te, ("ve", ve_t0[0] + i - 1))
                        ins = te.transpose(
                            ptp[:, (i % 2) * 128:(i % 2) * 128 + 128],
                            ap("u1")[:, src_off + kv * TC + ch2 * 128:
                                     src_off + kv * TC + (ch2 + 1) * 128],
                            ap("identt")[:, :])
                        ctx.ev[("bt_mm", i)] = c.inc(ins, "pe")
                        i += 1

        def ve_t(ve):
            assert ctx.semv["ve"] == ve_t0[0]
            i = 0
            for dst_off in (16384, 24576):
                for kv in range(4):
                    for ch2 in range(8):
                        c.wait("VE", ve, ctx.ev[("bt_mm", i)])
                        ins = ve.tensor_copy(
                            ap("u1")[:,
                                     dst_off + kv * 1024 + ch2 * 128:
                                     dst_off + kv * 1024 + (ch2 + 1) * 128],
                            ptp[:, (i % 2) * 128:(i % 2) * 128 + 128])
                        ctx.ev[("bt_ve", i)] = c.inc(ins, "ve")
                        i += 1
            ctx.ev["bt_all"] = c.now("ve")

        ve_t0[0] = ctx.semv["ve"]
        block.tensor(te_t)
        block.vector(ve_t)

        if sub < 3:
            return None

        def sy_ct(sy):
            c.wait("SY", sy, ctx.ev["bt_all"])
            for j in range(R):
                for hl in range(2):
                    h = 2 * j + hl
                    c.inc(sy.dma_start(
                        out=ap("qk2_in")[j * 384 + hl * 128: j * 384 + (hl + 1) * 128, :],
                        in_=ap("qbuf")[:, h * TC:(h + 1) * TC]), dsem, 16)
                kv = j // 2
                c.inc(sy.dma_start(
                    out=ap("qk2_in")[j * 384 + 256: j * 384 + 384, :],
                    in_=ap("u1")[:, kv * TC:(kv + 1) * TC]), dsem, 16)
                for ch2 in range(8):
                    c.inc(sy.dma_start(
                        out=ap("ktv_in")[j * TC + ch2 * 128: j * TC + (ch2 + 1) * 128, 0:128],
                        in_=ap("u1")[:, 16384 + kv * 1024 + ch2 * 128:
                                     16384 + kv * 1024 + (ch2 + 1) * 128]), dsem, 16)
                    c.inc(sy.dma_start(
                        out=ap("ktv_in")[j * TC + ch2 * 128: j * TC + (ch2 + 1) * 128, 128:256],
                        in_=ap("u1")[:, 24576 + kv * 1024 + ch2 * 128:
                                     24576 + kv * 1024 + (ch2 + 1) * 128]), dsem, 16)
            ctx.ev["qk2_in_done"] = c.now(dsem)

        block.sync(sy_ct)

        def gp_cc(gp):
            c.wait("GP", gp, ctx.ev["qk2_in_done"])
            with allow_ici_all_to_all():
                ctx.ev["cc_qk2"] = c.inc(gp.collective_compute(
                    "AllToAll", ALU.bypass, replica_groups=[list(range(R))],
                    ins=[cap("qk2_in")], outs=[cap("qk2_out")]), "cc")
                ctx.ev["cc_ktv"] = c.inc(gp.collective_compute(
                    "AllToAll", ALU.bypass, replica_groups=[list(range(R))],
                    ins=[cap("ktv_in")], outs=[cap("ktv_out")]), "cc")

        block.gpsimd(gp_cc)

        def sy_l2(sy):
            c.wait("SY", sy, ctx.ev["cc_qk2"])
            for hl in range(2):
                for r in range(R):
                    c.inc(sy.dma_start(
                        out=ap("qbuf")[:, hl * T + r * TC: hl * T + (r + 1) * TC],
                        in_=ap("qk2_out")[r * 384 + hl * 128:
                                          r * 384 + (hl + 1) * 128, :]), dsem, 16)
            for r in range(R):
                c.inc(sy.dma_start(
                    out=ap("u1")[:, r * TC:(r + 1) * TC],
                    in_=ap("qk2_out")[r * 384 + 256: r * 384 + 384, :]), dsem, 16)
            ctx.ev["b_ld"] = c.now(dsem)

        block.sync(sy_l2)

        if sub < 4:
            return None

        pe0 = ctx.semv["pe"]
        ve0 = ctx.semv["ve"]

        def ve_of_seg(sg):
            return ve0 + sg * 70

        def sy_seg(sy):
            c.wait("SY", sy, ctx.ev["cc_ktv"])
            for sg in range(8):
                if sg >= 2:
                    c.wait("SY", sy, ("pe", pe0 + 48 * (sg - 1)))
                    c.wait("SY", sy, ("ve", ve_of_seg(sg - 2) + 36))
                base = 8192 + (sg % 2) * 4096
                for ci2 in range(8):
                    c.inc(sy.dma_start(
                        out=ap("u1")[0:64, base + (2 * ci2) * 128:
                                     base + (2 * ci2 + 2) * 128]
                        .rearrange("p (a b) -> p a b", a=2),
                        in_=ap("ktv_out")[sg * TC + ci2 * 128:
                                          sg * TC + (ci2 + 1) * 128, 0:128]
                        .rearrange("(a j) d -> j a d", a=2)), f"sg{sg % 2}", 16)
                    c.inc(sy.dma_start(
                        out=ap("u1")[0:64, base + 2048 + (2 * ci2) * 128:
                                     base + 2048 + (2 * ci2 + 2) * 128]
                        .rearrange("p (a b) -> p a b", a=2),
                        in_=ap("ktv_out")[sg * TC + ci2 * 128:
                                          sg * TC + (ci2 + 1) * 128, 128:256]
                        .rearrange("(a j) d -> j a d", a=2)), f"sg{sg % 2}", 16)
                ctx.ev[("seg_ld", sg)] = c.now(f"sg{sg % 2}")

        block.sync(sy_seg)

        def ve_scan(ve):
            assert ctx.semv["ve"] == ve0
            c.wait("VE", ve, ctx.ev["pads_zero"])
            c.wait("VE", ve, ctx.ev["b_ld"])
            for sg in range(8):
                base = 8192 + (sg % 2) * 4096
                for half in range(2):
                    gh = sg * 2 + half
                    c.wait("VE", ve, ctx.ev[("seg_ld", sg)])
                    if gh >= 1:
                        c.wait("VE", ve, ("pe", pe0 + 3 * 8 * gh))
                    ins = None
                    for hl in range(2):
                        for cl in range(8):
                            ci = half * 8 + cl
                            ins = ve.tensor_scalar(
                                ap("ktmd")[0:64, (hl * 8 + cl) * 128:
                                           (hl * 8 + cl + 1) * 128],
                                ap("u1")[0:64, base + ci * 128: base + (ci + 1) * 128],
                                ap("kdect")[0:64, hl:hl + 1], None, op0=ALU.mult)
                    c.inc(ins, "ve")
                    for cl in range(8):
                        ci = half * 8 + cl
                        gc = sg * 16 + ci
                        for hl in range(2):
                            c.wait("VE", ve, ("pe", pe0 + 3 * gc + 2))
                            ve.tensor_tensor(
                                ap("smt")[0:64, hl * 64:(hl + 1) * 64],
                                pf[2 + gc % 2][0:64, hl * 64:(hl + 1) * 64],
                                ap("diagt")[:, hl * 64:(hl + 1) * 64], op=ALU.mult)
                            ins = ve.tensor_tensor(
                                ap("tmq")[:, hl * 64:(hl + 1) * 64],
                                ap("qbuf")[:, hl * T + gc * 64: hl * T + (gc + 1) * 64],
                                ap("qdect")[:, hl * 64:(hl + 1) * 64], op=ALU.mult)
                            c.inc(ins, "ve")
                        for hl in range(2):
                            c.wait("VE", ve, ("pe", pe0 + 3 * gc + 1))
                            kvf = ap("kvf0") if hl == 0 else ap("kvf1")
                            kvb = ap("kvb0") if hl == 0 else ap("kvb1")
                            ve.scalar_tensor_tensor(
                                kvf[:, :], kvf[:, :], ap("blkvt")[:, hl:hl + 1],
                                pf[gc % 2][:, hl * 128:(hl + 1) * 128],
                                op0=ALU.mult, op1=ALU.add)
                            ins = ve.tensor_copy(
                                kvb[:, ((gc + 1) % 2) * 128:((gc + 1) % 2) * 128 + 128],
                                kvf[:, :])
                            c.inc(ins, "ve")
                        if gc % 8 == 7:
                            for hl in range(2):
                                c.wait("VE", ve, ("pe", pe0 + 3 * gc + 3))
                                ins = ve.tensor_copy(
                                    ap("u1")[:, 16384 + hl * T + (gc - 7) * 64:
                                             16384 + hl * T + (gc + 1) * 64],
                                    pf[4 + hl][:, :])
                                c.inc(ins, "ve")
                assert ctx.semv["ve"] == ve_of_seg(sg + 1)
            ctx.ev["scan_done"] = c.now("ve")

        def te_scan(te):
            assert ctx.semv["pe"] == pe0
            c.wait("TE", te, ctx.ev["b_ld"])
            for i in range(7):
                c.wait("TE", te, ctx.ev_bank.get(i))
            for sg in range(8):
                base = 8192 + (sg % 2) * 4096
                for ci in range(16):
                    gc = sg * 16 + ci
                    half = ci // 8
                    gh = sg * 2 + half
                    cl = ci % 8
                    decv = ve_of_seg(sg) + half * 35 + 1
                    c.wait("TE", te, ctx.ev[("seg_ld", sg)])
                    c.wait("TE", te, ("ve", decv))
                    if gc >= 2:
                        c.wait("TE", te, ("ve", _kvb_val(sg, ci, ve0) - 0))
                    for hl in range(2):
                        ins = te.matmul(
                            pf[gc % 2][:, hl * 128:(hl + 1) * 128],
                            ap("ktmd")[:, (hl * 8 + cl) * 128:(hl * 8 + cl + 1) * 128],
                            ap("u1")[:, base + 2048 + ci * 128: base + (ci + 1) * 128 + 2048],
                            start=True, stop=True, skip_group_check=True)
                    c.inc(ins, "pe")
                    for hl in range(2):
                        ins = te.matmul(
                            pf[2 + gc % 2][0:64, hl * 64:(hl + 1) * 64],
                            ap("u1")[:, gc * 64:(gc + 1) * 64],
                            ap("qbuf")[:, hl * T + gc * 64: hl * T + (gc + 1) * 64],
                            start=True, stop=True, skip_group_check=True)
                    c.inc(ins, "pe")
                    slot = gc % 8
                    for hl in range(2):
                        c.wait("TE", te, ("ve", _smq_val(sg, ci, ve0)))
                        if gc > 0:
                            c.wait("TE", te, ("ve", _kvb_val2(sg, ci, ve0)))
                        if slot == 0 and gc >= 8:
                            c.wait("TE", te, ("ve", _oev_val(gc // 8 - 1, ve0)))
                        kvb = ap("kvb0") if hl == 0 else ap("kvb1")
                        te.matmul(pf[4 + hl][:, slot * 64:(slot + 1) * 64],
                                  ap("u1")[:, base + 2048 + ci * 128:
                                           base + 2048 + (ci + 1) * 128],
                                  ap("smt")[:, hl * 64:(hl + 1) * 64],
                                  start=True, stop=False, skip_group_check=True)
                        ins = te.matmul(pf[4 + hl][:, slot * 64:(slot + 1) * 64],
                                        kvb[:, (gc % 2) * 128:(gc % 2) * 128 + 128],
                                        ap("tmq")[:, hl * 64:(hl + 1) * 64],
                                        start=False, stop=True, skip_group_check=True)
                    c.inc(ins, "pe")
                    assert ctx.semv["pe"] == pe0 + 3 * gc + 3
            ctx.seal_banks(ctx.ev["scan_done"])

        def _smq_val(sg, ci, ve0_):
            half = ci // 8
            cl = ci % 8
            return ve_of_seg(sg) + half * 35 + 1 + cl * 4 + 2

        def _kvb_val(sg, ci, ve0_):
            gc = sg * 16 + ci - 2
            sg2, ci2 = divmod(gc, 16)
            half = ci2 // 8
            cl = ci2 % 8
            return ve_of_seg(sg2) + half * 35 + 1 + cl * 4 + 4

        def _kvb_val2(sg, ci, ve0_):
            gc = sg * 16 + ci - 1
            sg2, ci2 = divmod(gc, 16)
            half = ci2 // 8
            cl = ci2 % 8
            return ve_of_seg(sg2) + half * 35 + 1 + cl * 4 + 4

        def _oev_val(slotidx, ve0_):
            gc = slotidx * 8 + 7
            sg2, ci2 = divmod(gc, 16)
            half = ci2 // 8
            return ve_of_seg(sg2) + half * 35 + 1 + 8 * 4 + 2

        block.vector(ve_scan)
        block.tensor(te_scan)

        if sub < 5:
            return None

        def sy_o2(sy):
            c.wait("SY", sy, ctx.ev["scan_done"])
            for j in range(R):
                for hl in range(2):
                    c.inc(sy.dma_start(
                        out=ap("o2_in")[j * 256 + hl * 128: j * 256 + (hl + 1) * 128, :],
                        in_=ap("u1")[:, 16384 + hl * T + j * TC:
                                     16384 + hl * T + (j + 1) * TC]), dsem, 16)
            ctx.ev["o2_in_done"] = c.now(dsem)

        block.sync(sy_o2)

        def gp_cc2(gp):
            c.wait("GP", gp, ctx.ev["o2_in_done"])
            with allow_ici_all_to_all():
                ctx.ev["cc_o2"] = c.inc(gp.collective_compute(
                    "AllToAll", ALU.bypass, replica_groups=[list(range(R))],
                    ins=[cap("o2_in")], outs=[cap("o2_out")]), "cc")

        block.gpsimd(gp_cc2)

        def sy_l3(sy):
            c.wait("SY", sy, ctx.ev["cc_o2"])
            for h in range(16):
                c.inc(sy.dma_start(
                    out=ap("qbuf")[:, h * TC:(h + 1) * TC],
                    in_=ap("o2_out")[(h // 2) * 256 + (h % 2) * 128:
                                     (h // 2) * 256 + (h % 2) * 128 + 128, :]),
                    dsem, 16)
            ctx.ev["bo_ld"] = c.now(dsem)

        block.sync(sy_l3)

        for tb in range(2):
            def rhs(k, _tb=tb):
                return ap("qbuf")[:, k * TC + _tb * 512: k * TC + (_tb + 1) * 512]

            def wo_slab(gi, k):
                return ap("wb_out")[gi * DM + k * 128: gi * DM + (k + 1) * 128,
                                    384:640]

            ve_bw2 = ctx.semv["ve"]
            emit_stream_mm(f"bwo{tb}", dsem, 16, 2, wo_slab, 256, rhs,
                           sy_pre=[ctx.ev["bo_ld"]], te_pre=ctx.ev["bo_ld"],
                           release_fn=lambda oc, b=ve_bw2: ("ve", b + oc + 1))

            def ve_ev(ve, _tb=tb, _vb=ve_bw2):
                assert ctx.semv["ve"] == _vb
                ev = None
                for oc in range(16):
                    bk = ctx.ev[(f"bwo{_tb}", "bank", oc)]
                    c.wait("VE", ve, ctx.ev[(f"bwo{_tb}", "mm", oc)])
                    xs = ap("x32")[:, oc * TC + _tb * 512: oc * TC + (_tb + 1) * 512]
                    ins = ve.tensor_tensor(xs, xs, pf[bk][:, :], op=ALU.add)
                    ev = c.inc(ins, "ve")
                ctx.ev[f"res3_{_tb}"] = ev

            block.vector(ve_ev)

    def phase_store():
        def syb(sy):
            for s_ in list(ctx.semv):
                c.wait("SY", sy, (s_, ctx.semv[s_]))
            for ch in range(NCH):
                c.inc(sy.dma_start(
                    out=ap("out_x")[:, ch * TC:(ch + 1) * TC],
                    in_=ap("x32")[:, ch * TC:(ch + 1) * TC]), "d5", 16)
            c.wait("SY", sy, c.now("d5"))

        block.sync(syb)

    def phase_dbg():
        def gpd(gp):
            c.wait("GP", gp, ctx.ev["cc_ug"])
            c.wait("GP", gp, ctx.ev["cc_wd"])
            c.wait("GP", gp, ctx.ev["cc_wb"])
            c.inc(gp.dma_start(out=ap("dbg_ug")[:, :], in_=ap("ug_out")[:, :]), "g1", 16)
            c.inc(gp.dma_start(out=ap("dbg_wd")[:, :], in_=ap("wd_out")[:, :]), "g1", 16)
            c.inc(gp.dma_start(out=ap("dbg_wb")[:, :], in_=ap("wb_out")[:, :]), "g1", 16)
            c.inc(gp.dma_start(out=ap("dbg_wdi")[:, :], in_=ap("wdi")[:, :]), "g1", 16)
        block.gpsimd(gpd)

    phase1()
    fold_wd()
    phase2()
    if upto >= 2:
        phase_ffn(1, "d2", ctx.ev["o_in_done"])
    if 2 < upto < 3:
        phase4(None, sub=int(round((upto - 2) * 10)))
    if upto >= 3:
        phase4(None)
    if upto >= 4:
        phase_ffn(2, "d4", ctx.ev["o2_in_done"])
    if debug_wts:
        phase_dbg()
    phase_store()


def _hadamard(n):
    h = np.array([[1.0]], dtype=np.float64)
    while h.shape[0] < n:
        h = np.block([[h, h], [h, -h]])
    return h


def _rope_tables(pos, base):
    half = 64
    inv = base ** -(np.arange(half, dtype=np.float64) / half)
    ang = np.outer(inv, pos.astype(np.float64))
    cos = np.cos(ang)
    sin = np.sin(ang)
    return (np.concatenate([cos, cos], axis=0),
            np.concatenate([sin, sin], axis=0))


_NC_CACHE = {}


def _get_nc(upto, debug_wts=False):
    key = (upto, debug_wts)
    if key not in _NC_CACHE:
        _NC_CACHE[key] = build_program(upto, debug_wts)
    return _NC_CACHE[key]


def build_in_maps(L0_attn_a_x, L0_attn_a_Wq, L0_attn_a_bq, L0_attn_a_Wk,
                  L0_attn_a_bk, L0_attn_a_Wv, L0_attn_a_bv, L0_attn_a_Wo,
                  L0_attn_b_wq, L0_attn_b_wk, L0_attn_b_wv, L0_attn_b_wo,
                  L0_ffn_SU_ug, L0_ffn_SU_dp, L0_ffn_Wscale_ug,
                  L0_ffn_Wscale_dp, L0_ffn_W_up, L0_ffn_W_gate, L0_ffn_W_down):
    f32 = np.float32
    asn = np.ascontiguousarray
    x = np.asarray(L0_attn_a_x)
    Wq = np.asarray(L0_attn_a_Wq)
    Wk = np.asarray(L0_attn_a_Wk)
    Wv = np.asarray(L0_attn_a_Wv)
    Wo = np.asarray(L0_attn_a_Wo)
    bq = np.asarray(L0_attn_a_bq)
    bk = np.asarray(L0_attn_a_bk)
    bv = np.asarray(L0_attn_a_bv)
    wqT = asn(np.asarray(L0_attn_b_wq, f32).T)
    wkT = asn(np.asarray(L0_attn_b_wk, f32).T)
    wvT = asn(np.asarray(L0_attn_b_wv, f32).T)
    woT = asn(np.asarray(L0_attn_b_wo, f32).T)
    wupT = asn(np.asarray(L0_ffn_W_up, f32).T)
    wgT = asn(np.asarray(L0_ffn_W_gate, f32).T)
    wdT = asn(np.asarray(L0_ffn_W_down, f32).T)
    su_ug = np.asarray(L0_ffn_SU_ug, f32)
    su_dp = np.asarray(L0_ffn_SU_dp, f32)
    ws_ug = np.asarray(L0_ffn_Wscale_ug, f32)
    ws_dp = np.asarray(L0_ffn_Wscale_dp, f32)

    Hpm = _hadamard(2048).astype(bfdt)
    H64 = _hadamard(64).astype(bfdt)
    H128 = _hadamard(128).astype(bfdt)
    ident = np.eye(128, dtype=bfdt)
    ones = np.ones((128, 128), dtype=bfdt)
    Rm = np.zeros((128, 128), np.float64)
    for d in range(64):
        Rm[d, d + 64] = -1.0
        Rm[d + 64, d] = 1.0
    Rpm = np.ascontiguousarray(Rm.T).astype(bfdt)
    mask = (np.arange(128)[:, None] <= (np.arange(896)[None, :] - 384)).astype(bfdt)
    lam = np.exp2(-8.0 * np.arange(1, 17, dtype=np.float64) / 16.0)

    in_maps = []
    for cid in range(R):
        sl = slice(cid * TC, (cid + 1) * TC)
        pos = np.arange(cid * TC, (cid + 1) * TC)
        ca, sa = _rope_tables(pos, 1e6)
        cb, sb2 = _rope_tables(pos, 1e4)
        heads = [2 * cid, 2 * cid + 1]
        diagT = np.zeros((64, 128), dtype=f32)
        qdec = np.zeros((128, 128), dtype=f32)
        kdec = np.zeros((128, 2), dtype=f32)
        blkv = np.zeros((128, 2), dtype=f32)
        rr = np.arange(64, dtype=np.float64)
        for hi_, h in enumerate(heads):
            diff = rr[None, :] - rr[:, None]
            diagT[:, hi_ * 64:(hi_ + 1) * 64] = np.where(
                diff >= 0, np.exp(-lam[h] * diff), 0.0)
            qdec[:, hi_ * 64:(hi_ + 1) * 64] = np.exp(-lam[h] * (rr + 1.0))[None, :]
            kdec[0:64, hi_] = np.exp(-lam[h] * (63.0 - rr))
            kdec[64:128, hi_] = kdec[0:64, hi_]
            blkv[:, hi_] = math.exp(-lam[h] * 64.0)
        in_maps.append({
            "x_fm": asn(x[0, sl, :].T),
            "Wq": Wq, "Wk": Wk, "Wv": Wv, "Wo": Wo,
            "bqr": asn(bq.reshape(1, 2048)),
            "bkr": asn(bk.reshape(1, 512)),
            "bvr": asn(bv.reshape(1, 512)),
            "rpmc": Rpm,
            "wqT_sh": asn(wqT[:, cid * 256:(cid + 1) * 256]),
            "wkT_sh": asn(wkT[:, cid * 64:(cid + 1) * 64]),
            "wvT_sh": asn(wvT[:, cid * 64:(cid + 1) * 64]),
            "woT_sh": asn(woT[:, cid * 256:(cid + 1) * 256]),
            "wupT_sh": asn(wupT[:, cid * 1024:(cid + 1) * 1024]),
            "wgT_sh": asn(wgT[:, cid * 1024:(cid + 1) * 1024]),
            "wdT_sh": asn(wdT[:, cid * 256:(cid + 1) * 256]),
            "su_ug2": asn(su_ug.reshape(16, 128).T),
            "su_dp2": asn(su_dp.reshape(64, 128).T),
            "wsup2": asn(ws_ug[:INTER].reshape(64, 128).T),
            "wsg2": asn(ws_ug[INTER:].reshape(64, 128).T),
            "wsdp2": asn(ws_dp.reshape(16, 128).T),
            "Hpm": Hpm, "H64c": H64, "H128c": H128,
            "identc": ident, "onesc": ones, "maskcc": mask,
            "ra_cos": asn(ca.astype(bfdt)), "ra_sin": asn(sa.astype(bfdt)),
            "rb_cos": asn(cb.astype(bfdt)), "rb_sin": asn(sb2.astype(bfdt)),
            "diagTc": diagT, "qdecc": asn(qdec.astype(bfdt)),
            "kdecc": kdec, "blkvc": blkv,
        })
    return in_maps


def kernel(_upto=4, _debug_wts=False, **inputs):
    in_maps = build_in_maps(**inputs)
    nc = _get_nc(_upto, _debug_wts)
    res = run_bass_kernel_spmd(nc, in_maps, core_ids=list(range(R)))
    if _debug_wts:
        kernel._dbg = {k: np.asarray(res.results[0][k]) for k in ("dbg_ug", "dbg_wd", "dbg_wb", "dbg_wdi", "dbg_xh", "dbg_z1")}
        kernel._dbg_all = [{k: np.asarray(r[k]) for k in ("dbg_ug", "dbg_wd", "dbg_xh", "dbg_z1")} for r in res.results]
    out = np.empty((1, T, DM), np.float32)
    for cid in range(R):
        arr = np.asarray(res.results[cid]["out_x"])
        out[0, cid * TC:(cid + 1) * TC, :] = (
            arr.reshape(128, NCH, TC).transpose(2, 1, 0).reshape(TC, DM))
    return out


# version 03
# speedup vs baseline: 511.3171x; 1.3158x over previous
import math
from contextlib import ExitStack

import numpy as np
import ml_dtypes

import concourse.bass as bass
import concourse.mybir as mybir
from concourse.bass_utils import run_bass_kernel_spmd
from concourse.aot_env import allow_ici_all_to_all

BF16 = mybir.dt.bfloat16
F32 = mybir.dt.float32
AF = mybir.ActivationFunctionType
ALU = mybir.AluOpType
bfdt = ml_dtypes.bfloat16

R = 8
T = 8192
DM = 2048
TC = 1024
NCH = 16
INTER = 8192
EPS = 1e-5
SM_SCALE = 1.0 / math.sqrt(128.0)



class Ctx:
    def __init__(self):
        self.semh = {}
        self.semv = {}
        self.hw = {"TE": {}, "VE": {}, "SE": {}, "SY": {}, "GP": {}}
        self.ev = {}
        self.ev_bank = {}
        self.slab_ev = [None, None, None, None]
        self.slab_i = 0
        self.rope_i = 0
        self.rope_ev = {}
        self.dsi = 0

    def next_dsem(self):
        d = f"ds{self.dsi % 4}"
        self.dsi += 1
        return d

    def add_sem(self, name, handle):
        self.semh[name] = handle
        self.semv[name] = 0

    def inc(self, instr, s, n=1):
        self.semv[s] += n
        instr.then_inc(self.semh[s], n)
        return (s, self.semv[s])

    def wait(self, engname, eng, ev):
        if ev is None:
            return
        s, v = ev
        if v <= 0:
            return
        if self.hw[engname].get(s, 0) >= v:
            return
        eng.wait_ge(self.semh[s], v)
        self.hw[engname][s] = v

    def now(self, s):
        return (s, self.semv[s])

    def seal_banks(self, ev):
        for i in range(7):
            self.ev_bank[i] = ev


class BankPool:
    def __init__(self, ctx):
        self.ctx = ctx
        self.i = 0

    def acquire(self, te):
        b = self.i % 7
        self.i += 1
        self.ctx.wait("TE", te, self.ctx.ev_bank.get(b))
        return b

    def release(self, b, ev):
        self.ctx.ev_bank[b] = ev


def build_program(upto=4, debug_wts=False):
    nc = bass.Bass("TRN2", target_bir_lowering=False)
    D = {}

    def din(name, shape, dt):
        D[name] = nc.dram_tensor(name, shape, dt, kind="ExternalInput")

    din("x_fm", [DM, TC], BF16)
    din("Wq", [DM, DM], BF16)
    din("Wk", [DM, 512], BF16)
    din("Wv", [DM, 512], BF16)
    din("Wo", [DM, DM], BF16)
    din("bqr", [1, 2048], BF16)
    din("bkr", [1, 512], BF16)
    din("bvr", [1, 512], BF16)
    din("rpmc", [128, 128], BF16)
    din("wqT_sh", [DM, 256], F32)
    din("wkT_sh", [DM, 64], F32)
    din("wvT_sh", [DM, 64], F32)
    din("woT_sh", [DM, 256], F32)
    din("wupT_sh", [DM, 1024], F32)
    din("wgT_sh", [DM, 1024], F32)
    din("wdT_sh", [INTER, 256], F32)
    din("su_ug2", [128, 16], F32)
    din("su_dp2", [128, 64], F32)
    din("wsup2", [128, 64], F32)
    din("wsg2", [128, 64], F32)
    din("wsdp2", [128, 16], F32)
    din("Hpm", [DM, DM], BF16)
    din("H64c", [64, 64], BF16)
    din("H128c", [128, 128], BF16)
    din("identc", [128, 128], BF16)
    din("onesc", [128, 128], BF16)
    din("maskcc", [128, 896], BF16)
    din("ra_cos", [128, TC], BF16)
    din("ra_sin", [128, TC], BF16)
    din("rb_cos", [128, TC], BF16)
    din("rb_sin", [128, TC], BF16)
    din("diagTc", [64, 128], F32)
    din("qdecc", [128, 128], BF16)
    din("kdecc", [128, 2], F32)
    din("blkvc", [128, 2], F32)
    D["out_x"] = nc.dram_tensor("out_x", [128, NCH * TC], F32, kind="ExternalOutput")
    if debug_wts:
        D["dbg_ug"] = nc.dram_tensor("dbg_ug", [R * DM, 2048], BF16, kind="ExternalOutput")
        D["dbg_wd"] = nc.dram_tensor("dbg_wd", [R * INTER, 256], BF16, kind="ExternalOutput")
        D["dbg_wb"] = nc.dram_tensor("dbg_wb", [R * DM, 640], BF16, kind="ExternalOutput")
        D["dbg_wdi"] = nc.dram_tensor("dbg_wdi", [INTER, 256], BF16, kind="ExternalOutput")
        D["dbg_xh"] = nc.dram_tensor("dbg_xh", [128, NCH * TC], BF16, kind="ExternalOutput")
        D["dbg_z1"] = nc.dram_tensor("dbg_z1", [128, 2 * NCH * TC], BF16, kind="ExternalOutput")

    for name, shape, dt in [
        ("ug_in", [DM, 2048], BF16), ("ug_out", [R * DM, 2048], BF16),
        ("wb_in", [DM, 640], BF16), ("wb_out", [R * DM, 640], BF16),
        ("wdi", [INTER, 256], BF16), ("wd_in", [INTER, 256], BF16),
        ("wd_out", [R * INTER, 256], BF16),
        ("qk_in", [R * 384, TC], BF16), ("qk_out", [R * 384, TC], BF16),
        ("v_in", [R * TC, 128], BF16), ("v_out", [R * TC, 128], BF16),
        ("o_in", [R * 256, TC], BF16), ("o_out", [R * 256, TC], BF16),
        ("qk2_in", [R * 384, TC], BF16), ("qk2_out", [R * 384, TC], BF16),
        ("ktv_in", [R * TC, 256], BF16), ("ktv_out", [R * TC, 256], BF16),
        ("o2_in", [R * 256, TC], BF16), ("o2_out", [R * 256, TC], BF16),
    ]:
        D[name] = nc.dram_tensor(name, shape, dt)

    with ExitStack() as st:
        def sbt(name, shape, dt):
            D[name] = st.enter_context(nc.sbuf_tensor(name, shape, dt))

        sbt("x32", [128, NCH * TC], F32)
        sbt("qbuf", [128, NCH * TC], BF16)
        sbt("u1", [128, 2 * NCH * TC], BF16)
        sbt("wstream", [128, 4 * 512], BF16)
        sbt("t1", [128, 512], F32)
        sbt("t2", [128, 512], F32)
        sbt("tg", [128, 512], F32)
        sbt("ptile", [128, 3 * 512], BF16)
        sbt("mrow", [1, 512], F32)
        sbt("vrow", [1, 512], F32)
        sbt("wrow", [1, 512], F32)
        sbt("mr_row", [1, 2 * 512], BF16)
        sbt("rlrow", [1, 512], F32)
        sbt("epsrow", [1, 1], F32)
        sbt("rlrow_bf", [1, 512], BF16)
        sbt("ropec", [128, TC], BF16)
        sbt("ropes", [128, TC], BF16)
        sbt("maskt", [128, 896], BF16)
        sbt("diagt", [64, 128], F32)
        sbt("qdect", [128, 128], BF16)
        sbt("kdect", [128, 2], F32)
        sbt("blkvt", [128, 2], F32)
        sbt("identt", [128, 128], BF16)
        sbt("onest", [128, 128], BF16)
        sbt("h64t", [64, 64], BF16)
        sbt("h128t", [128, 128], BF16)
        sbt("sut", [128, 16], F32)
        sbt("suwut", [128, 64], F32)
        sbt("wsupt", [128, 64], F32)
        sbt("wsgt", [128, 64], F32)
        sbt("wsdpt", [128, 16], F32)
        sbt("bqrt", [1, 2048], BF16)
        sbt("bkrt", [1, 512], BF16)
        sbt("bvrt", [1, 512], BF16)
        sbt("rpmt", [128, 128], BF16)
        sbt("kvf0", [128, 128], F32)
        sbt("kvf1", [128, 128], F32)
        sbt("kvb0", [128, 2 * 128], BF16)
        sbt("kvb1", [128, 2 * 128], BF16)
        sbt("smt", [128, 2 * 64], BF16)
        sbt("tmq", [128, 2 * 64], BF16)
        sbt("ktmd", [128, 2 * 8 * 128], BF16)
        sbt("fold_a", [64, 2 * 512], BF16)

        pf = [st.enter_context(nc.psum_tensor(f"pf{i}", [128, 512], F32))
              for i in range(7)]
        ptp = st.enter_context(nc.psum_tensor("ptp", [128, 1024], BF16))
        D["pf"] = pf
        D["ptp"] = ptp

        semnames = ["d0", "d1", "d2", "d3", "d4", "d5",
                    "ds0", "ds1", "ds2", "ds3", "fs",
                    "fa0", "fa1", "fa2", "fa3", "sa0", "sa1", "sa2", "sa3",
                    "sg0", "sg1", "g0", "g1", "pe", "ve", "se", "cc"]
        for n in semnames:
            D["sem_" + n] = st.enter_context(nc.semaphore(n))

        block = st.enter_context(nc.Block())
        ctx = Ctx()
        for n in semnames:
            ctx.add_sem(n, D["sem_" + n])

        _emit(ctx, block, D, upto, debug_wts)

    return nc


def _emit(ctx, block, D, upto, debug_wts=False):
    c = ctx
    pf = D["pf"]
    ptp = D["ptp"]
    rot = BankPool(ctx)

    def ap(name):
        return D[name]

    def cap(name):
        return D[name].ap().opt()

    def p0_sync(sy):
        def ld(dst, src):
            return c.inc(sy.dma_start(out=dst, in_=src), "d0", 16)

        ld(ap("identt")[:, :], ap("identc")[:, :])
        ld(ap("onest")[:, :], ap("onesc")[:, :])
        ld(ap("h64t")[:, :], ap("H64c")[:, :])
        ld(ap("h128t")[:, :], ap("H128c")[:, :])
        ld(ap("maskt")[:, :], ap("maskcc")[:, :])
        ld(ap("ropec")[:, :], ap("ra_cos")[:, :])
        ld(ap("ropes")[:, :], ap("ra_sin")[:, :])
        ld(ap("diagt")[:, :], ap("diagTc")[:, :])
        ld(ap("qdect")[:, :], ap("qdecc")[:, :])
        ld(ap("kdect")[:, :], ap("kdecc")[:, :])
        ld(ap("blkvt")[:, :], ap("blkvc")[:, :])
        ld(ap("sut")[:, :], ap("su_ug2")[:, :])
        ld(ap("suwut")[:, :], ap("su_dp2")[:, :])
        ld(ap("wsupt")[:, :], ap("wsup2")[:, :])
        ld(ap("wsgt")[:, :], ap("wsg2")[:, :])
        ld(ap("wsdpt")[:, :], ap("wsdp2")[:, :])
        ld(ap("bqrt")[:, :], ap("bqr")[:, :])
        ld(ap("bkrt")[:, :], ap("bkr")[:, :])
        ld(ap("bvrt")[:, :], ap("bvr")[:, :])
        ld(ap("rpmt")[:, :], ap("rpmc")[:, :])
        for ch in range(NCH):
            ld(ap("u1")[:, 8192 + ch * TC: 8192 + (ch + 1) * TC],
               ap("x_fm")[ch * 128:(ch + 1) * 128, :])
        ctx.ev["p0_loads"] = c.now("d0")

    block.sync(p0_sync)

    def p0_gp(gp):
        for lc in range(8):
            c.inc(gp.dma_start(out=ap("ug_in")[:, lc * 256: lc * 256 + 128],
                               in_=ap("wupT_sh")[:, lc * 128:(lc + 1) * 128]), "g0", 16)
            c.inc(gp.dma_start(out=ap("ug_in")[:, lc * 256 + 128: lc * 256 + 256],
                               in_=ap("wgT_sh")[:, lc * 128:(lc + 1) * 128]), "g0", 16)
        c.inc(gp.dma_start(out=ap("wb_in")[:, 0:256], in_=ap("wqT_sh")[:, :]), "g0", 16)
        c.inc(gp.dma_start(out=ap("wb_in")[:, 256:320], in_=ap("wkT_sh")[:, :]), "g0", 16)
        c.inc(gp.dma_start(out=ap("wb_in")[:, 320:384], in_=ap("wvT_sh")[:, :]), "g0", 16)
        c.inc(gp.dma_start(out=ap("wb_in")[:, 384:640], in_=ap("woT_sh")[:, :]), "g0", 16)
        ctx.ev["ugwb_in"] = c.now("g0")

    block.gpsimd(p0_gp)

    def p0_se(se):
        c.wait("SE", se, ctx.ev["p0_loads"])
        ins = None
        for ch in range(NCH):
            ins = se.copy(ap("x32")[:, ch * TC:(ch + 1) * TC],
                          ap("u1")[:, 8192 + ch * TC: 8192 + (ch + 1) * TC])
        ctx.ev["x32_ready"] = c.inc(ins, "se")

    block.scalar(p0_se)

    def p0_ve(ve):
        c.wait("VE", ve, ctx.ev["p0_loads"])
        ve.tensor_scalar(ap("wsupt")[:, :], ap("wsupt")[:, :],
                         1.0 / math.sqrt(2048.0), None, op0=ALU.mult)
        ve.tensor_scalar(ap("wsgt")[:, :], ap("wsgt")[:, :],
                         1.0 / math.sqrt(2048.0), None, op0=ALU.mult)
        ve.tensor_scalar(ap("wsdpt")[:, :], ap("wsdpt")[:, :],
                         1.0 / math.sqrt(8192.0), None, op0=ALU.mult)
        ins = ve.tensor_tensor(ap("suwut")[:, :], ap("suwut")[:, :],
                               ap("wsupt")[:, :], op=ALU.mult)
        ctx.ev["tabs_ready"] = c.inc(ins, "ve")
        ve.memset(ap("epsrow")[:, :], EPS)
        ve.memset(ap("smt")[:, :], 0)
        ve.memset(ap("ktmd")[:, :], 0)
        ve.memset(ap("kvb0")[:, :], 0)
        ve.memset(ap("kvb1")[:, :], 0)
        ve.memset(ap("kvf0")[:, :], 0)
        ins = ve.memset(ap("kvf1")[:, :], 0)
        ctx.ev["pads_zero"] = c.inc(ins, "ve")

    block.vector(p0_ve)

    def slab_slot():
        s = ctx.slab_i % 4
        ctx.slab_i += 1
        return s, ctx.slab_ev[s]

    def emit_ln(tb, su, xn_dst, pre_ev=None):
        x32 = ap("x32")

        def ve1(ve):
            c.wait("VE", ve, ctx.ev["x32_ready"])
            c.wait("VE", ve, pre_ev)
            ins = None
            for ch in range(NCH):
                xs = x32[:, ch * TC + tb * 512: ch * TC + (tb + 1) * 512]
                ve.tensor_copy(ap("u1")[:, 16384 + ch * 512: 16384 + (ch + 1) * 512], xs)
                ins = ve.tensor_tensor(
                    ap("u1")[:, 24576 + ch * 512: 24576 + (ch + 1) * 512],
                    xs, xs, op=ALU.mult)
            ctx.ev["ln_pre"] = c.inc(ins, "ve")

        def te1(te):
            bs = rot.acquire(te)
            bq_ = rot.acquire(te)
            c.wait("TE", te, ctx.ev["ln_pre"])
            for ch in range(NCH):
                te.matmul(pf[bs][0:1, :], ap("onest")[:, 0:1],
                          ap("u1")[:, 16384 + ch * 512: 16384 + (ch + 1) * 512],
                          start=(ch == 0), stop=(ch == NCH - 1))
            for ch in range(NCH):
                ins = te.matmul(pf[bq_][0:1, :], ap("onest")[:, 0:1],
                                ap("u1")[:, 24576 + ch * 512: 24576 + (ch + 1) * 512],
                                start=(ch == 0), stop=(ch == NCH - 1))
            ctx.ev["ln_sums"] = c.inc(ins, "pe")
            ctx.ev["ln_banks"] = (bs, bq_)

        def ve2(ve):
            bs, bq_ = ctx.ev["ln_banks"]
            c.wait("VE", ve, ctx.ev["ln_sums"])
            ve.tensor_scalar(ap("mrow")[:, :], pf[bs][0:1, :], 1.0 / DM, None,
                             op0=ALU.mult)
            ve.tensor_scalar(ap("vrow")[:, :], pf[bq_][0:1, :], 1.0 / DM, None,
                             op0=ALU.mult)
            ve.tensor_tensor(ap("wrow")[:, :], ap("mrow")[:, :], ap("mrow")[:, :],
                             op=ALU.mult)
            ve.tensor_copy(ap("mr_row")[:, 0:512], ap("mrow")[:, :])
            ins = ve.tensor_tensor(ap("vrow")[:, :], ap("vrow")[:, :],
                                   ap("wrow")[:, :], op=ALU.subtract)
            ctx.ev["ln_var"] = c.inc(ins, "ve")
            rot.release(bs, ctx.ev["ln_var"])
            rot.release(bq_, ctx.ev["ln_var"])

        def se1(se):
            c.wait("SE", se, ctx.ev["ln_var"])
            se.copy(ap("wrow")[:, :], ap("mrow")[:, :])
            se.activation(ap("wrow")[:, 0:256], ap("vrow")[:, 0:256], AF.Ln,
                          bias=ap("epsrow")[:, :])
            se.activation(ap("wrow")[:, 256:512], ap("vrow")[:, 256:512], AF.Ln,
                          bias=ap("epsrow")[:, :])
            se.activation(ap("wrow")[:, 0:256], ap("wrow")[:, 0:256], AF.Exp,
                          scale=-0.5)
            ins = se.activation(ap("wrow")[:, 256:512], ap("wrow")[:, 256:512],
                                AF.Exp, scale=-0.5)
            ctx.ev["ln_rstd"] = c.inc(ins, "se")

        def ve3(ve):
            c.wait("VE", ve, ctx.ev["ln_rstd"])
            ve.tensor_copy(ap("mr_row")[:, 0:512], ap("mrow")[:, :])
            ins = ve.tensor_copy(ap("mr_row")[:, 512:1024], ap("wrow")[:, :])
            ctx.ev["ln_mr"] = c.inc(ins, "ve")

        def te2(te):
            b1 = rot.acquire(te)
            b2 = rot.acquire(te)
            c.wait("TE", te, ctx.ev["ln_mr"])
            te.matmul(pf[b1][:, :], ap("onest")[0:1, :], ap("mr_row")[:, 0:512],
                      start=True, stop=True)
            ins = te.matmul(pf[b2][:, :], ap("onest")[0:1, :], ap("mr_row")[:, 512:1024],
                            start=True, stop=True)
            ctx.ev["ln_bc"] = c.inc(ins, "pe")
            ctx.ev["ln_bcb"] = (b1, b2)

        def ve4(ve):
            b1, b2 = ctx.ev["ln_bcb"]
            c.wait("VE", ve, ctx.ev["ln_bc"])
            ve.tensor_copy(ap("ptile")[:, 0:512], pf[b1][:, :])
            ins = ve.tensor_copy(ap("ptile")[:, 512:1024], pf[b2][:, :])
            ev = c.inc(ins, "ve")
            rot.release(b1, ev)
            rot.release(b2, ev)
            pend = []
            for ch in range(NCH):
                xs = x32[:, ch * TC + tb * 512: ch * TC + (tb + 1) * 512]
                tmp = ap("t1") if ch % 2 == 0 else ap("t2")
                ve.tensor_tensor(tmp[:, :], xs, ap("ptile")[:, 0:512],
                                 op=ALU.subtract)
                if pend:
                    ch0, t0 = pend.pop()
                    ve.scalar_tensor_tensor(
                        xn_dst[:, ch0 * 512:(ch0 + 1) * 512], t0[:, :],
                        (ap("sut")[:, ch0:ch0 + 1] if su else 1.0),
                        ap("ptile")[:, 512:1024], op0=ALU.mult, op1=ALU.mult)
                pend.append((ch, tmp))
            ch0, t0 = pend.pop()
            ins = ve.scalar_tensor_tensor(
                xn_dst[:, ch0 * 512:(ch0 + 1) * 512], t0[:, :],
                (ap("sut")[:, ch0:ch0 + 1] if su else 1.0),
                ap("ptile")[:, 512:1024], op0=ALU.mult, op1=ALU.mult)
            ctx.ev["xn_ready"] = c.inc(ins, "ve")

        block.vector(ve1)
        block.tensor(te1)
        block.vector(ve2)
        block.scalar(se1)
        block.vector(ve3)
        block.tensor(te2)
        block.vector(ve4)
        return ctx.ev["xn_ready"]

    QROT = 24576

    def emit_rope_evac(name, tb, items, tabc, tabs, ve_pre=None):
        ve_base = ctx.semv["ve"]
        se_base = ctx.semv["se"]
        n_rope = sum(1 for it in items if it[2])
        rbanks = {}
        slots = []
        slot_wait = []
        for ri in range(n_rope):
            sl = ctx.rope_i % 16
            ctx.rope_i += 1
            slots.append(sl)
            slot_wait.append(ctx.rope_ev.get(sl))
            ctx.rope_ev[sl] = ("ve", ve_base + ri + 1)

        def seb(se):
            assert ctx.semv["se"] == se_base
            ri = 0
            for oc, dst, do_rope in items:
                bk = ctx.ev[(name, "bank", oc)]
                c.wait("SE", se, ctx.ev[(name, "mm", oc)])
                if do_rope:
                    c.wait("SE", se, slot_wait[ri])
                    sl = slots[ri]
                    ins = se.copy(
                        ap("u1")[:, QROT + sl * 512: QROT + (sl + 1) * 512],
                        pf[bk][:, :])
                    ctx.ev[(name, "qb", ri)] = c.inc(ins, "se")
                    ri += 1
                else:
                    ins = se.copy(dst, pf[bk][:, :])
                    ev = c.inc(ins, "se")
                    ctx.ev[(name, "evdone")] = ev

        def teb(te):
            ri = 0
            for oc, dst, do_rope in items:
                if not do_rope:
                    continue
                rb = rot.acquire(te)
                c.wait("TE", te, ctx.ev[(name, "qb", ri)])
                sl = slots[ri]
                ins = te.matmul(
                    pf[rb][:, :], ap("rpmt")[:, :],
                    ap("u1")[:, QROT + sl * 512: QROT + (sl + 1) * 512],
                    start=True, stop=True)
                c.inc(ins, "pe")
                ctx.ev[(name, "rot", ri)] = c.now("pe")
                rbanks[ri] = rb
                rot.release(rb, ("ve", ve_base + ri + 1))
                ri += 1

        def veb(ve):
            assert ctx.semv["ve"] == ve_base
            c.wait("VE", ve, ve_pre)
            cos = tabc[:, tb * 512:(tb + 1) * 512]
            sin = tabs[:, tb * 512:(tb + 1) * 512]
            ritems = [(oc, dst) for oc, dst, dr in items if dr]
            pend = []
            for ri, (oc, dst) in enumerate(ritems):
                rb = rbanks[ri]
                sl = slots[ri]
                c.wait("VE", ve, ctx.ev[(name, "rot", ri)])
                slap = ap("u1")[:, QROT + sl * 512: QROT + (sl + 1) * 512]
                tmp = ap("t2") if ri % 2 == 0 else ap("tg")
                ve.tensor_tensor(dst, slap, cos, op=ALU.mult)
                ve.tensor_tensor(tmp[:, :], pf[rb][:, :], sin, op=ALU.mult)
                if pend:
                    d0, t0 = pend.pop()
                    c.inc(ve.tensor_tensor(d0, d0, t0[:, :], op=ALU.add), "ve")
                pend.append((dst, tmp))
            d0, t0 = pend.pop()
            c.inc(ve.tensor_tensor(d0, d0, t0[:, :], op=ALU.add), "ve")
            assert ctx.semv["ve"] == ve_base + n_rope
            ctx.ev[(name, "evdone")] = c.now("ve")

        block.scalar(seb)
        block.tensor(teb)
        if n_rope:
            block.vector(veb)
        return ctx.ev[(name, "evdone")]

    def emit_stream_mm(name, dsem, n_oc, grp, slab_fn, slab_w, rhs_fn,
                       sy_pre=None, te_pre=None, nK=NCH, dual_dma=False,
                       bias_fn=None, release_fn=None):
        dsem = ctx.next_dsem()
        ngrp = (n_oc + grp - 1) // grp
        pe_base = ctx.semv["pe"]
        n_slab = ngrp * nK
        plan = []
        si = 0
        for gi in range(ngrp):
            for k in range(nK):
                plan.append((gi, k, si))
                si += 1

        def syb(sy):
            for s_, v_ in ([] if sy_pre is None else sy_pre):
                c.wait("SY", sy, (s_, v_))
            for gi, k, i in plan:
                s, wev = slab_slot()
                c.wait("SY", sy, wev)
                src = slab_fn(gi, k)
                if dual_dma:
                    i1 = sy.dma_start(
                        out=ap("wstream")[:, s * 512: s * 512 + slab_w // 2],
                        in_=src[0])
                    c.inc(i1, dsem, 16)
                    ins = sy.dma_start(
                        out=ap("wstream")[:, s * 512 + slab_w // 2: s * 512 + slab_w],
                        in_=src[1])
                    ev = c.inc(ins, dsem, 16)
                else:
                    ins = sy.dma_start(out=ap("wstream")[:, s * 512: s * 512 + slab_w],
                                       in_=src)
                    ev = c.inc(ins, dsem, 16)
                ctx.ev[(name, "slab", i)] = (ev, s)
                ctx.slab_ev[s] = ("pe", pe_base + i + 1)
            gi, k, i = plan[-1]
            _, s = ctx.ev[(name, "slab", i)]
            src = slab_fn(gi, k)
            if dual_dma:
                src = src[1]
                ins = sy.dma_start(
                    out=ap("wstream")[:, s * 512 + slab_w // 2: s * 512 + slab_w],
                    in_=src)
            else:
                ins = sy.dma_start(out=ap("wstream")[:, s * 512: s * 512 + slab_w],
                                   in_=src)
            ctx.ev[(name, "slab_tail")] = c.inc(ins, dsem, 16)

        def teb(te):
            assert ctx.semv["pe"] == pe_base, (name, ctx.semv["pe"], pe_base)
            if te_pre is not None:
                c.wait("TE", te, te_pre)
            for gi in range(ngrp):
                ocs = [gi * grp + j for j in range(grp) if gi * grp + j < n_oc]
                banks = [rot.acquire(te) for _ in ocs]
                for k in range(nK):
                    i = gi * nK + k
                    ev, s = ctx.ev[(name, "slab", i)]
                    if i + 1 < n_slab:
                        ev = ctx.ev[(name, "slab", i + 1)][0]
                    else:
                        ev = ctx.ev[(name, "slab_tail")]
                    c.wait("TE", te, ev)
                    last = (k == nK - 1) and bias_fn is None
                    for j, oc in enumerate(ocs):
                        ins = te.matmul(
                            pf[banks[j]][:, :],
                            ap("wstream")[:, s * 512 + j * 128: s * 512 + (j + 1) * 128],
                            rhs_fn(k), start=(k == 0), stop=last)
                    if k < nK - 1 or bias_fn is None:
                        c.inc(ins, "pe")
                if bias_fn is not None:
                    for j, oc in enumerate(ocs):
                        ins = te.matmul(pf[banks[j]][:, :], bias_fn(oc),
                                        ap("maskt")[0:1, 384:896],
                                        start=False, stop=True)
                    c.inc(ins, "pe")
                for j, oc in enumerate(ocs):
                    ctx.ev[(name, "mm", oc)] = c.now("pe")
                    ctx.ev[(name, "bank", oc)] = banks[j]
                    if release_fn is not None:
                        rot.release(banks[j], release_fn(oc))
            assert ctx.semv["pe"] == pe_base + n_slab

        block.sync(syb)
        block.tensor(teb)


    def phase1():
        for tb in range(2):
            xn_ev = emit_ln(tb, False, ap("u1")[:, 8192:16384])

            def rhs(k):
                return ap("u1")[:, 8192 + k * 512: 8192 + (k + 1) * 512]

            se_b = ctx.semv["se"]
            emit_stream_mm(f"qp{tb}", "d1", 16, 4,
                           lambda gi, k: ap("Wq")[k * 128:(k + 1) * 128,
                                                  gi * 512:(gi + 1) * 512],
                           512, rhs, te_pre=xn_ev,
                           bias_fn=lambda oc: ap("bqrt")[0:1, oc * 128:(oc + 1) * 128],
                           release_fn=lambda oc: ("se", se_b + oc + 1))
            items = [(h, ap("qbuf")[:, h * TC + tb * 512: h * TC + (tb + 1) * 512],
                      True) for h in range(16)]
            e1 = emit_rope_evac(f"qp{tb}", tb, items, ap("ropec"), ap("ropes"))
            se_b2 = ctx.semv["se"]
            emit_stream_mm(f"kp{tb}", "d1", 4, 4,
                           lambda gi, k: ap("Wk")[k * 128:(k + 1) * 128, 0:512],
                           512, rhs,
                           bias_fn=lambda oc: ap("bkrt")[0:1, oc * 128:(oc + 1) * 128],
                           release_fn=lambda oc: ("se", se_b2 + oc + 1))
            items = [(kv, ap("u1")[:, kv * TC + tb * 512: kv * TC + (tb + 1) * 512],
                      True) for kv in range(4)]
            e2 = emit_rope_evac(f"kp{tb}", tb, items, ap("ropec"), ap("ropes"))
            se_b3 = ctx.semv["se"]
            emit_stream_mm(f"vp{tb}", "d1", 4, 4,
                           lambda gi, k: ap("Wv")[k * 128:(k + 1) * 128, 0:512],
                           512, rhs,
                           bias_fn=lambda oc: ap("bvrt")[0:1, oc * 128:(oc + 1) * 128],
                           release_fn=lambda oc: ("se", se_b3 + oc + 1))
            items = [(kv, ap("u1")[:, 4096 + kv * TC + tb * 512:
                                   4096 + kv * TC + (tb + 1) * 512], False)
                     for kv in range(4)]
            e3 = emit_rope_evac(f"vp{tb}", tb, items, ap("ropec"), ap("ropes"))
            ctx.ev[f"qkv_ev{tb}"] = [e1, e2, e3]

        se_vt0 = ctx.semv["se"]
        d1_vt0 = ctx.semv["d1"]

        def te_vt(te):
            for e in ctx.ev["qkv_ev0"] + ctx.ev["qkv_ev1"]:
                c.wait("TE", te, e)
            for i in range(32):
                kv, ti = divmod(i, 8)
                if i >= 2:
                    c.wait("TE", te, ("se", se_vt0 + i - 1))
                ins = te.transpose(ptp[:, (i % 2) * 128:(i % 2) * 128 + 128],
                                   ap("u1")[:, 4096 + kv * TC + ti * 128:
                                            4096 + kv * TC + (ti + 1) * 128],
                                   ap("identt")[:, :])
                ctx.ev[("vtmm", i)] = c.inc(ins, "pe")

        def se_vt(se):
            assert ctx.semv["se"] == se_vt0
            for i in range(32):
                c.wait("SE", se, ctx.ev[("vtmm", i)])
                if i >= 2:
                    c.wait("SE", se, ("d1", d1_vt0 + 32 * (i - 1)))
                ins = se.copy(ap("u1")[:, 16384 + (i % 2) * 128: 16384 + (i % 2) * 128 + 128],
                              ptp[:, (i % 2) * 128:(i % 2) * 128 + 128])
                ctx.ev[("vtse", i)] = c.inc(ins, "se")

        def sy_vt(sy):
            assert ctx.semv["d1"] == d1_vt0
            for i in range(32):
                kv, ti = divmod(i, 8)
                c.wait("SY", sy, ctx.ev[("vtse", i)])
                for j in (2 * kv, 2 * kv + 1):
                    ins = sy.dma_start(
                        out=ap("v_in")[j * TC + ti * 128: j * TC + (ti + 1) * 128, :],
                        in_=ap("u1")[:, 16384 + (i % 2) * 128: 16384 + (i % 2) * 128 + 128])
                    c.inc(ins, "d1", 16)
            ctx.ev["v_in_done"] = c.now("d1")
            for e in ctx.ev["qkv_ev0"] + ctx.ev["qkv_ev1"]:
                c.wait("SY", sy, e)
            for j in range(R):
                for hl in range(2):
                    h = 2 * j + hl
                    c.inc(sy.dma_start(
                        out=ap("qk_in")[j * 384 + hl * 128: j * 384 + (hl + 1) * 128, :],
                        in_=ap("qbuf")[:, h * TC:(h + 1) * TC]), "d0", 16)
                kv = j // 2
                c.inc(sy.dma_start(
                    out=ap("qk_in")[j * 384 + 256: j * 384 + 384, :],
                    in_=ap("u1")[:, kv * TC:(kv + 1) * TC]), "d0", 16)
            ctx.ev["qk_in_done"] = c.now("d0")

        block.tensor(te_vt)
        block.scalar(se_vt)
        block.sync(sy_vt)

        def gp_cc(gp):
            c.wait("GP", gp, ctx.ev["qk_in_done"])
            with allow_ici_all_to_all():
                ctx.ev["cc_qk"] = c.inc(gp.collective_compute(
                    "AllToAll", ALU.bypass, replica_groups=[list(range(R))],
                    ins=[cap("qk_in")], outs=[cap("qk_out")]), "cc")
                c.wait("GP", gp, ctx.ev["v_in_done"])
                ctx.ev["cc_v"] = c.inc(gp.collective_compute(
                    "AllToAll", ALU.bypass, replica_groups=[list(range(R))],
                    ins=[cap("v_in")], outs=[cap("v_out")]), "cc")

        block.gpsimd(gp_cc)

    def fold_wd():
        pe_base = ctx.semv["pe"]
        se_base = ctx.semv["se"]
        d0_base = ctx.semv["d0"]

        ve_cast0 = [None]
        fsA = [None]

        def sy_lda(sy):
            wdr = ap("wdT_sh").rearrange("(a r) h -> a (r h)", a=64)
            for bg in range(64):
                if bg >= 2:
                    c.wait("SY", sy, ("ve", ve_cast0[0] + bg - 1))
                stage = ap("t2") if bg % 2 == 0 else ap("tg")
                c.inc(sy.dma_start(
                    out=stage[0:64, :],
                    in_=wdr[:, bg * 512:(bg + 1) * 512]), f"fa{bg % 4}", 16)

        def ve_lda(ve):
            assert ctx.semv["ve"] == ve_cast0[0]
            for bg in range(64):
                c.wait("VE", ve, (f"fa{bg % 4}", 16 * (bg // 4 + 1)))
                if bg >= 2:
                    c.wait("VE", ve, ("pe", pe_base + 2 * (bg - 2) + 2))
                stage = ap("t2") if bg % 2 == 0 else ap("tg")
                ins = ve.tensor_copy(
                    ap("fold_a")[:, (bg % 2) * 512:(bg % 2) * 512 + 512],
                    stage[0:64, :])
                ctx.ev[("fa_cast", bg)] = c.inc(ins, "ve")

        ve_cast0[0] = ctx.semv["ve"]
        block.sync(sy_lda)
        block.vector(ve_lda)

        def teA(te):
            assert ctx.semv["pe"] == pe_base
            for b_ in range(2):
                c.wait("TE", te, ctx.ev_bank.get(b_))
            for i in range(128):
                bg = i // 2
                c.wait("TE", te, ctx.ev[("fa_cast", bg)])
                if i >= 2:
                    c.wait("TE", te, ("se", se_base + i - 1))
                ins = te.matmul(pf[i % 2][0:64, 0:256], ap("h64t")[:, :],
                                ap("fold_a")[:, (bg % 2) * 512 + (i % 2) * 256:
                                             (bg % 2) * 512 + (i % 2) * 256 + 256],
                                start=True, stop=True)
                c.inc(ins, "pe")
            assert ctx.semv["pe"] == pe_base + 128

        def seA(se):
            assert ctx.semv["se"] == se_base
            for i in range(128):
                c.wait("SE", se, ("pe", pe_base + i + 1))
                if i >= 2:
                    c.wait("SE", se, (f"sa{(i - 2) % 4}", ((i - 2) // 4 + 1) * 16))
                ins = se.copy(ap("ptile")[0:64, 1024 + (i % 2) * 256: 1024 + (i % 2) * 256 + 256],
                              pf[i % 2][0:64, 0:256])
                c.inc(ins, "se")
            assert ctx.semv["se"] == se_base + 128

        def stA(gp):
            wdiv = ap("wdi").rearrange("(a b) h -> b a h", a=64, b=128)
            for i in range(128):
                c.wait("GP", gp, ("se", se_base + i + 1))
                c.inc(gp.dma_start(out=wdiv[i],
                                   in_=ap("ptile")[0:64, 1024 + (i % 2) * 256: 1024 + (i % 2) * 256 + 256]),
                      f"sa{i % 4}", 16)
            ctx.ev["wdi_done"] = [(f"sa{j}", ctx.semv[f"sa{j}"]) for j in range(4)]

        block.tensor(teA)
        block.scalar(seA)
        block.gpsimd(stA)

        pe_b = ctx.semv["pe"]
        ve_b = ctx.semv["ve"]
        d0_b = ctx.semv["d0"]

        def syB(sy):
            assert ctx.semv["d0"] == d0_b
            for e in ctx.ev["wdi_done"]:
                c.wait("SY", sy, e)
            c.wait("SY", sy, ctx.ev["qk_in_done"])
            for a in range(64):
                if a >= 2:
                    c.wait("SY", sy, ("pe", pe_b + a - 1))
                c.inc(sy.dma_start(
                    out=ap("qbuf")[:, (a % 2) * 256:(a % 2) * 256 + 256],
                    in_=ap("wdi")[a * 128:(a + 1) * 128, :]), "d0", 16)
            ctx.ev["foldb_ld"] = c.now("d0")

        def teB(te):
            assert ctx.semv["pe"] == pe_b
            for b_ in (2, 3):
                c.wait("TE", te, ctx.ev_bank.get(b_))
            for a in range(64):
                c.wait("TE", te, ("d0", d0_b + 16 * (a + 1)))
                if a >= 2:
                    c.wait("TE", te, ("ve", ve_b + a - 1))
                ins = te.matmul(pf[2 + a % 2][:, 0:256], ap("h128t")[:, :],
                                ap("qbuf")[:, (a % 2) * 256:(a % 2) * 256 + 256],
                                start=True, stop=True)
                c.inc(ins, "pe")
            assert ctx.semv["pe"] == pe_b + 64

        g0B = [None]

        def veB(ve):
            assert ctx.semv["ve"] == ve_b
            c.wait("VE", ve, ctx.ev["tabs_ready"])
            for a in range(64):
                c.wait("VE", ve, ("pe", pe_b + a + 1))
                if a >= 2:
                    c.wait("VE", ve, ("fs", g0B[0] + (a - 1) * 16))
                ins = ve.tensor_scalar(
                    ap("ptile")[:, 1024 + (a % 2) * 256: 1024 + (a % 2) * 256 + 256],
                    pf[2 + a % 2][:, 0:256],
                    ap("suwut")[:, a:a + 1], None, op0=ALU.mult)
                c.inc(ins, "ve")
            assert ctx.semv["ve"] == ve_b + 64

        def stB(gp):
            assert ctx.semv["fs"] == g0B[0]
            for a in range(64):
                c.wait("GP", gp, ("ve", ve_b + a + 1))
                c.inc(gp.dma_start(
                    out=ap("wd_in")[a * 128:(a + 1) * 128, :],
                    in_=ap("ptile")[:, 1024 + (a % 2) * 256: 1024 + (a % 2) * 256 + 256]),
                    "fs", 16)
            ctx.ev["wd_in_done"] = c.now("fs")

        block.sync(syB)
        block.tensor(teB)
        ctx.ev["foldB_pe_end"] = c.now("pe")
        g0B[0] = ctx.semv["fs"]
        block.vector(veB)
        block.gpsimd(stB)
        ctx.seal_banks(c.now("ve"))

        def gp_cc(gp):
            c.wait("GP", gp, ctx.ev["ugwb_in"])
            with allow_ici_all_to_all():
                ctx.ev["cc_ug"] = c.inc(gp.collective_compute(
                    "AllGather", ALU.bypass, replica_groups=[list(range(R))],
                    ins=[cap("ug_in")], outs=[cap("ug_out")]), "cc")
                ctx.ev["cc_wb"] = c.inc(gp.collective_compute(
                    "AllGather", ALU.bypass, replica_groups=[list(range(R))],
                    ins=[cap("wb_in")], outs=[cap("wb_out")]), "cc")
                c.wait("GP", gp, ctx.ev["wd_in_done"])
                ctx.ev["cc_wd"] = c.inc(gp.collective_compute(
                    "AllGather", ALU.bypass, replica_groups=[list(range(R))],
                    ins=[cap("wd_in")], outs=[cap("wd_out")]), "cc")

        block.gpsimd(gp_cc)

    def phase2():
        def sy_ld(sy):
            c.wait("SY", sy, ctx.ev["cc_qk"])
            c.wait("SY", sy, ctx.ev["foldB_pe_end"])
            for hl in range(2):
                for r in range(R):
                    c.inc(sy.dma_start(
                        out=ap("qbuf")[:, hl * T + r * TC: hl * T + (r + 1) * TC],
                        in_=ap("qk_out")[r * 384 + hl * 128:
                                         r * 384 + (hl + 1) * 128, :]), "d1", 16)
            for r in range(R):
                c.inc(sy.dma_start(
                    out=ap("u1")[:, r * TC:(r + 1) * TC],
                    in_=ap("qk_out")[r * 384 + 256: r * 384 + 384, :]), "d1", 16)
            c.wait("SY", sy, ctx.ev["cc_v"])
            for jt in range(64):
                c.inc(sy.dma_start(
                    out=ap("u1")[:, 8192 + jt * 128: 8192 + (jt + 1) * 128],
                    in_=ap("v_out")[jt * 128:(jt + 1) * 128, :]), "d1", 16)
            ctx.ev["att_ld"] = c.now("d1")

        block.sync(sy_ld)

        pe0 = ctx.semv["pe"]
        se0 = ctx.semv["se"]
        ve0 = ctx.semv["ve"]
        s_val, pv_val, exp_val, p_val = {}, {}, {}, {}
        lo_val, bc_val, rl_val, oev_val = {}, {}, {}, {}
        pe_n, se_n, ve_n = pe0, se0, ve0
        si_of = {}
        si = 0
        for hl in range(2):
            for ib in range(16):
                ext = 4 * ib + 4
                pend = []
                for jc in range(ext):
                    si_of[(hl, ib, jc)] = si
                    pe_n += 1
                    s_val[si] = pe_n
                    pend.append(si)
                    si += 1
                    if len(pend) >= 3:
                        s0 = pend.pop(0)
                        pe_n += 1
                        pv_val[s0] = pe_n
                while pend:
                    s0 = pend.pop(0)
                    pe_n += 1
                    pv_val[s0] = pe_n
                lo_val[(hl, ib)] = pe_n
                pe_n += 1
                bc_val[(hl, ib)] = pe_n
        for hl in range(2):
            for ib in range(16):
                for jc in range(4 * ib + 4):
                    s0 = si_of[(hl, ib, jc)]
                    se_n += 1
                    exp_val[s0] = se_n
        for hl in range(2):
            for ib in range(16):
                for jc in range(4 * ib + 4):
                    s0 = si_of[(hl, ib, jc)]
                    if jc >= 4 * ib:
                        ve_n += 1
                        p_val[s0] = ("ve", ve_n)
                    else:
                        p_val[s0] = ("se", exp_val[s0])
                ve_n += 1
                rl_val[(hl, ib)] = ve_n
                ve_n += 1
                oev_val[(hl, ib)] = ve_n
        n_si = si

        def se_att(se):
            assert ctx.semv["se"] == se0
            c.wait("SE", se, ctx.ev["wd_in_done"])
            for hl in range(2):
                for ib in range(16):
                    for jc in range(4 * ib + 4):
                        s0 = si_of[(hl, ib, jc)]
                        c.wait("SE", se, ("pe", s_val[s0]))
                        if s0 >= 3:
                            c.wait("SE", se, ("pe", pv_val[s0 - 3]))
                        ins = se.activation(
                            ap("ptile")[:, (s0 % 3) * 512:(s0 % 3 + 1) * 512],
                            pf[s0 % 3][:, :], AF.Exp, scale=SM_SCALE)
                        c.inc(ins, "se")
            assert ctx.semv["se"] == se_n

        def ve_att(ve):
            assert ctx.semv["ve"] == ve0
            for hl in range(2):
                for ib in range(16):
                    for jc in range(4 * ib, 4 * ib + 4):
                        s0 = si_of[(hl, ib, jc)]
                        tau = jc - 4 * ib
                        c.wait("VE", ve, ("se", exp_val[s0]))
                        ins = ve.tensor_tensor(
                            ap("ptile")[:, (s0 % 3) * 512:(s0 % 3 + 1) * 512],
                            ap("ptile")[:, (s0 % 3) * 512:(s0 % 3 + 1) * 512],
                            ap("maskt")[:, 384 - tau * 128: 896 - tau * 128],
                            op=ALU.mult)
                        c.inc(ins, "ve")
                    c.wait("VE", ve, ("pe", lo_val[(hl, ib)]))
                    ve.reciprocal(ap("rlrow")[:, :], pf[4][0:1, :])
                    ins = ve.tensor_copy(ap("rlrow_bf")[:, :], ap("rlrow")[:, :])
                    c.inc(ins, "ve")
                    c.wait("VE", ve, ("pe", bc_val[(hl, ib)]))
                    ve.tensor_copy(ap("tg")[:, :], pf[5][:, :])
                    ins = ve.tensor_tensor(
                        ap("u1")[:, 16384 + hl * T + ib * 512:
                                 16384 + hl * T + (ib + 1) * 512],
                        pf[3][:, :], ap("tg")[:, :], op=ALU.mult)
                    c.inc(ins, "ve")
            assert ctx.semv["ve"] == ve_n
            ctx.ev["att_done_ve"] = c.now("ve")

        def te_att(te):
            assert ctx.semv["pe"] == pe0
            c.wait("TE", te, ctx.ev["att_ld"])
            for i in range(7):
                c.wait("TE", te, ctx.ev_bank.get(i))
            for hl in range(2):
                for ib in range(16):
                    ext = 4 * ib + 4
                    if ib > 0 or hl > 0:
                        prev = (hl, ib - 1) if ib > 0 else (hl - 1, 15)
                        c.wait("TE", te, ("ve", oev_val[prev]))
                    pend = []
                    for jc in range(ext):
                        s0 = si_of[(hl, ib, jc)]
                        if s0 >= 3:
                            c.wait("TE", te, p_val[s0 - 3])
                        ins = te.matmul(
                            pf[s0 % 3][:, :],
                            ap("u1")[:, jc * 128:(jc + 1) * 128],
                            ap("qbuf")[:, hl * T + ib * 512: hl * T + (ib + 1) * 512],
                            start=True, stop=True)
                        c.inc(ins, "pe")
                        assert ctx.semv["pe"] == s_val[s0]
                        pend.append((jc, s0))
                        if len(pend) >= 3:
                            _lpv(te, hl, ib, pend.pop(0))
                    while pend:
                        _lpv(te, hl, ib, pend.pop(0))
                    c.wait("TE", te, ("ve", rl_val[(hl, ib)]))
                    ins = te.matmul(pf[5][:, :], ap("onest")[0:1, :],
                                    ap("rlrow_bf")[0:1, :], start=True, stop=True)
                    c.inc(ins, "pe")
                    assert ctx.semv["pe"] == bc_val[(hl, ib)]
            ctx.seal_banks(ctx.ev["att_done_ve"])

        def _lpv(te, hl, ib, item):
            jc, s0 = item
            first = (jc == 0)
            last = (jc == 4 * ib + 3)
            c.wait("TE", te, p_val[s0])
            te.matmul(pf[4][0:1, :], ap("onest")[:, 0:1],
                      ap("ptile")[:, (s0 % 3) * 512:(s0 % 3 + 1) * 512],
                      start=first, stop=last, skip_group_check=True)
            ins = te.matmul(pf[3][:, :],
                            ap("u1")[:, 8192 + jc * 128: 8192 + (jc + 1) * 128],
                            ap("ptile")[:, (s0 % 3) * 512:(s0 % 3 + 1) * 512],
                            start=first, stop=last, skip_group_check=True)
            c.inc(ins, "pe")
            assert ctx.semv["pe"] == pv_val[s0]

        block.scalar(se_att)
        block.vector(ve_att)
        block.tensor(te_att)

        def sy_oc(sy):
            c.wait("SY", sy, ctx.ev["att_done_ve"])
            for j in range(R):
                for hl in range(2):
                    c.inc(sy.dma_start(
                        out=ap("o_in")[j * 256 + hl * 128: j * 256 + (hl + 1) * 128, :],
                        in_=ap("u1")[:, 16384 + hl * T + j * TC:
                                     16384 + hl * T + (j + 1) * TC]), "d1", 16)
            ctx.ev["o_in_done"] = c.now("d1")

        block.sync(sy_oc)

        def gp_cc(gp):
            c.wait("GP", gp, ctx.ev["o_in_done"])
            with allow_ici_all_to_all():
                ctx.ev["cc_o"] = c.inc(gp.collective_compute(
                    "AllToAll", ALU.bypass, replica_groups=[list(range(R))],
                    ins=[cap("o_in")], outs=[cap("o_out")]), "cc")

        block.gpsimd(gp_cc)

        def sy_ld2(sy):
            c.wait("SY", sy, ctx.ev["cc_o"])
            for h in range(16):
                c.inc(sy.dma_start(
                    out=ap("qbuf")[:, h * TC:(h + 1) * TC],
                    in_=ap("o_out")[(h // 2) * 256 + (h % 2) * 128:
                                    (h // 2) * 256 + (h % 2) * 128 + 128, :]),
                    "d1", 16)
            ctx.ev["ao_ld"] = c.now("d1")

        block.sync(sy_ld2)

        for tb in range(2):
            def rhs(k, _tb=tb):
                return ap("qbuf")[:, k * TC + _tb * 512: k * TC + (_tb + 1) * 512]

            ve_bw = ctx.semv["ve"]
            emit_stream_mm(f"wo{tb}", "d1", 16, 4,
                           lambda gi, k: ap("Wo")[k * 128:(k + 1) * 128,
                                                  gi * 512:(gi + 1) * 512],
                           512, rhs, sy_pre=[ctx.ev["ao_ld"]],
                           te_pre=ctx.ev["ao_ld"],
                           release_fn=lambda oc, b=ve_bw: ("ve", b + oc + 1))

            def ve_ev(ve, _tb=tb, _vb=ve_bw):
                assert ctx.semv["ve"] == _vb
                ev = None
                for oc in range(16):
                    bk = ctx.ev[(f"wo{_tb}", "bank", oc)]
                    c.wait("VE", ve, ctx.ev[(f"wo{_tb}", "mm", oc)])
                    xs = ap("x32")[:, oc * TC + _tb * 512: oc * TC + (_tb + 1) * 512]
                    ins = ve.tensor_tensor(xs, xs, pf[bk][:, :], op=ALU.add)
                    ev = c.inc(ins, "ve")
                ctx.ev[f"res1_{_tb}"] = ev

            block.vector(ve_ev)

    def phase_ffn(call, dsem, pre_ev):
        for tb in range(2):
            tag = f"f{call}_{tb}"
            xn_ev = emit_ln(tb, True, ap("qbuf")[:, 8192:16384],
                            pre_ev=(pre_ev if tb == 0 else
                                    ctx.ev.get("dbg_ffn")))

            def rhs_xn(k):
                return ap("qbuf")[:, 8192 + k * 512: 8192 + (k + 1) * 512]

            se_bx = ctx.semv["se"]
            emit_stream_mm(f"xh{tag}", dsem, 16, 4,
                           lambda gi, k: ap("Hpm")[k * 128:(k + 1) * 128,
                                                   gi * 512:(gi + 1) * 512],
                           512, rhs_xn, te_pre=xn_ev,
                           release_fn=lambda oc: ("se", se_bx + oc + 1))

            def se_xh(se):
                assert ctx.semv["se"] == se_bx
                ev = None
                for oc in range(16):
                    bk = ctx.ev[(f"xh{tag}", "bank", oc)]
                    c.wait("SE", se, ctx.ev[(f"xh{tag}", "mm", oc)])
                    ins = se.copy(ap("qbuf")[:, oc * 512:(oc + 1) * 512], pf[bk][:, :])
                    ev = c.inc(ins, "se")
                ctx.ev[f"xh_ev{tag}"] = ev

            block.scalar(se_xh)

            def rhs_xh(k):
                return ap("qbuf")[:, k * 512:(k + 1) * 512]

            def ug_slab(gi, k):
                ci = 2 * gi
                r_ = ci // 8
                lc = ci % 8
                return ap("ug_out")[r_ * DM + k * 128: r_ * DM + (k + 1) * 128,
                                    lc * 256: lc * 256 + 512]

            se_bu = ctx.semv["se"]
            ve_bu = ctx.semv["ve"]

            def ug_rel(oc):
                ioc = oc // 2
                if oc % 2:
                    return ("se", se_bu + 2 * ioc + 2)
                return ("ve", ve_bu + ioc + 1)

            emit_stream_mm(f"ug{tag}", dsem, 128, 4, ug_slab, 512, rhs_xh,
                           sy_pre=[ctx.ev["cc_ug"]],
                           te_pre=ctx.ev[f"xh_ev{tag}"], release_fn=ug_rel)

            def se_ug(se):
                assert ctx.semv["se"] == se_bu
                for ioc in range(64):
                    bg = ctx.ev[(f"ug{tag}", "bank", 2 * ioc + 1)]
                    c.wait("SE", se, ctx.ev[(f"ug{tag}", "mm", 2 * ioc + 1)])
                    if ioc >= 2:
                        c.wait("SE", se, ("ve", ve_bu + ioc - 1))
                    tgb = ap("tg") if ioc % 2 == 0 else ap("t2")
                    ins = se.activation(tgb[:, 0:256], pf[bg][:, 0:256],
                                        AF.Silu,
                                        scale=ap("wsgt")[:, ioc:ioc + 1])
                    c.inc(ins, "se")
                    ins = se.activation(tgb[:, 256:512], pf[bg][:, 256:512],
                                        AF.Silu,
                                        scale=ap("wsgt")[:, ioc:ioc + 1])
                    ctx.ev[(tag, "tg", ioc)] = c.inc(ins, "se")

            def ve_ug(ve):
                assert ctx.semv["ve"] == ve_bu
                for ioc in range(64):
                    bu = ctx.ev[(f"ug{tag}", "bank", 2 * ioc)]
                    c.wait("VE", ve, ctx.ev[(tag, "tg", ioc)])
                    tgb = ap("tg") if ioc % 2 == 0 else ap("t2")
                    ins = ve.tensor_tensor(
                        ap("u1")[:, ioc * 512:(ioc + 1) * 512],
                        pf[bu][:, :], tgb[:, :], op=ALU.mult)
                    ev = c.inc(ins, "ve")
                    ctx.ev[(tag, "z1", ioc)] = ev
                    if ioc == 63:
                        ins = ve.tensor_copy(ap("t1")[:, 0:8], ap("t1")[:, 0:8])
                        c.inc(ins, "ve")

            block.scalar(se_ug)
            block.vector(ve_ug)

            if debug_wts and call == 1 and tb == 0:
                def sy_dbgf(sy):
                    c.wait("SY", sy, ctx.ev[(tag, "z1", 63)])
                    c.inc(sy.dma_start(out=ap("dbg_xh")[:, :],
                                       in_=ap("qbuf")[:, :]), "d5", 16)
                    c.inc(sy.dma_start(out=ap("dbg_z1")[:, :],
                                       in_=ap("u1")[:, :]), "d5", 16)
                    ctx.ev["dbg_ffn"] = c.now("d5")
                block.sync(sy_dbgf)

            pe_base = ctx.semv["pe"]

            dnsem = ctx.next_dsem()
            dnsem2 = ctx.next_dsem()

            def dn_loader(eng, engname, parity, qsem):
                c.wait(engname, eng, ctx.ev["cc_wd"])
                last = None
                for i in range(512):
                    if i % 2 != parity:
                        continue
                    hcp, ioc = divmod(i, 64)
                    s = i % 4
                    c.wait(engname, eng, ctx.slab_ev[s])
                    ins = eng.dma_start(
                        out=ap("wstream")[:, s * 512: s * 512 + 256],
                        in_=ap("wd_out")[hcp * INTER + ioc * 128:
                                         hcp * INTER + (ioc + 1) * 128, :])
                    ctx.ev[(tag, "dnslab", i)] = (c.inc(ins, qsem, 16), s)
                    ctx.slab_ev[s] = ("pe", pe_base + i + 1)
                    last = (hcp, ioc, s)
                hcp, ioc, s = last
                ins = eng.dma_start(
                    out=ap("wstream")[:, s * 512: s * 512 + 256],
                    in_=ap("wd_out")[hcp * INTER + ioc * 128:
                                     hcp * INTER + (ioc + 1) * 128, :])
                ctx.ev[(tag, "dntail", parity)] = c.inc(ins, qsem, 16)

            def sy_dn(sy):
                dn_loader(sy, "SY", 0, dnsem)

            def se_dn(se):
                dn_loader(se, "SE", 1, dnsem2)

            ve_bd = [None]

            def te_dn(te):
                assert ctx.semv["pe"] == pe_base
                ve_b = ve_bd[0]
                i = 0
                for hcp in range(8):
                    b0 = rot.acquire(te)
                    b1 = rot.acquire(te)
                    rot.release(b0, ("ve", ve_b + 2 * hcp + 1))
                    rot.release(b1, ("ve", ve_b + 2 * hcp + 2))
                    for ioc in range(64):
                        ev, s = ctx.ev[(tag, "dnslab", i)]
                        ev = (ctx.ev[(tag, "dnslab", i + 2)][0] if i + 2 < 512
                              else ctx.ev[(tag, "dntail", i % 2)])
                        c.wait("TE", te, ev)
                        zev = ctx.ev[(tag, "z1", ioc)]
                        c.wait("TE", te, (zev[0], zev[1] + 1))
                        te.matmul(pf[b0][:, :],
                                  ap("wstream")[:, s * 512: s * 512 + 128],
                                  ap("u1")[:, ioc * 512:(ioc + 1) * 512],
                                  start=(ioc == 0), stop=(ioc == 63))
                        ins = te.matmul(pf[b1][:, :],
                                        ap("wstream")[:, s * 512 + 128: s * 512 + 256],
                                        ap("u1")[:, ioc * 512:(ioc + 1) * 512],
                                        start=(ioc == 0), stop=(ioc == 63))
                        c.inc(ins, "pe")
                        i += 1
                    ctx.ev[(tag, "dnmm", hcp)] = c.now("pe")
                    ctx.ev[(tag, "dnbank", hcp)] = (b0, b1)

            def ve_dn(ve):
                assert ctx.semv["ve"] == ve_bd[0]
                ev = None
                for hcp in range(8):
                    b0, b1 = ctx.ev[(tag, "dnbank", hcp)]
                    c.wait("VE", ve, ctx.ev[(tag, "dnmm", hcp)])
                    for s_, bk in ((0, b0), (1, b1)):
                        hc = 2 * hcp + s_
                        xs = ap("x32")[:, hc * TC + tb * 512: hc * TC + (tb + 1) * 512]
                        ins = ve.scalar_tensor_tensor(
                            xs, pf[bk][:, :], ap("wsdpt")[:, hc:hc + 1], xs,
                            op0=ALU.mult, op1=ALU.add)
                        ev = c.inc(ins, "ve")
                ctx.ev[f"ffn_done{tag}"] = ev

            block.sync(sy_dn)
            block.scalar(se_dn)
            ve_bd[0] = ctx.semv["ve"]
            block.tensor(te_dn)
            block.vector(ve_dn)
        return ctx.ev[f"ffn_done{tag}"]

    def phase4(pre_ev, sub=99):
        dsem = "d3"

        def sy_rt(sy):
            c.inc(sy.dma_start(out=ap("ropec")[:, :], in_=ap("rb_cos")[:, :]),
                  "d1", 16)
            c.inc(sy.dma_start(out=ap("ropes")[:, :], in_=ap("rb_sin")[:, :]),
                  "d1", 16)
            ctx.ev["rope_rl"] = c.now("d1")

        block.sync(sy_rt)
        for tb in range(2):
            xn_ev = emit_ln(tb, False, ap("u1")[:, 8192:16384],
                            pre_ev=(pre_ev if tb == 0 else None))

            def rhs(k):
                return ap("u1")[:, 8192 + k * 512: 8192 + (k + 1) * 512]

            def q_slab(gi, k):
                return ap("wb_out")[gi * DM + k * 128: gi * DM + (k + 1) * 128, 0:256]

            se_b4 = ctx.semv["se"]
            emit_stream_mm(f"bq{tb}", dsem, 16, 2, q_slab, 256, rhs,
                           sy_pre=[ctx.ev["cc_wb"]], te_pre=xn_ev,
                           release_fn=lambda oc: ("se", se_b4 + oc + 1))
            items = [(h, ap("qbuf")[:, h * TC + tb * 512: h * TC + (tb + 1) * 512],
                      True) for h in range(16)]
            e1 = emit_rope_evac(f"bq{tb}", tb, items, ap("ropec"), ap("ropes"),
                                ve_pre=ctx.ev["rope_rl"])

            def kv_slab(colbase):
                def f(gi, k):
                    return (ap("wb_out")[(2 * gi) * DM + k * 128:
                                         (2 * gi) * DM + (k + 1) * 128,
                                         colbase: colbase + 64],
                            ap("wb_out")[(2 * gi + 1) * DM + k * 128:
                                         (2 * gi + 1) * DM + (k + 1) * 128,
                                         colbase: colbase + 64])
                return f

            se_b5 = ctx.semv["se"]
            emit_stream_mm(f"bk{tb}", dsem, 4, 1, kv_slab(256), 128, rhs,
                           dual_dma=True,
                           release_fn=lambda oc: ("se", se_b5 + oc + 1))
            items = [(kv, ap("u1")[:, kv * TC + tb * 512: kv * TC + (tb + 1) * 512],
                      True) for kv in range(4)]
            e2 = emit_rope_evac(f"bk{tb}", tb, items, ap("ropec"), ap("ropes"),
                                ve_pre=ctx.ev["rope_rl"])
            se_b6 = ctx.semv["se"]
            emit_stream_mm(f"bv{tb}", dsem, 4, 1, kv_slab(320), 128, rhs,
                           dual_dma=True,
                           release_fn=lambda oc: ("se", se_b6 + oc + 1))
            items = [(kv, ap("u1")[:, 4096 + kv * TC + tb * 512:
                                   4096 + kv * TC + (tb + 1) * 512], False)
                     for kv in range(4)]
            e3 = emit_rope_evac(f"bv{tb}", tb, items, ap("ropec"), ap("ropes"))
            ctx.ev[f"bqkv_ev{tb}"] = [e1, e2, e3]

        if sub < 2:
            return None

        ve_t0 = [None]

        def te_t(te):
            for e in ctx.ev["bqkv_ev0"] + ctx.ev["bqkv_ev1"]:
                c.wait("TE", te, e)
            i = 0
            for src_off in (0, 4096):
                for kv in range(4):
                    for ch2 in range(8):
                        if i >= 2:
                            c.wait("TE", te, ("ve", ve_t0[0] + i - 1))
                        ins = te.transpose(
                            ptp[:, (i % 2) * 128:(i % 2) * 128 + 128],
                            ap("u1")[:, src_off + kv * TC + ch2 * 128:
                                     src_off + kv * TC + (ch2 + 1) * 128],
                            ap("identt")[:, :])
                        ctx.ev[("bt_mm", i)] = c.inc(ins, "pe")
                        i += 1

        def ve_t(ve):
            assert ctx.semv["ve"] == ve_t0[0]
            i = 0
            for dst_off in (16384, 24576):
                for kv in range(4):
                    for ch2 in range(8):
                        c.wait("VE", ve, ctx.ev[("bt_mm", i)])
                        ins = ve.tensor_copy(
                            ap("u1")[:,
                                     dst_off + kv * 1024 + ch2 * 128:
                                     dst_off + kv * 1024 + (ch2 + 1) * 128],
                            ptp[:, (i % 2) * 128:(i % 2) * 128 + 128])
                        ctx.ev[("bt_ve", i)] = c.inc(ins, "ve")
                        i += 1
            ctx.ev["bt_all"] = c.now("ve")

        ve_t0[0] = ctx.semv["ve"]
        block.tensor(te_t)
        block.vector(ve_t)

        if sub < 3:
            return None

        def sy_ct(sy):
            c.wait("SY", sy, ctx.ev["bt_all"])
            for j in range(R):
                for hl in range(2):
                    h = 2 * j + hl
                    c.inc(sy.dma_start(
                        out=ap("qk2_in")[j * 384 + hl * 128: j * 384 + (hl + 1) * 128, :],
                        in_=ap("qbuf")[:, h * TC:(h + 1) * TC]), dsem, 16)
                kv = j // 2
                c.inc(sy.dma_start(
                    out=ap("qk2_in")[j * 384 + 256: j * 384 + 384, :],
                    in_=ap("u1")[:, kv * TC:(kv + 1) * TC]), dsem, 16)
                for ch2 in range(8):
                    c.inc(sy.dma_start(
                        out=ap("ktv_in")[j * TC + ch2 * 128: j * TC + (ch2 + 1) * 128, 0:128],
                        in_=ap("u1")[:, 16384 + kv * 1024 + ch2 * 128:
                                     16384 + kv * 1024 + (ch2 + 1) * 128]), dsem, 16)
                    c.inc(sy.dma_start(
                        out=ap("ktv_in")[j * TC + ch2 * 128: j * TC + (ch2 + 1) * 128, 128:256],
                        in_=ap("u1")[:, 24576 + kv * 1024 + ch2 * 128:
                                     24576 + kv * 1024 + (ch2 + 1) * 128]), dsem, 16)
            ctx.ev["qk2_in_done"] = c.now(dsem)

        block.sync(sy_ct)

        def gp_cc(gp):
            c.wait("GP", gp, ctx.ev["qk2_in_done"])
            with allow_ici_all_to_all():
                ctx.ev["cc_qk2"] = c.inc(gp.collective_compute(
                    "AllToAll", ALU.bypass, replica_groups=[list(range(R))],
                    ins=[cap("qk2_in")], outs=[cap("qk2_out")]), "cc")
                ctx.ev["cc_ktv"] = c.inc(gp.collective_compute(
                    "AllToAll", ALU.bypass, replica_groups=[list(range(R))],
                    ins=[cap("ktv_in")], outs=[cap("ktv_out")]), "cc")

        block.gpsimd(gp_cc)

        def sy_l2(sy):
            c.wait("SY", sy, ctx.ev["cc_qk2"])
            for hl in range(2):
                for r in range(R):
                    c.inc(sy.dma_start(
                        out=ap("qbuf")[:, hl * T + r * TC: hl * T + (r + 1) * TC],
                        in_=ap("qk2_out")[r * 384 + hl * 128:
                                          r * 384 + (hl + 1) * 128, :]), dsem, 16)
            for r in range(R):
                c.inc(sy.dma_start(
                    out=ap("u1")[:, r * TC:(r + 1) * TC],
                    in_=ap("qk2_out")[r * 384 + 256: r * 384 + 384, :]), dsem, 16)
            ctx.ev["b_ld"] = c.now(dsem)

        block.sync(sy_l2)

        if sub < 4:
            return None

        pe0 = ctx.semv["pe"]
        ve0 = ctx.semv["ve"]

        def ve_of_seg(sg):
            return ve0 + sg * 70

        def sy_seg(sy):
            c.wait("SY", sy, ctx.ev["cc_ktv"])
            for sg in range(8):
                if sg >= 2:
                    c.wait("SY", sy, ("pe", pe0 + 48 * (sg - 1)))
                    c.wait("SY", sy, ("ve", ve_of_seg(sg - 2) + 36))
                base = 8192 + (sg % 2) * 4096
                for ci2 in range(8):
                    c.inc(sy.dma_start(
                        out=ap("u1")[0:64, base + (2 * ci2) * 128:
                                     base + (2 * ci2 + 2) * 128]
                        .rearrange("p (a b) -> p a b", a=2),
                        in_=ap("ktv_out")[sg * TC + ci2 * 128:
                                          sg * TC + (ci2 + 1) * 128, 0:128]
                        .rearrange("(a j) d -> j a d", a=2)), f"sg{sg % 2}", 16)
                    c.inc(sy.dma_start(
                        out=ap("u1")[0:64, base + 2048 + (2 * ci2) * 128:
                                     base + 2048 + (2 * ci2 + 2) * 128]
                        .rearrange("p (a b) -> p a b", a=2),
                        in_=ap("ktv_out")[sg * TC + ci2 * 128:
                                          sg * TC + (ci2 + 1) * 128, 128:256]
                        .rearrange("(a j) d -> j a d", a=2)), f"sg{sg % 2}", 16)
                ctx.ev[("seg_ld", sg)] = c.now(f"sg{sg % 2}")

        block.sync(sy_seg)

        def ve_scan(ve):
            assert ctx.semv["ve"] == ve0
            c.wait("VE", ve, ctx.ev["pads_zero"])
            c.wait("VE", ve, ctx.ev["b_ld"])
            for sg in range(8):
                base = 8192 + (sg % 2) * 4096
                for half in range(2):
                    gh = sg * 2 + half
                    c.wait("VE", ve, ctx.ev[("seg_ld", sg)])
                    if gh >= 1:
                        c.wait("VE", ve, ("pe", pe0 + 3 * 8 * gh))
                    ins = None
                    for hl in range(2):
                        for cl in range(8):
                            ci = half * 8 + cl
                            ins = ve.tensor_scalar(
                                ap("ktmd")[0:64, (hl * 8 + cl) * 128:
                                           (hl * 8 + cl + 1) * 128],
                                ap("u1")[0:64, base + ci * 128: base + (ci + 1) * 128],
                                ap("kdect")[0:64, hl:hl + 1], None, op0=ALU.mult)
                    c.inc(ins, "ve")
                    for cl in range(8):
                        ci = half * 8 + cl
                        gc = sg * 16 + ci
                        for hl in range(2):
                            c.wait("VE", ve, ("pe", pe0 + 3 * gc + 2))
                            ve.tensor_tensor(
                                ap("smt")[0:64, hl * 64:(hl + 1) * 64],
                                pf[2 + gc % 2][0:64, hl * 64:(hl + 1) * 64],
                                ap("diagt")[:, hl * 64:(hl + 1) * 64], op=ALU.mult)
                            ins = ve.tensor_tensor(
                                ap("tmq")[:, hl * 64:(hl + 1) * 64],
                                ap("qbuf")[:, hl * T + gc * 64: hl * T + (gc + 1) * 64],
                                ap("qdect")[:, hl * 64:(hl + 1) * 64], op=ALU.mult)
                            c.inc(ins, "ve")
                        for hl in range(2):
                            c.wait("VE", ve, ("pe", pe0 + 3 * gc + 1))
                            kvf = ap("kvf0") if hl == 0 else ap("kvf1")
                            kvb = ap("kvb0") if hl == 0 else ap("kvb1")
                            ve.scalar_tensor_tensor(
                                kvf[:, :], kvf[:, :], ap("blkvt")[:, hl:hl + 1],
                                pf[gc % 2][:, hl * 128:(hl + 1) * 128],
                                op0=ALU.mult, op1=ALU.add)
                            ins = ve.tensor_copy(
                                kvb[:, ((gc + 1) % 2) * 128:((gc + 1) % 2) * 128 + 128],
                                kvf[:, :])
                            c.inc(ins, "ve")
                        if gc % 8 == 7:
                            for hl in range(2):
                                c.wait("VE", ve, ("pe", pe0 + 3 * gc + 3))
                                ins = ve.tensor_copy(
                                    ap("u1")[:, 16384 + hl * T + (gc - 7) * 64:
                                             16384 + hl * T + (gc + 1) * 64],
                                    pf[4 + hl][:, :])
                                c.inc(ins, "ve")
                assert ctx.semv["ve"] == ve_of_seg(sg + 1)
            ctx.ev["scan_done"] = c.now("ve")

        def te_scan(te):
            assert ctx.semv["pe"] == pe0
            c.wait("TE", te, ctx.ev["b_ld"])
            for i in range(7):
                c.wait("TE", te, ctx.ev_bank.get(i))
            for sg in range(8):
                base = 8192 + (sg % 2) * 4096
                for ci in range(16):
                    gc = sg * 16 + ci
                    half = ci // 8
                    gh = sg * 2 + half
                    cl = ci % 8
                    decv = ve_of_seg(sg) + half * 35 + 1
                    c.wait("TE", te, ctx.ev[("seg_ld", sg)])
                    c.wait("TE", te, ("ve", decv))
                    if gc >= 2:
                        c.wait("TE", te, ("ve", _kvb_val(sg, ci, ve0) - 0))
                    for hl in range(2):
                        ins = te.matmul(
                            pf[gc % 2][:, hl * 128:(hl + 1) * 128],
                            ap("ktmd")[:, (hl * 8 + cl) * 128:(hl * 8 + cl + 1) * 128],
                            ap("u1")[:, base + 2048 + ci * 128: base + (ci + 1) * 128 + 2048],
                            start=True, stop=True, skip_group_check=True)
                    c.inc(ins, "pe")
                    for hl in range(2):
                        ins = te.matmul(
                            pf[2 + gc % 2][0:64, hl * 64:(hl + 1) * 64],
                            ap("u1")[:, gc * 64:(gc + 1) * 64],
                            ap("qbuf")[:, hl * T + gc * 64: hl * T + (gc + 1) * 64],
                            start=True, stop=True, skip_group_check=True)
                    c.inc(ins, "pe")
                    slot = gc % 8
                    for hl in range(2):
                        c.wait("TE", te, ("ve", _smq_val(sg, ci, ve0)))
                        if gc > 0:
                            c.wait("TE", te, ("ve", _kvb_val2(sg, ci, ve0)))
                        if slot == 0 and gc >= 8:
                            c.wait("TE", te, ("ve", _oev_val(gc // 8 - 1, ve0)))
                        kvb = ap("kvb0") if hl == 0 else ap("kvb1")
                        te.matmul(pf[4 + hl][:, slot * 64:(slot + 1) * 64],
                                  ap("u1")[:, base + 2048 + ci * 128:
                                           base + 2048 + (ci + 1) * 128],
                                  ap("smt")[:, hl * 64:(hl + 1) * 64],
                                  start=True, stop=False, skip_group_check=True)
                        ins = te.matmul(pf[4 + hl][:, slot * 64:(slot + 1) * 64],
                                        kvb[:, (gc % 2) * 128:(gc % 2) * 128 + 128],
                                        ap("tmq")[:, hl * 64:(hl + 1) * 64],
                                        start=False, stop=True, skip_group_check=True)
                    c.inc(ins, "pe")
                    assert ctx.semv["pe"] == pe0 + 3 * gc + 3
            ctx.seal_banks(ctx.ev["scan_done"])

        def _smq_val(sg, ci, ve0_):
            half = ci // 8
            cl = ci % 8
            return ve_of_seg(sg) + half * 35 + 1 + cl * 4 + 2

        def _kvb_val(sg, ci, ve0_):
            gc = sg * 16 + ci - 2
            sg2, ci2 = divmod(gc, 16)
            half = ci2 // 8
            cl = ci2 % 8
            return ve_of_seg(sg2) + half * 35 + 1 + cl * 4 + 4

        def _kvb_val2(sg, ci, ve0_):
            gc = sg * 16 + ci - 1
            sg2, ci2 = divmod(gc, 16)
            half = ci2 // 8
            cl = ci2 % 8
            return ve_of_seg(sg2) + half * 35 + 1 + cl * 4 + 4

        def _oev_val(slotidx, ve0_):
            gc = slotidx * 8 + 7
            sg2, ci2 = divmod(gc, 16)
            half = ci2 // 8
            return ve_of_seg(sg2) + half * 35 + 1 + 8 * 4 + 2

        block.vector(ve_scan)
        block.tensor(te_scan)

        if sub < 5:
            return None

        def sy_o2(sy):
            c.wait("SY", sy, ctx.ev["scan_done"])
            for j in range(R):
                for hl in range(2):
                    c.inc(sy.dma_start(
                        out=ap("o2_in")[j * 256 + hl * 128: j * 256 + (hl + 1) * 128, :],
                        in_=ap("u1")[:, 16384 + hl * T + j * TC:
                                     16384 + hl * T + (j + 1) * TC]), dsem, 16)
            ctx.ev["o2_in_done"] = c.now(dsem)

        block.sync(sy_o2)

        def gp_cc2(gp):
            c.wait("GP", gp, ctx.ev["o2_in_done"])
            with allow_ici_all_to_all():
                ctx.ev["cc_o2"] = c.inc(gp.collective_compute(
                    "AllToAll", ALU.bypass, replica_groups=[list(range(R))],
                    ins=[cap("o2_in")], outs=[cap("o2_out")]), "cc")

        block.gpsimd(gp_cc2)

        def sy_l3(sy):
            c.wait("SY", sy, ctx.ev["cc_o2"])
            for h in range(16):
                c.inc(sy.dma_start(
                    out=ap("qbuf")[:, h * TC:(h + 1) * TC],
                    in_=ap("o2_out")[(h // 2) * 256 + (h % 2) * 128:
                                     (h // 2) * 256 + (h % 2) * 128 + 128, :]),
                    dsem, 16)
            ctx.ev["bo_ld"] = c.now(dsem)

        block.sync(sy_l3)

        for tb in range(2):
            def rhs(k, _tb=tb):
                return ap("qbuf")[:, k * TC + _tb * 512: k * TC + (_tb + 1) * 512]

            def wo_slab(gi, k):
                return ap("wb_out")[gi * DM + k * 128: gi * DM + (k + 1) * 128,
                                    384:640]

            ve_bw2 = ctx.semv["ve"]
            emit_stream_mm(f"bwo{tb}", dsem, 16, 2, wo_slab, 256, rhs,
                           sy_pre=[ctx.ev["bo_ld"]], te_pre=ctx.ev["bo_ld"],
                           release_fn=lambda oc, b=ve_bw2: ("ve", b + oc + 1))

            def ve_ev(ve, _tb=tb, _vb=ve_bw2):
                assert ctx.semv["ve"] == _vb
                ev = None
                for oc in range(16):
                    bk = ctx.ev[(f"bwo{_tb}", "bank", oc)]
                    c.wait("VE", ve, ctx.ev[(f"bwo{_tb}", "mm", oc)])
                    xs = ap("x32")[:, oc * TC + _tb * 512: oc * TC + (_tb + 1) * 512]
                    ins = ve.tensor_tensor(xs, xs, pf[bk][:, :], op=ALU.add)
                    ev = c.inc(ins, "ve")
                ctx.ev[f"res3_{_tb}"] = ev

            block.vector(ve_ev)

    def phase_store():
        def syb(sy):
            for s_ in list(ctx.semv):
                c.wait("SY", sy, (s_, ctx.semv[s_]))
            for ch in range(NCH):
                c.inc(sy.dma_start(
                    out=ap("out_x")[:, ch * TC:(ch + 1) * TC],
                    in_=ap("x32")[:, ch * TC:(ch + 1) * TC]), "d5", 16)
            c.wait("SY", sy, c.now("d5"))

        block.sync(syb)

    def phase_dbg():
        def gpd(gp):
            c.wait("GP", gp, ctx.ev["cc_ug"])
            c.wait("GP", gp, ctx.ev["cc_wd"])
            c.wait("GP", gp, ctx.ev["cc_wb"])
            c.inc(gp.dma_start(out=ap("dbg_ug")[:, :], in_=ap("ug_out")[:, :]), "g1", 16)
            c.inc(gp.dma_start(out=ap("dbg_wd")[:, :], in_=ap("wd_out")[:, :]), "g1", 16)
            c.inc(gp.dma_start(out=ap("dbg_wb")[:, :], in_=ap("wb_out")[:, :]), "g1", 16)
            c.inc(gp.dma_start(out=ap("dbg_wdi")[:, :], in_=ap("wdi")[:, :]), "g1", 16)
        block.gpsimd(gpd)

    phase1()
    fold_wd()
    phase2()
    if upto >= 2:
        phase_ffn(1, "d2", ctx.ev["o_in_done"])
    if 2 < upto < 3:
        phase4(None, sub=int(round((upto - 2) * 10)))
    if upto >= 3:
        phase4(None)
    if upto >= 4:
        phase_ffn(2, "d4", ctx.ev["o2_in_done"])
    if debug_wts:
        phase_dbg()
    phase_store()


def _hadamard(n):
    h = np.array([[1.0]], dtype=np.float64)
    while h.shape[0] < n:
        h = np.block([[h, h], [h, -h]])
    return h


def _rope_tables(pos, base):
    half = 64
    inv = base ** -(np.arange(half, dtype=np.float64) / half)
    ang = np.outer(inv, pos.astype(np.float64))
    cos = np.cos(ang)
    sin = np.sin(ang)
    return (np.concatenate([cos, cos], axis=0),
            np.concatenate([sin, sin], axis=0))


_NC_CACHE = {}


def _get_nc(upto, debug_wts=False):
    key = (upto, debug_wts)
    if key not in _NC_CACHE:
        _NC_CACHE[key] = build_program(upto, debug_wts)
    return _NC_CACHE[key]


def build_in_maps(L0_attn_a_x, L0_attn_a_Wq, L0_attn_a_bq, L0_attn_a_Wk,
                  L0_attn_a_bk, L0_attn_a_Wv, L0_attn_a_bv, L0_attn_a_Wo,
                  L0_attn_b_wq, L0_attn_b_wk, L0_attn_b_wv, L0_attn_b_wo,
                  L0_ffn_SU_ug, L0_ffn_SU_dp, L0_ffn_Wscale_ug,
                  L0_ffn_Wscale_dp, L0_ffn_W_up, L0_ffn_W_gate, L0_ffn_W_down):
    f32 = np.float32
    asn = np.ascontiguousarray
    x = np.asarray(L0_attn_a_x)
    Wq = np.asarray(L0_attn_a_Wq)
    Wk = np.asarray(L0_attn_a_Wk)
    Wv = np.asarray(L0_attn_a_Wv)
    Wo = np.asarray(L0_attn_a_Wo)
    bq = np.asarray(L0_attn_a_bq)
    bk = np.asarray(L0_attn_a_bk)
    bv = np.asarray(L0_attn_a_bv)
    wqT = asn(np.asarray(L0_attn_b_wq, f32).T)
    wkT = asn(np.asarray(L0_attn_b_wk, f32).T)
    wvT = asn(np.asarray(L0_attn_b_wv, f32).T)
    woT = asn(np.asarray(L0_attn_b_wo, f32).T)
    wupT = asn(np.asarray(L0_ffn_W_up, f32).T)
    wgT = asn(np.asarray(L0_ffn_W_gate, f32).T)
    wdT = asn(np.asarray(L0_ffn_W_down, f32).T)
    su_ug = np.asarray(L0_ffn_SU_ug, f32)
    su_dp = np.asarray(L0_ffn_SU_dp, f32)
    ws_ug = np.asarray(L0_ffn_Wscale_ug, f32)
    ws_dp = np.asarray(L0_ffn_Wscale_dp, f32)

    Hpm = _hadamard(2048).astype(bfdt)
    H64 = _hadamard(64).astype(bfdt)
    H128 = _hadamard(128).astype(bfdt)
    ident = np.eye(128, dtype=bfdt)
    ones = np.ones((128, 128), dtype=bfdt)
    Rm = np.zeros((128, 128), np.float64)
    for d in range(64):
        Rm[d, d + 64] = -1.0
        Rm[d + 64, d] = 1.0
    Rpm = np.ascontiguousarray(Rm.T).astype(bfdt)
    mask = (np.arange(128)[:, None] <= (np.arange(896)[None, :] - 384)).astype(bfdt)
    lam = np.exp2(-8.0 * np.arange(1, 17, dtype=np.float64) / 16.0)

    in_maps = []
    for cid in range(R):
        sl = slice(cid * TC, (cid + 1) * TC)
        pos = np.arange(cid * TC, (cid + 1) * TC)
        ca, sa = _rope_tables(pos, 1e6)
        cb, sb2 = _rope_tables(pos, 1e4)
        heads = [2 * cid, 2 * cid + 1]
        diagT = np.zeros((64, 128), dtype=f32)
        qdec = np.zeros((128, 128), dtype=f32)
        kdec = np.zeros((128, 2), dtype=f32)
        blkv = np.zeros((128, 2), dtype=f32)
        rr = np.arange(64, dtype=np.float64)
        for hi_, h in enumerate(heads):
            diff = rr[None, :] - rr[:, None]
            diagT[:, hi_ * 64:(hi_ + 1) * 64] = np.where(
                diff >= 0, np.exp(-lam[h] * diff), 0.0)
            qdec[:, hi_ * 64:(hi_ + 1) * 64] = np.exp(-lam[h] * (rr + 1.0))[None, :]
            kdec[0:64, hi_] = np.exp(-lam[h] * (63.0 - rr))
            kdec[64:128, hi_] = kdec[0:64, hi_]
            blkv[:, hi_] = math.exp(-lam[h] * 64.0)
        in_maps.append({
            "x_fm": asn(x[0, sl, :].T),
            "Wq": Wq, "Wk": Wk, "Wv": Wv, "Wo": Wo,
            "bqr": asn(bq.reshape(1, 2048)),
            "bkr": asn(bk.reshape(1, 512)),
            "bvr": asn(bv.reshape(1, 512)),
            "rpmc": Rpm,
            "wqT_sh": asn(wqT[:, cid * 256:(cid + 1) * 256]),
            "wkT_sh": asn(wkT[:, cid * 64:(cid + 1) * 64]),
            "wvT_sh": asn(wvT[:, cid * 64:(cid + 1) * 64]),
            "woT_sh": asn(woT[:, cid * 256:(cid + 1) * 256]),
            "wupT_sh": asn(wupT[:, cid * 1024:(cid + 1) * 1024]),
            "wgT_sh": asn(wgT[:, cid * 1024:(cid + 1) * 1024]),
            "wdT_sh": asn(wdT[:, cid * 256:(cid + 1) * 256]),
            "su_ug2": asn(su_ug.reshape(16, 128).T),
            "su_dp2": asn(su_dp.reshape(64, 128).T),
            "wsup2": asn(ws_ug[:INTER].reshape(64, 128).T),
            "wsg2": asn(ws_ug[INTER:].reshape(64, 128).T),
            "wsdp2": asn(ws_dp.reshape(16, 128).T),
            "Hpm": Hpm, "H64c": H64, "H128c": H128,
            "identc": ident, "onesc": ones, "maskcc": mask,
            "ra_cos": asn(ca.astype(bfdt)), "ra_sin": asn(sa.astype(bfdt)),
            "rb_cos": asn(cb.astype(bfdt)), "rb_sin": asn(sb2.astype(bfdt)),
            "diagTc": diagT, "qdecc": asn(qdec.astype(bfdt)),
            "kdecc": kdec, "blkvc": blkv,
        })
    return in_maps


def kernel(_upto=4, _debug_wts=False, **inputs):
    in_maps = build_in_maps(**inputs)
    nc = _get_nc(_upto, _debug_wts)
    res = run_bass_kernel_spmd(nc, in_maps, core_ids=list(range(R)))
    if _debug_wts:
        kernel._dbg = {k: np.asarray(res.results[0][k]) for k in ("dbg_ug", "dbg_wd", "dbg_wb", "dbg_wdi", "dbg_xh", "dbg_z1")}
        kernel._dbg_all = [{k: np.asarray(r[k]) for k in ("dbg_ug", "dbg_wd", "dbg_xh", "dbg_z1")} for r in res.results]
    out = np.empty((1, T, DM), np.float32)
    for cid in range(R):
        arr = np.asarray(res.results[cid]["out_x"])
        out[0, cid * TC:(cid + 1) * TC, :] = (
            arr.reshape(128, NCH, TC).transpose(2, 1, 0).reshape(TC, DM))
    return out


# version 05
# speedup vs baseline: 597.7938x; 1.1691x over previous
import math
from contextlib import ExitStack

import numpy as np
import ml_dtypes

import concourse.bass as bass
import concourse.mybir as mybir
from concourse.bass_utils import run_bass_kernel_spmd
from concourse.aot_env import allow_ici_all_to_all

BF16 = mybir.dt.bfloat16
F32 = mybir.dt.float32
AF = mybir.ActivationFunctionType
ALU = mybir.AluOpType
bfdt = ml_dtypes.bfloat16

R = 8
T = 8192
DM = 2048
TC = 1024
NCH = 16
INTER = 8192
EPS = 1e-5
SM_SCALE = 1.0 / math.sqrt(128.0)



class Ctx:
    def __init__(self):
        self.semh = {}
        self.semv = {}
        self.hw = {"TE": {}, "VE": {}, "SE": {}, "SY": {}, "GP": {}}
        self.ev = {}
        self.ev_bank = {}
        self.slab_ev = [None, None, None, None]
        self.slab_i = 0
        self.rope_i = 0
        self.rope_ev = {}
        self.dsi = 0

    def next_dsem(self):
        d = f"ds{self.dsi % 4}"
        self.dsi += 1
        return d

    def add_sem(self, name, handle):
        self.semh[name] = handle
        self.semv[name] = 0

    def inc(self, instr, s, n=1):
        self.semv[s] += n
        instr.then_inc(self.semh[s], n)
        return (s, self.semv[s])

    def wait(self, engname, eng, ev):
        if ev is None:
            return
        s, v = ev
        if v <= 0:
            return
        if self.hw[engname].get(s, 0) >= v:
            return
        eng.wait_ge(self.semh[s], v)
        self.hw[engname][s] = v

    def now(self, s):
        return (s, self.semv[s])

    def seal_banks(self, ev):
        for i in range(7):
            self.ev_bank[i] = ev


class BankPool:
    def __init__(self, ctx):
        self.ctx = ctx
        self.i = 0

    def acquire(self, te):
        b = self.i % 7
        self.i += 1
        self.ctx.wait("TE", te, self.ctx.ev_bank.get(b))
        return b

    def release(self, b, ev):
        self.ctx.ev_bank[b] = ev


def build_program(upto=4, debug_wts=False):
    nc = bass.Bass("TRN2", target_bir_lowering=False)
    D = {}

    def din(name, shape, dt):
        D[name] = nc.dram_tensor(name, shape, dt, kind="ExternalInput")

    din("x_fm", [DM, TC], BF16)
    din("Wq", [DM, DM], BF16)
    din("Wk", [DM, 512], BF16)
    din("Wv", [DM, 512], BF16)
    din("Wo", [DM, DM], BF16)
    din("bqr", [1, 2048], BF16)
    din("bkr", [1, 512], BF16)
    din("bvr", [1, 512], BF16)
    din("rpmc", [128, 128], BF16)
    din("wqT_sh", [DM, 256], F32)
    din("wkT_sh", [DM, 64], F32)
    din("wvT_sh", [DM, 64], F32)
    din("woT_sh", [DM, 256], F32)
    din("wupT_sh", [DM, 1024], F32)
    din("wgT_sh", [DM, 1024], F32)
    din("wdT_sh", [INTER, 256], F32)
    din("su_ug2", [128, 16], F32)
    din("su_dp2", [128, 64], F32)
    din("wsup2", [128, 64], F32)
    din("wsg2", [128, 64], F32)
    din("wsdp2", [128, 16], F32)
    din("Hpm", [DM, DM], BF16)
    din("H64c", [64, 64], BF16)
    din("H128c", [128, 128], BF16)
    din("identc", [128, 128], BF16)
    din("onesc", [128, 128], BF16)
    din("maskcc", [128, 896], BF16)
    din("ra_cos", [128, TC], BF16)
    din("ra_sin", [128, TC], BF16)
    din("rb_cos", [128, TC], BF16)
    din("rb_sin", [128, TC], BF16)
    din("diagTc", [64, 128], F32)
    din("qdecc", [128, 128], BF16)
    din("kdecc", [128, 2], F32)
    din("blkvc", [128, 2], F32)
    D["out_x"] = nc.dram_tensor("out_x", [128, NCH * TC], F32, kind="ExternalOutput")
    if debug_wts:
        D["dbg_ug"] = nc.dram_tensor("dbg_ug", [R * DM, 2048], BF16, kind="ExternalOutput")
        D["dbg_wd"] = nc.dram_tensor("dbg_wd", [R * INTER, 256], BF16, kind="ExternalOutput")
        D["dbg_wb"] = nc.dram_tensor("dbg_wb", [R * DM, 640], BF16, kind="ExternalOutput")
        D["dbg_wdi"] = nc.dram_tensor("dbg_wdi", [INTER, 256], BF16, kind="ExternalOutput")
        D["dbg_xh"] = nc.dram_tensor("dbg_xh", [128, NCH * TC], BF16, kind="ExternalOutput")
        D["dbg_z1"] = nc.dram_tensor("dbg_z1", [128, 2 * NCH * TC], BF16, kind="ExternalOutput")

    for name, shape, dt in [
        ("ug_in", [DM, 2048], BF16), ("ug_out", [R * DM, 2048], BF16),
        ("wb_in", [DM, 640], BF16), ("wb_out", [R * DM, 640], BF16),
        ("wdi", [INTER, 256], BF16), ("wd_in", [INTER, 256], BF16),
        ("wd_out", [R * INTER, 256], BF16),
        ("qk_in", [R * 384, TC], BF16), ("qk_out", [R * 384, TC], BF16),
        ("v_in", [R * TC, 128], BF16), ("v_out", [R * TC, 128], BF16),
        ("o_in", [R * 256, TC], BF16), ("o_out", [R * 256, TC], BF16),
        ("qk2_in", [R * 384, TC], BF16), ("qk2_out", [R * 384, TC], BF16),
        ("ktv_in", [R * TC, 256], BF16), ("ktv_out", [R * TC, 256], BF16),
        ("o2_in", [R * 256, TC], BF16), ("o2_out", [R * 256, TC], BF16),
    ]:
        D[name] = nc.dram_tensor(
            name, shape, dt,
            addr_space=("Shared" if name in ("ug_out", "wb_out", "wd_out")
                        else "Local"))

    with ExitStack() as st:
        def sbt(name, shape, dt):
            D[name] = st.enter_context(nc.sbuf_tensor(name, shape, dt))

        sbt("x32", [128, NCH * TC], F32)
        sbt("qbuf", [128, NCH * TC], BF16)
        sbt("u1", [128, 2 * NCH * TC], BF16)
        sbt("wstream", [128, 4 * 512], BF16)
        sbt("t1", [128, 512], F32)
        sbt("t2", [128, 512], F32)
        sbt("tg", [128, 512], F32)
        sbt("ptile", [128, 3 * 512], BF16)
        sbt("mrow", [1, 512], F32)
        sbt("vrow", [1, 512], F32)
        sbt("wrow", [1, 512], F32)
        sbt("mr_row", [1, 2 * 512], BF16)
        sbt("rlrow", [1, 512], F32)
        sbt("epsrow", [1, 1], F32)
        sbt("rlrow_bf", [1, 512], BF16)
        sbt("ropec", [128, TC], BF16)
        sbt("ropes", [128, TC], BF16)
        sbt("maskt", [128, 896], BF16)
        sbt("diagt", [64, 128], F32)
        sbt("qdect", [128, 128], BF16)
        sbt("kdect", [128, 2], F32)
        sbt("blkvt", [128, 2], F32)
        sbt("identt", [128, 128], BF16)
        sbt("onest", [128, 128], BF16)
        sbt("h64t", [64, 64], BF16)
        sbt("h128t", [128, 128], BF16)
        sbt("sut", [128, 16], F32)
        sbt("suwut", [128, 64], F32)
        sbt("wsupt", [128, 64], F32)
        sbt("wsgt", [128, 64], F32)
        sbt("wsdpt", [128, 16], F32)
        sbt("bqrt", [1, 2048], BF16)
        sbt("bkrt", [1, 512], BF16)
        sbt("bvrt", [1, 512], BF16)
        sbt("rpmt", [128, 128], BF16)
        sbt("kvf0", [128, 128], F32)
        sbt("kvf1", [128, 128], F32)
        sbt("kvb0", [128, 2 * 128], BF16)
        sbt("kvb1", [128, 2 * 128], BF16)
        sbt("smt", [128, 2 * 64], BF16)
        sbt("tmq", [128, 2 * 64], BF16)
        sbt("ktmd", [128, 2 * 8 * 128], BF16)
        sbt("fold_a", [64, 2 * 512], BF16)

        pf = [st.enter_context(nc.psum_tensor(f"pf{i}", [128, 512], F32))
              for i in range(7)]
        ptp = st.enter_context(nc.psum_tensor("ptp", [128, 1024], BF16))
        D["pf"] = pf
        D["ptp"] = ptp

        semnames = ["d0", "d1", "d2", "d3", "d4", "d5",
                    "ds0", "ds1", "ds2", "ds3", "fs",
                    "fa0", "fa1", "fa2", "fa3", "sa0", "sa1", "sa2", "sa3",
                    "sg0", "sg1", "g0", "g1", "pe", "ve", "se", "cc"]
        for n in semnames:
            D["sem_" + n] = st.enter_context(nc.semaphore(n))

        block = st.enter_context(nc.Block())
        ctx = Ctx()
        for n in semnames:
            ctx.add_sem(n, D["sem_" + n])

        _emit(ctx, block, D, upto, debug_wts)

    return nc


def _emit(ctx, block, D, upto, debug_wts=False):
    c = ctx
    pf = D["pf"]
    ptp = D["ptp"]
    rot = BankPool(ctx)

    def ap(name):
        return D[name]

    def cap(name):
        return D[name].ap().opt()

    def p0_sync(sy):
        def ld(dst, src):
            return c.inc(sy.dma_start(out=dst, in_=src), "d0", 16)

        ld(ap("identt")[:, :], ap("identc")[:, :])
        ld(ap("onest")[:, :], ap("onesc")[:, :])
        ld(ap("h64t")[:, :], ap("H64c")[:, :])
        ld(ap("h128t")[:, :], ap("H128c")[:, :])
        ld(ap("maskt")[:, :], ap("maskcc")[:, :])
        ld(ap("ropec")[:, :], ap("ra_cos")[:, :])
        ld(ap("ropes")[:, :], ap("ra_sin")[:, :])
        ld(ap("diagt")[:, :], ap("diagTc")[:, :])
        ld(ap("qdect")[:, :], ap("qdecc")[:, :])
        ld(ap("kdect")[:, :], ap("kdecc")[:, :])
        ld(ap("blkvt")[:, :], ap("blkvc")[:, :])
        ld(ap("sut")[:, :], ap("su_ug2")[:, :])
        ld(ap("suwut")[:, :], ap("su_dp2")[:, :])
        ld(ap("wsupt")[:, :], ap("wsup2")[:, :])
        ld(ap("wsgt")[:, :], ap("wsg2")[:, :])
        ld(ap("wsdpt")[:, :], ap("wsdp2")[:, :])
        ld(ap("bqrt")[:, :], ap("bqr")[:, :])
        ld(ap("bkrt")[:, :], ap("bkr")[:, :])
        ld(ap("bvrt")[:, :], ap("bvr")[:, :])
        ld(ap("rpmt")[:, :], ap("rpmc")[:, :])
        for ch in range(NCH):
            ld(ap("u1")[:, 8192 + ch * TC: 8192 + (ch + 1) * TC],
               ap("x_fm")[ch * 128:(ch + 1) * 128, :])
        ctx.ev["p0_loads"] = c.now("d0")

    block.sync(p0_sync)

    def p0_gp(gp):
        for lc in range(8):
            c.inc(gp.dma_start(out=ap("ug_in")[:, lc * 256: lc * 256 + 128],
                               in_=ap("wupT_sh")[:, lc * 128:(lc + 1) * 128]), "g0", 16)
            c.inc(gp.dma_start(out=ap("ug_in")[:, lc * 256 + 128: lc * 256 + 256],
                               in_=ap("wgT_sh")[:, lc * 128:(lc + 1) * 128]), "g0", 16)
        c.inc(gp.dma_start(out=ap("wb_in")[:, 0:256], in_=ap("wqT_sh")[:, :]), "g0", 16)
        c.inc(gp.dma_start(out=ap("wb_in")[:, 256:320], in_=ap("wkT_sh")[:, :]), "g0", 16)
        c.inc(gp.dma_start(out=ap("wb_in")[:, 320:384], in_=ap("wvT_sh")[:, :]), "g0", 16)
        c.inc(gp.dma_start(out=ap("wb_in")[:, 384:640], in_=ap("woT_sh")[:, :]), "g0", 16)
        ctx.ev["ugwb_in"] = c.now("g0")

    block.gpsimd(p0_gp)

    def p0_se(se):
        c.wait("SE", se, ctx.ev["p0_loads"])
        ins = None
        for ch in range(NCH):
            ins = se.copy(ap("x32")[:, ch * TC:(ch + 1) * TC],
                          ap("u1")[:, 8192 + ch * TC: 8192 + (ch + 1) * TC])
        ctx.ev["x32_ready"] = c.inc(ins, "se")

    block.scalar(p0_se)

    def p0_ve(ve):
        c.wait("VE", ve, ctx.ev["p0_loads"])
        ve.tensor_scalar(ap("wsupt")[:, :], ap("wsupt")[:, :],
                         1.0 / math.sqrt(2048.0), None, op0=ALU.mult)
        ve.tensor_scalar(ap("wsgt")[:, :], ap("wsgt")[:, :],
                         1.0 / math.sqrt(2048.0), None, op0=ALU.mult)
        ve.tensor_scalar(ap("wsdpt")[:, :], ap("wsdpt")[:, :],
                         1.0 / math.sqrt(8192.0), None, op0=ALU.mult)
        ins = ve.tensor_tensor(ap("suwut")[:, :], ap("suwut")[:, :],
                               ap("wsupt")[:, :], op=ALU.mult)
        ctx.ev["tabs_ready"] = c.inc(ins, "ve")
        ve.memset(ap("epsrow")[:, :], EPS)
        ve.memset(ap("smt")[:, :], 0)
        ve.memset(ap("ktmd")[:, :], 0)
        ve.memset(ap("kvb0")[:, :], 0)
        ve.memset(ap("kvb1")[:, :], 0)
        ve.memset(ap("kvf0")[:, :], 0)
        ins = ve.memset(ap("kvf1")[:, :], 0)
        ctx.ev["pads_zero"] = c.inc(ins, "ve")

    block.vector(p0_ve)

    def slab_slot():
        s = ctx.slab_i % 4
        ctx.slab_i += 1
        return s, ctx.slab_ev[s]

    def emit_ln(tb, su, xn_dst, pre_ev=None):
        x32 = ap("x32")

        def ve1(ve):
            c.wait("VE", ve, ctx.ev["x32_ready"])
            c.wait("VE", ve, pre_ev)
            ins = None
            for ch in range(NCH):
                xs = x32[:, ch * TC + tb * 512: ch * TC + (tb + 1) * 512]
                ve.tensor_copy(ap("u1")[:, 16384 + ch * 512: 16384 + (ch + 1) * 512], xs)
                ins = ve.tensor_tensor(
                    ap("u1")[:, 24576 + ch * 512: 24576 + (ch + 1) * 512],
                    xs, xs, op=ALU.mult)
            ctx.ev["ln_pre"] = c.inc(ins, "ve")

        def te1(te):
            bs = rot.acquire(te)
            bq_ = rot.acquire(te)
            c.wait("TE", te, ctx.ev["ln_pre"])
            for ch in range(NCH):
                te.matmul(pf[bs][0:1, :], ap("onest")[:, 0:1],
                          ap("u1")[:, 16384 + ch * 512: 16384 + (ch + 1) * 512],
                          start=(ch == 0), stop=(ch == NCH - 1))
            for ch in range(NCH):
                ins = te.matmul(pf[bq_][0:1, :], ap("onest")[:, 0:1],
                                ap("u1")[:, 24576 + ch * 512: 24576 + (ch + 1) * 512],
                                start=(ch == 0), stop=(ch == NCH - 1))
            ctx.ev["ln_sums"] = c.inc(ins, "pe")
            ctx.ev["ln_banks"] = (bs, bq_)

        def ve2(ve):
            bs, bq_ = ctx.ev["ln_banks"]
            c.wait("VE", ve, ctx.ev["ln_sums"])
            ve.tensor_scalar(ap("mrow")[:, :], pf[bs][0:1, :], 1.0 / DM, None,
                             op0=ALU.mult)
            ve.tensor_scalar(ap("vrow")[:, :], pf[bq_][0:1, :], 1.0 / DM, None,
                             op0=ALU.mult)
            ve.tensor_tensor(ap("wrow")[:, :], ap("mrow")[:, :], ap("mrow")[:, :],
                             op=ALU.mult)
            ve.tensor_copy(ap("mr_row")[:, 0:512], ap("mrow")[:, :])
            ins = ve.tensor_tensor(ap("vrow")[:, :], ap("vrow")[:, :],
                                   ap("wrow")[:, :], op=ALU.subtract)
            ctx.ev["ln_var"] = c.inc(ins, "ve")
            rot.release(bs, ctx.ev["ln_var"])
            rot.release(bq_, ctx.ev["ln_var"])

        def se1(se):
            c.wait("SE", se, ctx.ev["ln_var"])
            se.copy(ap("wrow")[:, :], ap("mrow")[:, :])
            se.activation(ap("wrow")[:, 0:256], ap("vrow")[:, 0:256], AF.Ln,
                          bias=ap("epsrow")[:, :])
            se.activation(ap("wrow")[:, 256:512], ap("vrow")[:, 256:512], AF.Ln,
                          bias=ap("epsrow")[:, :])
            se.activation(ap("wrow")[:, 0:256], ap("wrow")[:, 0:256], AF.Exp,
                          scale=-0.5)
            ins = se.activation(ap("wrow")[:, 256:512], ap("wrow")[:, 256:512],
                                AF.Exp, scale=-0.5)
            ctx.ev["ln_rstd"] = c.inc(ins, "se")

        def ve3(ve):
            c.wait("VE", ve, ctx.ev["ln_rstd"])
            ve.tensor_copy(ap("mr_row")[:, 0:512], ap("mrow")[:, :])
            ins = ve.tensor_copy(ap("mr_row")[:, 512:1024], ap("wrow")[:, :])
            ctx.ev["ln_mr"] = c.inc(ins, "ve")

        def te2(te):
            b1 = rot.acquire(te)
            b2 = rot.acquire(te)
            c.wait("TE", te, ctx.ev["ln_mr"])
            te.matmul(pf[b1][:, :], ap("onest")[0:1, :], ap("mr_row")[:, 0:512],
                      start=True, stop=True)
            ins = te.matmul(pf[b2][:, :], ap("onest")[0:1, :], ap("mr_row")[:, 512:1024],
                            start=True, stop=True)
            ctx.ev["ln_bc"] = c.inc(ins, "pe")
            ctx.ev["ln_bcb"] = (b1, b2)

        def ve4(ve):
            b1, b2 = ctx.ev["ln_bcb"]
            c.wait("VE", ve, ctx.ev["ln_bc"])
            ve.tensor_copy(ap("ptile")[:, 0:512], pf[b1][:, :])
            ins = ve.tensor_copy(ap("ptile")[:, 512:1024], pf[b2][:, :])
            ev = c.inc(ins, "ve")
            rot.release(b1, ev)
            rot.release(b2, ev)
            pend = []
            for ch in range(NCH):
                xs = x32[:, ch * TC + tb * 512: ch * TC + (tb + 1) * 512]
                tmp = ap("t1") if ch % 2 == 0 else ap("t2")
                ve.tensor_tensor(tmp[:, :], xs, ap("ptile")[:, 0:512],
                                 op=ALU.subtract)
                if pend:
                    ch0, t0 = pend.pop()
                    ve.scalar_tensor_tensor(
                        xn_dst[:, ch0 * 512:(ch0 + 1) * 512], t0[:, :],
                        (ap("sut")[:, ch0:ch0 + 1] if su else 1.0),
                        ap("ptile")[:, 512:1024], op0=ALU.mult, op1=ALU.mult)
                pend.append((ch, tmp))
            ch0, t0 = pend.pop()
            ins = ve.scalar_tensor_tensor(
                xn_dst[:, ch0 * 512:(ch0 + 1) * 512], t0[:, :],
                (ap("sut")[:, ch0:ch0 + 1] if su else 1.0),
                ap("ptile")[:, 512:1024], op0=ALU.mult, op1=ALU.mult)
            ctx.ev["xn_ready"] = c.inc(ins, "ve")

        block.vector(ve1)
        block.tensor(te1)
        block.vector(ve2)
        block.scalar(se1)
        block.vector(ve3)
        block.tensor(te2)
        block.vector(ve4)
        return ctx.ev["xn_ready"]

    QROT = 24576

    def emit_rope_evac(name, tb, items, tabc, tabs, ve_pre=None):
        ve_base = ctx.semv["ve"]
        se_base = ctx.semv["se"]
        n_rope = sum(1 for it in items if it[2])
        rbanks = {}
        slots = []
        slot_wait = []
        for ri in range(n_rope):
            sl = ctx.rope_i % 16
            ctx.rope_i += 1
            slots.append(sl)
            slot_wait.append(ctx.rope_ev.get(sl))
            ctx.rope_ev[sl] = ("ve", ve_base + ri + 1)

        def seb(se):
            assert ctx.semv["se"] == se_base
            ri = 0
            for oc, dst, do_rope in items:
                bk = ctx.ev[(name, "bank", oc)]
                c.wait("SE", se, ctx.ev[(name, "mm", oc)])
                if do_rope:
                    c.wait("SE", se, slot_wait[ri])
                    sl = slots[ri]
                    ins = se.copy(
                        ap("u1")[:, QROT + sl * 512: QROT + (sl + 1) * 512],
                        pf[bk][:, :])
                    ctx.ev[(name, "qb", ri)] = c.inc(ins, "se")
                    ri += 1
                else:
                    ins = se.copy(dst, pf[bk][:, :])
                    ev = c.inc(ins, "se")
                    ctx.ev[(name, "evdone")] = ev

        def teb(te):
            ri = 0
            for oc, dst, do_rope in items:
                if not do_rope:
                    continue
                rb = rot.acquire(te)
                c.wait("TE", te, ctx.ev[(name, "qb", ri)])
                sl = slots[ri]
                ins = te.matmul(
                    pf[rb][:, :], ap("rpmt")[:, :],
                    ap("u1")[:, QROT + sl * 512: QROT + (sl + 1) * 512],
                    start=True, stop=True)
                c.inc(ins, "pe")
                ctx.ev[(name, "rot", ri)] = c.now("pe")
                rbanks[ri] = rb
                rot.release(rb, ("ve", ve_base + ri + 1))
                ri += 1

        def veb(ve):
            assert ctx.semv["ve"] == ve_base
            c.wait("VE", ve, ve_pre)
            cos = tabc[:, tb * 512:(tb + 1) * 512]
            sin = tabs[:, tb * 512:(tb + 1) * 512]
            ritems = [(oc, dst) for oc, dst, dr in items if dr]
            pend = []
            for ri, (oc, dst) in enumerate(ritems):
                rb = rbanks[ri]
                sl = slots[ri]
                c.wait("VE", ve, ctx.ev[(name, "rot", ri)])
                slap = ap("u1")[:, QROT + sl * 512: QROT + (sl + 1) * 512]
                tmp = ap("t2") if ri % 2 == 0 else ap("tg")
                ve.tensor_tensor(dst, slap, cos, op=ALU.mult)
                ve.tensor_tensor(tmp[:, :], pf[rb][:, :], sin, op=ALU.mult)
                if pend:
                    d0, t0 = pend.pop()
                    c.inc(ve.tensor_tensor(d0, d0, t0[:, :], op=ALU.add), "ve")
                pend.append((dst, tmp))
            d0, t0 = pend.pop()
            c.inc(ve.tensor_tensor(d0, d0, t0[:, :], op=ALU.add), "ve")
            assert ctx.semv["ve"] == ve_base + n_rope
            ctx.ev[(name, "evdone")] = c.now("ve")

        block.scalar(seb)
        block.tensor(teb)
        if n_rope:
            block.vector(veb)
        return ctx.ev[(name, "evdone")]

    def emit_stream_mm(name, dsem, n_oc, grp, slab_fn, slab_w, rhs_fn,
                       sy_pre=None, te_pre=None, nK=NCH, dual_dma=False,
                       bias_fn=None, release_fn=None):
        dsem = ctx.next_dsem()
        ngrp = (n_oc + grp - 1) // grp
        pe_base = ctx.semv["pe"]
        n_slab = ngrp * nK
        plan = []
        si = 0
        for gi in range(ngrp):
            for k in range(nK):
                plan.append((gi, k, si))
                si += 1

        def syb(sy):
            for s_, v_ in ([] if sy_pre is None else sy_pre):
                c.wait("SY", sy, (s_, v_))
            for gi, k, i in plan:
                s, wev = slab_slot()
                c.wait("SY", sy, wev)
                src = slab_fn(gi, k)
                if dual_dma:
                    i1 = sy.dma_start(
                        out=ap("wstream")[:, s * 512: s * 512 + slab_w // 2],
                        in_=src[0])
                    c.inc(i1, dsem, 16)
                    ins = sy.dma_start(
                        out=ap("wstream")[:, s * 512 + slab_w // 2: s * 512 + slab_w],
                        in_=src[1])
                    ev = c.inc(ins, dsem, 16)
                else:
                    ins = sy.dma_start(out=ap("wstream")[:, s * 512: s * 512 + slab_w],
                                       in_=src)
                    ev = c.inc(ins, dsem, 16)
                ctx.ev[(name, "slab", i)] = (ev, s)
                ctx.slab_ev[s] = ("pe", pe_base + i + 1)
            gi, k, i = plan[-1]
            _, s = ctx.ev[(name, "slab", i)]
            src = slab_fn(gi, k)
            if dual_dma:
                src = src[1]
                ins = sy.dma_start(
                    out=ap("wstream")[:, s * 512 + slab_w // 2: s * 512 + slab_w],
                    in_=src)
            else:
                ins = sy.dma_start(out=ap("wstream")[:, s * 512: s * 512 + slab_w],
                                   in_=src)
            ctx.ev[(name, "slab_tail")] = c.inc(ins, dsem, 16)

        def teb(te):
            assert ctx.semv["pe"] == pe_base, (name, ctx.semv["pe"], pe_base)
            if te_pre is not None:
                c.wait("TE", te, te_pre)
            for gi in range(ngrp):
                ocs = [gi * grp + j for j in range(grp) if gi * grp + j < n_oc]
                banks = [rot.acquire(te) for _ in ocs]
                for k in range(nK):
                    i = gi * nK + k
                    ev, s = ctx.ev[(name, "slab", i)]
                    if i + 1 < n_slab:
                        ev = ctx.ev[(name, "slab", i + 1)][0]
                    else:
                        ev = ctx.ev[(name, "slab_tail")]
                    c.wait("TE", te, ev)
                    last = (k == nK - 1) and bias_fn is None
                    for j, oc in enumerate(ocs):
                        ins = te.matmul(
                            pf[banks[j]][:, :],
                            ap("wstream")[:, s * 512 + j * 128: s * 512 + (j + 1) * 128],
                            rhs_fn(k), start=(k == 0), stop=last)
                    if k < nK - 1 or bias_fn is None:
                        c.inc(ins, "pe")
                if bias_fn is not None:
                    for j, oc in enumerate(ocs):
                        ins = te.matmul(pf[banks[j]][:, :], bias_fn(oc),
                                        ap("maskt")[0:1, 384:896],
                                        start=False, stop=True)
                    c.inc(ins, "pe")
                for j, oc in enumerate(ocs):
                    ctx.ev[(name, "mm", oc)] = c.now("pe")
                    ctx.ev[(name, "bank", oc)] = banks[j]
                    if release_fn is not None:
                        rot.release(banks[j], release_fn(oc))
            assert ctx.semv["pe"] == pe_base + n_slab

        block.sync(syb)
        block.tensor(teb)


    def phase1():
        for tb in range(2):
            xn_ev = emit_ln(tb, False, ap("u1")[:, 8192:16384])

            def rhs(k):
                return ap("u1")[:, 8192 + k * 512: 8192 + (k + 1) * 512]

            se_b = ctx.semv["se"]
            emit_stream_mm(f"qp{tb}", "d1", 16, 4,
                           lambda gi, k: ap("Wq")[k * 128:(k + 1) * 128,
                                                  gi * 512:(gi + 1) * 512],
                           512, rhs, te_pre=xn_ev,
                           bias_fn=lambda oc: ap("bqrt")[0:1, oc * 128:(oc + 1) * 128],
                           release_fn=lambda oc: ("se", se_b + oc + 1))
            items = [(h, ap("qbuf")[:, h * TC + tb * 512: h * TC + (tb + 1) * 512],
                      True) for h in range(16)]
            e1 = emit_rope_evac(f"qp{tb}", tb, items, ap("ropec"), ap("ropes"))
            se_b2 = ctx.semv["se"]
            emit_stream_mm(f"kp{tb}", "d1", 4, 4,
                           lambda gi, k: ap("Wk")[k * 128:(k + 1) * 128, 0:512],
                           512, rhs,
                           bias_fn=lambda oc: ap("bkrt")[0:1, oc * 128:(oc + 1) * 128],
                           release_fn=lambda oc: ("se", se_b2 + oc + 1))
            items = [(kv, ap("u1")[:, kv * TC + tb * 512: kv * TC + (tb + 1) * 512],
                      True) for kv in range(4)]
            e2 = emit_rope_evac(f"kp{tb}", tb, items, ap("ropec"), ap("ropes"))
            se_b3 = ctx.semv["se"]
            emit_stream_mm(f"vp{tb}", "d1", 4, 4,
                           lambda gi, k: ap("Wv")[k * 128:(k + 1) * 128, 0:512],
                           512, rhs,
                           bias_fn=lambda oc: ap("bvrt")[0:1, oc * 128:(oc + 1) * 128],
                           release_fn=lambda oc: ("se", se_b3 + oc + 1))
            items = [(kv, ap("u1")[:, 4096 + kv * TC + tb * 512:
                                   4096 + kv * TC + (tb + 1) * 512], False)
                     for kv in range(4)]
            e3 = emit_rope_evac(f"vp{tb}", tb, items, ap("ropec"), ap("ropes"))
            ctx.ev[f"qkv_ev{tb}"] = [e1, e2, e3]

        se_vt0 = ctx.semv["se"]
        d1_vt0 = ctx.semv["d1"]

        def te_vt(te):
            for e in ctx.ev["qkv_ev0"] + ctx.ev["qkv_ev1"]:
                c.wait("TE", te, e)
            for i in range(32):
                kv, ti = divmod(i, 8)
                if i >= 2:
                    c.wait("TE", te, ("se", se_vt0 + i - 1))
                ins = te.transpose(ptp[:, (i % 2) * 128:(i % 2) * 128 + 128],
                                   ap("u1")[:, 4096 + kv * TC + ti * 128:
                                            4096 + kv * TC + (ti + 1) * 128],
                                   ap("identt")[:, :])
                ctx.ev[("vtmm", i)] = c.inc(ins, "pe")

        def se_vt(se):
            assert ctx.semv["se"] == se_vt0
            for i in range(32):
                c.wait("SE", se, ctx.ev[("vtmm", i)])
                if i >= 2:
                    c.wait("SE", se, ("d1", d1_vt0 + 32 * (i - 1)))
                ins = se.copy(ap("u1")[:, 16384 + (i % 2) * 128: 16384 + (i % 2) * 128 + 128],
                              ptp[:, (i % 2) * 128:(i % 2) * 128 + 128])
                ctx.ev[("vtse", i)] = c.inc(ins, "se")

        def sy_vt(sy):
            assert ctx.semv["d1"] == d1_vt0
            for i in range(32):
                kv, ti = divmod(i, 8)
                c.wait("SY", sy, ctx.ev[("vtse", i)])
                for j in (2 * kv, 2 * kv + 1):
                    ins = sy.dma_start(
                        out=ap("v_in")[j * TC + ti * 128: j * TC + (ti + 1) * 128, :],
                        in_=ap("u1")[:, 16384 + (i % 2) * 128: 16384 + (i % 2) * 128 + 128])
                    c.inc(ins, "d1", 16)
            ctx.ev["v_in_done"] = c.now("d1")
            for e in ctx.ev["qkv_ev0"] + ctx.ev["qkv_ev1"]:
                c.wait("SY", sy, e)
            for j in range(R):
                for hl in range(2):
                    h = 2 * j + hl
                    c.inc(sy.dma_start(
                        out=ap("qk_in")[j * 384 + hl * 128: j * 384 + (hl + 1) * 128, :],
                        in_=ap("qbuf")[:, h * TC:(h + 1) * TC]), "d0", 16)
                kv = j // 2
                c.inc(sy.dma_start(
                    out=ap("qk_in")[j * 384 + 256: j * 384 + 384, :],
                    in_=ap("u1")[:, kv * TC:(kv + 1) * TC]), "d0", 16)
            ctx.ev["qk_in_done"] = c.now("d0")

        block.tensor(te_vt)
        block.scalar(se_vt)
        block.sync(sy_vt)

        def gp_cc(gp):
            c.wait("GP", gp, ctx.ev["qk_in_done"])
            with allow_ici_all_to_all():
                ctx.ev["cc_qk"] = c.inc(gp.collective_compute(
                    "AllToAll", ALU.bypass, replica_groups=[list(range(R))],
                    ins=[cap("qk_in")], outs=[cap("qk_out")]), "cc")
                c.wait("GP", gp, ctx.ev["v_in_done"])
                ctx.ev["cc_v"] = c.inc(gp.collective_compute(
                    "AllToAll", ALU.bypass, replica_groups=[list(range(R))],
                    ins=[cap("v_in")], outs=[cap("v_out")]), "cc")

        block.gpsimd(gp_cc)

    def fold_wd():
        pe_base = ctx.semv["pe"]
        se_base = ctx.semv["se"]
        d0_base = ctx.semv["d0"]

        ve_cast0 = [None]
        fsA = [None]

        def sy_lda(sy):
            wdr = ap("wdT_sh").rearrange("(a r) h -> a (r h)", a=64)
            for bg in range(64):
                if bg >= 2:
                    c.wait("SY", sy, ("ve", ve_cast0[0] + bg - 1))
                stage = ap("t2") if bg % 2 == 0 else ap("tg")
                c.inc(sy.dma_start(
                    out=stage[0:64, :],
                    in_=wdr[:, bg * 512:(bg + 1) * 512]), f"fa{bg % 4}", 16)

        def ve_lda(ve):
            assert ctx.semv["ve"] == ve_cast0[0]
            for bg in range(64):
                c.wait("VE", ve, (f"fa{bg % 4}", 16 * (bg // 4 + 1)))
                if bg >= 2:
                    c.wait("VE", ve, ("pe", pe_base + 2 * (bg - 2) + 2))
                stage = ap("t2") if bg % 2 == 0 else ap("tg")
                ins = ve.tensor_copy(
                    ap("fold_a")[:, (bg % 2) * 512:(bg % 2) * 512 + 512],
                    stage[0:64, :])
                ctx.ev[("fa_cast", bg)] = c.inc(ins, "ve")

        ve_cast0[0] = ctx.semv["ve"]
        block.sync(sy_lda)
        block.vector(ve_lda)

        def teA(te):
            assert ctx.semv["pe"] == pe_base
            for b_ in range(2):
                c.wait("TE", te, ctx.ev_bank.get(b_))
            for i in range(128):
                bg = i // 2
                c.wait("TE", te, ctx.ev[("fa_cast", bg)])
                if i >= 2:
                    c.wait("TE", te, ("se", se_base + i - 1))
                ins = te.matmul(pf[i % 2][0:64, 0:256], ap("h64t")[:, :],
                                ap("fold_a")[:, (bg % 2) * 512 + (i % 2) * 256:
                                             (bg % 2) * 512 + (i % 2) * 256 + 256],
                                start=True, stop=True)
                c.inc(ins, "pe")
            assert ctx.semv["pe"] == pe_base + 128

        def seA(se):
            assert ctx.semv["se"] == se_base
            for i in range(128):
                c.wait("SE", se, ("pe", pe_base + i + 1))
                if i >= 2:
                    c.wait("SE", se, (f"sa{(i - 2) % 4}", ((i - 2) // 4 + 1) * 16))
                ins = se.copy(ap("ptile")[0:64, 1024 + (i % 2) * 256: 1024 + (i % 2) * 256 + 256],
                              pf[i % 2][0:64, 0:256])
                c.inc(ins, "se")
            assert ctx.semv["se"] == se_base + 128

        def stA(gp):
            wdiv = ap("wdi").rearrange("(a b) h -> b a h", a=64, b=128)
            for i in range(128):
                c.wait("GP", gp, ("se", se_base + i + 1))
                c.inc(gp.dma_start(out=wdiv[i],
                                   in_=ap("ptile")[0:64, 1024 + (i % 2) * 256: 1024 + (i % 2) * 256 + 256]),
                      f"sa{i % 4}", 16)
            ctx.ev["wdi_done"] = [(f"sa{j}", ctx.semv[f"sa{j}"]) for j in range(4)]

        block.tensor(teA)
        block.scalar(seA)
        block.gpsimd(stA)

        pe_b = ctx.semv["pe"]
        ve_b = ctx.semv["ve"]
        d0_b = ctx.semv["d0"]

        def syB(sy):
            assert ctx.semv["d0"] == d0_b
            for e in ctx.ev["wdi_done"]:
                c.wait("SY", sy, e)
            c.wait("SY", sy, ctx.ev["qk_in_done"])
            for a in range(64):
                if a >= 2:
                    c.wait("SY", sy, ("pe", pe_b + a - 1))
                c.inc(sy.dma_start(
                    out=ap("qbuf")[:, (a % 2) * 256:(a % 2) * 256 + 256],
                    in_=ap("wdi")[a * 128:(a + 1) * 128, :]), "d0", 16)
            ctx.ev["foldb_ld"] = c.now("d0")

        def teB(te):
            assert ctx.semv["pe"] == pe_b
            for b_ in (2, 3):
                c.wait("TE", te, ctx.ev_bank.get(b_))
            for a in range(64):
                c.wait("TE", te, ("d0", d0_b + 16 * (a + 1)))
                if a >= 2:
                    c.wait("TE", te, ("ve", ve_b + a - 1))
                ins = te.matmul(pf[2 + a % 2][:, 0:256], ap("h128t")[:, :],
                                ap("qbuf")[:, (a % 2) * 256:(a % 2) * 256 + 256],
                                start=True, stop=True)
                c.inc(ins, "pe")
            assert ctx.semv["pe"] == pe_b + 64

        g0B = [None]

        def veB(ve):
            assert ctx.semv["ve"] == ve_b
            c.wait("VE", ve, ctx.ev["tabs_ready"])
            for a in range(64):
                c.wait("VE", ve, ("pe", pe_b + a + 1))
                if a >= 2:
                    c.wait("VE", ve, ("fs", g0B[0] + (a - 1) * 16))
                ins = ve.tensor_scalar(
                    ap("ptile")[:, 1024 + (a % 2) * 256: 1024 + (a % 2) * 256 + 256],
                    pf[2 + a % 2][:, 0:256],
                    ap("suwut")[:, a:a + 1], None, op0=ALU.mult)
                c.inc(ins, "ve")
            assert ctx.semv["ve"] == ve_b + 64

        def stB(gp):
            assert ctx.semv["fs"] == g0B[0]
            for a in range(64):
                c.wait("GP", gp, ("ve", ve_b + a + 1))
                c.inc(gp.dma_start(
                    out=ap("wd_in")[a * 128:(a + 1) * 128, :],
                    in_=ap("ptile")[:, 1024 + (a % 2) * 256: 1024 + (a % 2) * 256 + 256]),
                    "fs", 16)
            ctx.ev["wd_in_done"] = c.now("fs")

        block.sync(syB)
        block.tensor(teB)
        ctx.ev["foldB_pe_end"] = c.now("pe")
        g0B[0] = ctx.semv["fs"]
        block.vector(veB)
        block.gpsimd(stB)
        ctx.seal_banks(c.now("ve"))

        def gp_cc(gp):
            c.wait("GP", gp, ctx.ev["ugwb_in"])
            with allow_ici_all_to_all():
                ctx.ev["cc_ug"] = c.inc(gp.collective_compute(
                    "AllGather", ALU.bypass, replica_groups=[list(range(R))],
                    ins=[cap("ug_in")], outs=[cap("ug_out")]), "cc")
                ctx.ev["cc_wb"] = c.inc(gp.collective_compute(
                    "AllGather", ALU.bypass, replica_groups=[list(range(R))],
                    ins=[cap("wb_in")], outs=[cap("wb_out")]), "cc")
                c.wait("GP", gp, ctx.ev["wd_in_done"])
                ctx.ev["cc_wd"] = c.inc(gp.collective_compute(
                    "AllGather", ALU.bypass, replica_groups=[list(range(R))],
                    ins=[cap("wd_in")], outs=[cap("wd_out")]), "cc")

        block.gpsimd(gp_cc)

    def phase2():
        def sy_ld(sy):
            c.wait("SY", sy, ctx.ev["cc_qk"])
            c.wait("SY", sy, ctx.ev["foldB_pe_end"])
            for hl in range(2):
                for r in range(R):
                    c.inc(sy.dma_start(
                        out=ap("qbuf")[:, hl * T + r * TC: hl * T + (r + 1) * TC],
                        in_=ap("qk_out")[r * 384 + hl * 128:
                                         r * 384 + (hl + 1) * 128, :]), "d1", 16)
            for r in range(R):
                c.inc(sy.dma_start(
                    out=ap("u1")[:, r * TC:(r + 1) * TC],
                    in_=ap("qk_out")[r * 384 + 256: r * 384 + 384, :]), "d1", 16)
            c.wait("SY", sy, ctx.ev["cc_v"])
            for jt in range(64):
                c.inc(sy.dma_start(
                    out=ap("u1")[:, 8192 + jt * 128: 8192 + (jt + 1) * 128],
                    in_=ap("v_out")[jt * 128:(jt + 1) * 128, :]), "d1", 16)
            ctx.ev["att_ld"] = c.now("d1")

        block.sync(sy_ld)

        pe0 = ctx.semv["pe"]
        se0 = ctx.semv["se"]
        ve0 = ctx.semv["ve"]
        s_val, pv_val, exp_val, p_val = {}, {}, {}, {}
        lo_val, bc_val, rl_val, oev_val = {}, {}, {}, {}
        pe_n, se_n, ve_n = pe0, se0, ve0
        si_of = {}
        si = 0
        for hl in range(2):
            for ib in range(16):
                ext = 4 * ib + 4
                pend = []
                for jc in range(ext):
                    si_of[(hl, ib, jc)] = si
                    pe_n += 1
                    s_val[si] = pe_n
                    pend.append(si)
                    si += 1
                    if len(pend) >= 3:
                        s0 = pend.pop(0)
                        pe_n += 1
                        pv_val[s0] = pe_n
                while pend:
                    s0 = pend.pop(0)
                    pe_n += 1
                    pv_val[s0] = pe_n
                lo_val[(hl, ib)] = pe_n
                pe_n += 1
                bc_val[(hl, ib)] = pe_n
        for hl in range(2):
            for ib in range(16):
                for jc in range(4 * ib + 4):
                    s0 = si_of[(hl, ib, jc)]
                    se_n += 1
                    exp_val[s0] = se_n
        for hl in range(2):
            for ib in range(16):
                for jc in range(4 * ib + 4):
                    s0 = si_of[(hl, ib, jc)]
                    if jc >= 4 * ib:
                        ve_n += 1
                        p_val[s0] = ("ve", ve_n)
                    else:
                        p_val[s0] = ("se", exp_val[s0])
                ve_n += 1
                rl_val[(hl, ib)] = ve_n
                ve_n += 1
                oev_val[(hl, ib)] = ve_n
        n_si = si

        def se_att(se):
            assert ctx.semv["se"] == se0
            c.wait("SE", se, ctx.ev["wd_in_done"])
            for hl in range(2):
                for ib in range(16):
                    for jc in range(4 * ib + 4):
                        s0 = si_of[(hl, ib, jc)]
                        c.wait("SE", se, ("pe", s_val[s0]))
                        if s0 >= 3:
                            c.wait("SE", se, ("pe", pv_val[s0 - 3]))
                        ins = se.activation(
                            ap("ptile")[:, (s0 % 3) * 512:(s0 % 3 + 1) * 512],
                            pf[s0 % 3][:, :], AF.Exp, scale=SM_SCALE)
                        c.inc(ins, "se")
            assert ctx.semv["se"] == se_n

        def ve_att(ve):
            assert ctx.semv["ve"] == ve0
            for hl in range(2):
                for ib in range(16):
                    for jc in range(4 * ib, 4 * ib + 4):
                        s0 = si_of[(hl, ib, jc)]
                        tau = jc - 4 * ib
                        c.wait("VE", ve, ("se", exp_val[s0]))
                        ins = ve.tensor_tensor(
                            ap("ptile")[:, (s0 % 3) * 512:(s0 % 3 + 1) * 512],
                            ap("ptile")[:, (s0 % 3) * 512:(s0 % 3 + 1) * 512],
                            ap("maskt")[:, 384 - tau * 128: 896 - tau * 128],
                            op=ALU.mult)
                        c.inc(ins, "ve")
                    c.wait("VE", ve, ("pe", lo_val[(hl, ib)]))
                    ve.reciprocal(ap("rlrow")[:, :], pf[4][0:1, :])
                    ins = ve.tensor_copy(ap("rlrow_bf")[:, :], ap("rlrow")[:, :])
                    c.inc(ins, "ve")
                    c.wait("VE", ve, ("pe", bc_val[(hl, ib)]))
                    ve.tensor_copy(ap("tg")[:, :], pf[5][:, :])
                    ins = ve.tensor_tensor(
                        ap("u1")[:, 16384 + hl * T + ib * 512:
                                 16384 + hl * T + (ib + 1) * 512],
                        pf[3][:, :], ap("tg")[:, :], op=ALU.mult)
                    c.inc(ins, "ve")
            assert ctx.semv["ve"] == ve_n
            ctx.ev["att_done_ve"] = c.now("ve")

        def te_att(te):
            assert ctx.semv["pe"] == pe0
            c.wait("TE", te, ctx.ev["att_ld"])
            for i in range(7):
                c.wait("TE", te, ctx.ev_bank.get(i))
            for hl in range(2):
                for ib in range(16):
                    ext = 4 * ib + 4
                    if ib > 0 or hl > 0:
                        prev = (hl, ib - 1) if ib > 0 else (hl - 1, 15)
                        c.wait("TE", te, ("ve", oev_val[prev]))
                    pend = []
                    for jc in range(ext):
                        s0 = si_of[(hl, ib, jc)]
                        if s0 >= 3:
                            c.wait("TE", te, p_val[s0 - 3])
                        ins = te.matmul(
                            pf[s0 % 3][:, :],
                            ap("u1")[:, jc * 128:(jc + 1) * 128],
                            ap("qbuf")[:, hl * T + ib * 512: hl * T + (ib + 1) * 512],
                            start=True, stop=True)
                        c.inc(ins, "pe")
                        assert ctx.semv["pe"] == s_val[s0]
                        pend.append((jc, s0))
                        if len(pend) >= 3:
                            _lpv(te, hl, ib, pend.pop(0))
                    while pend:
                        _lpv(te, hl, ib, pend.pop(0))
                    c.wait("TE", te, ("ve", rl_val[(hl, ib)]))
                    ins = te.matmul(pf[5][:, :], ap("onest")[0:1, :],
                                    ap("rlrow_bf")[0:1, :], start=True, stop=True)
                    c.inc(ins, "pe")
                    assert ctx.semv["pe"] == bc_val[(hl, ib)]
            ctx.seal_banks(ctx.ev["att_done_ve"])

        def _lpv(te, hl, ib, item):
            jc, s0 = item
            first = (jc == 0)
            last = (jc == 4 * ib + 3)
            c.wait("TE", te, p_val[s0])
            te.matmul(pf[4][0:1, :], ap("onest")[:, 0:1],
                      ap("ptile")[:, (s0 % 3) * 512:(s0 % 3 + 1) * 512],
                      start=first, stop=last, skip_group_check=True)
            ins = te.matmul(pf[3][:, :],
                            ap("u1")[:, 8192 + jc * 128: 8192 + (jc + 1) * 128],
                            ap("ptile")[:, (s0 % 3) * 512:(s0 % 3 + 1) * 512],
                            start=first, stop=last, skip_group_check=True)
            c.inc(ins, "pe")
            assert ctx.semv["pe"] == pv_val[s0]

        block.scalar(se_att)
        block.vector(ve_att)
        block.tensor(te_att)

        def sy_oc(sy):
            c.wait("SY", sy, ctx.ev["att_done_ve"])
            for j in range(R):
                for hl in range(2):
                    c.inc(sy.dma_start(
                        out=ap("o_in")[j * 256 + hl * 128: j * 256 + (hl + 1) * 128, :],
                        in_=ap("u1")[:, 16384 + hl * T + j * TC:
                                     16384 + hl * T + (j + 1) * TC]), "d1", 16)
            ctx.ev["o_in_done"] = c.now("d1")

        block.sync(sy_oc)

        def gp_cc(gp):
            c.wait("GP", gp, ctx.ev["o_in_done"])
            with allow_ici_all_to_all():
                ctx.ev["cc_o"] = c.inc(gp.collective_compute(
                    "AllToAll", ALU.bypass, replica_groups=[list(range(R))],
                    ins=[cap("o_in")], outs=[cap("o_out")]), "cc")

        block.gpsimd(gp_cc)

        def sy_ld2(sy):
            c.wait("SY", sy, ctx.ev["cc_o"])
            for h in range(16):
                c.inc(sy.dma_start(
                    out=ap("qbuf")[:, h * TC:(h + 1) * TC],
                    in_=ap("o_out")[(h // 2) * 256 + (h % 2) * 128:
                                    (h // 2) * 256 + (h % 2) * 128 + 128, :]),
                    "d1", 16)
            ctx.ev["ao_ld"] = c.now("d1")

        block.sync(sy_ld2)

        for tb in range(2):
            def rhs(k, _tb=tb):
                return ap("qbuf")[:, k * TC + _tb * 512: k * TC + (_tb + 1) * 512]

            ve_bw = ctx.semv["ve"]
            emit_stream_mm(f"wo{tb}", "d1", 16, 4,
                           lambda gi, k: ap("Wo")[k * 128:(k + 1) * 128,
                                                  gi * 512:(gi + 1) * 512],
                           512, rhs, sy_pre=[ctx.ev["ao_ld"]],
                           te_pre=ctx.ev["ao_ld"],
                           release_fn=lambda oc, b=ve_bw: ("ve", b + oc + 1))

            def ve_ev(ve, _tb=tb, _vb=ve_bw):
                assert ctx.semv["ve"] == _vb
                ev = None
                for oc in range(16):
                    bk = ctx.ev[(f"wo{_tb}", "bank", oc)]
                    c.wait("VE", ve, ctx.ev[(f"wo{_tb}", "mm", oc)])
                    xs = ap("x32")[:, oc * TC + _tb * 512: oc * TC + (_tb + 1) * 512]
                    ins = ve.tensor_tensor(xs, xs, pf[bk][:, :], op=ALU.add)
                    ev = c.inc(ins, "ve")
                ctx.ev[f"res1_{_tb}"] = ev

            block.vector(ve_ev)

    def phase_ffn(call, dsem, pre_ev):
        for tb in range(2):
            tag = f"f{call}_{tb}"
            xn_ev = emit_ln(tb, True, ap("qbuf")[:, 8192:16384],
                            pre_ev=(pre_ev if tb == 0 else
                                    ctx.ev.get("dbg_ffn")))

            def rhs_xn(k):
                return ap("qbuf")[:, 8192 + k * 512: 8192 + (k + 1) * 512]

            se_bx = ctx.semv["se"]
            emit_stream_mm(f"xh{tag}", dsem, 16, 4,
                           lambda gi, k: ap("Hpm")[k * 128:(k + 1) * 128,
                                                   gi * 512:(gi + 1) * 512],
                           512, rhs_xn, te_pre=xn_ev,
                           release_fn=lambda oc: ("se", se_bx + oc + 1))

            def se_xh(se):
                assert ctx.semv["se"] == se_bx
                ev = None
                for oc in range(16):
                    bk = ctx.ev[(f"xh{tag}", "bank", oc)]
                    c.wait("SE", se, ctx.ev[(f"xh{tag}", "mm", oc)])
                    ins = se.copy(ap("qbuf")[:, oc * 512:(oc + 1) * 512], pf[bk][:, :])
                    ev = c.inc(ins, "se")
                ctx.ev[f"xh_ev{tag}"] = ev

            block.scalar(se_xh)

            def rhs_xh(k):
                return ap("qbuf")[:, k * 512:(k + 1) * 512]

            def ug_slab(gi, k):
                ci = 2 * gi
                r_ = ci // 8
                lc = ci % 8
                return ap("ug_out")[r_ * DM + k * 128: r_ * DM + (k + 1) * 128,
                                    lc * 256: lc * 256 + 512]

            se_bu = ctx.semv["se"]
            ve_bu = ctx.semv["ve"]

            def ug_rel(oc):
                ioc = oc // 2
                if oc % 2:
                    return ("se", se_bu + 2 * ioc + 2)
                return ("ve", ve_bu + ioc + 1)

            emit_stream_mm(f"ug{tag}", dsem, 128, 4, ug_slab, 512, rhs_xh,
                           sy_pre=[ctx.ev["cc_ug"]],
                           te_pre=ctx.ev[f"xh_ev{tag}"], release_fn=ug_rel)

            def se_ug(se):
                assert ctx.semv["se"] == se_bu
                for ioc in range(64):
                    bg = ctx.ev[(f"ug{tag}", "bank", 2 * ioc + 1)]
                    c.wait("SE", se, ctx.ev[(f"ug{tag}", "mm", 2 * ioc + 1)])
                    if ioc >= 2:
                        c.wait("SE", se, ("ve", ve_bu + ioc - 1))
                    tgb = ap("tg") if ioc % 2 == 0 else ap("t2")
                    ins = se.activation(tgb[:, 0:256], pf[bg][:, 0:256],
                                        AF.Silu,
                                        scale=ap("wsgt")[:, ioc:ioc + 1])
                    c.inc(ins, "se")
                    ins = se.activation(tgb[:, 256:512], pf[bg][:, 256:512],
                                        AF.Silu,
                                        scale=ap("wsgt")[:, ioc:ioc + 1])
                    ctx.ev[(tag, "tg", ioc)] = c.inc(ins, "se")

            def ve_ug(ve):
                assert ctx.semv["ve"] == ve_bu
                for ioc in range(64):
                    bu = ctx.ev[(f"ug{tag}", "bank", 2 * ioc)]
                    c.wait("VE", ve, ctx.ev[(tag, "tg", ioc)])
                    tgb = ap("tg") if ioc % 2 == 0 else ap("t2")
                    ins = ve.tensor_tensor(
                        ap("u1")[:, ioc * 512:(ioc + 1) * 512],
                        pf[bu][:, :], tgb[:, :], op=ALU.mult)
                    ev = c.inc(ins, "ve")
                    ctx.ev[(tag, "z1", ioc)] = ev
                    if ioc == 63:
                        ins = ve.tensor_copy(ap("t1")[:, 0:8], ap("t1")[:, 0:8])
                        c.inc(ins, "ve")

            block.scalar(se_ug)
            block.vector(ve_ug)

            if debug_wts and call == 1 and tb == 0:
                def sy_dbgf(sy):
                    c.wait("SY", sy, ctx.ev[(tag, "z1", 63)])
                    c.inc(sy.dma_start(out=ap("dbg_xh")[:, :],
                                       in_=ap("qbuf")[:, :]), "d5", 16)
                    c.inc(sy.dma_start(out=ap("dbg_z1")[:, :],
                                       in_=ap("u1")[:, :]), "d5", 16)
                    ctx.ev["dbg_ffn"] = c.now("d5")
                block.sync(sy_dbgf)

            pe_base = ctx.semv["pe"]

            dnsem = ctx.next_dsem()

            def sy_dn(sy):
                c.wait("SY", sy, ctx.ev["cc_wd"])
                i = 0
                for hcp in range(8):
                    for ioc in range(64):
                        s, wev = slab_slot()
                        c.wait("SY", sy, wev)
                        ins = sy.dma_start(
                            out=ap("wstream")[:, s * 512: s * 512 + 256],
                            in_=ap("wd_out")[hcp * INTER + ioc * 128:
                                             hcp * INTER + (ioc + 1) * 128, :])
                        ctx.ev[(tag, "dnslab", i)] = (c.inc(ins, dnsem, 16), s)
                        ctx.slab_ev[s] = ("pe", pe_base + i + 1)
                        i += 1
                ins = sy.dma_start(
                    out=ap("wstream")[:, s * 512: s * 512 + 256],
                    in_=ap("wd_out")[7 * INTER + 63 * 128: 8 * INTER, :])
                ctx.ev[(tag, "dntail")] = c.inc(ins, dnsem, 16)

            ve_bd = [None]

            def te_dn(te):
                assert ctx.semv["pe"] == pe_base
                ve_b = ve_bd[0]
                i = 0
                for hcp in range(8):
                    b0 = rot.acquire(te)
                    b1 = rot.acquire(te)
                    rot.release(b0, ("ve", ve_b + 2 * hcp + 1))
                    rot.release(b1, ("ve", ve_b + 2 * hcp + 2))
                    for ioc in range(64):
                        ev, s = ctx.ev[(tag, "dnslab", i)]
                        ev = (ctx.ev[(tag, "dnslab", i + 1)][0] if i + 1 < 512
                              else ctx.ev[(tag, "dntail")])
                        c.wait("TE", te, ev)
                        zev = ctx.ev[(tag, "z1", ioc)]
                        c.wait("TE", te, (zev[0], zev[1] + 1))
                        te.matmul(pf[b0][:, :],
                                  ap("wstream")[:, s * 512: s * 512 + 128],
                                  ap("u1")[:, ioc * 512:(ioc + 1) * 512],
                                  start=(ioc == 0), stop=(ioc == 63))
                        ins = te.matmul(pf[b1][:, :],
                                        ap("wstream")[:, s * 512 + 128: s * 512 + 256],
                                        ap("u1")[:, ioc * 512:(ioc + 1) * 512],
                                        start=(ioc == 0), stop=(ioc == 63))
                        c.inc(ins, "pe")
                        i += 1
                    ctx.ev[(tag, "dnmm", hcp)] = c.now("pe")
                    ctx.ev[(tag, "dnbank", hcp)] = (b0, b1)

            def ve_dn(ve):
                assert ctx.semv["ve"] == ve_bd[0]
                ev = None
                for hcp in range(8):
                    b0, b1 = ctx.ev[(tag, "dnbank", hcp)]
                    c.wait("VE", ve, ctx.ev[(tag, "dnmm", hcp)])
                    for s_, bk in ((0, b0), (1, b1)):
                        hc = 2 * hcp + s_
                        xs = ap("x32")[:, hc * TC + tb * 512: hc * TC + (tb + 1) * 512]
                        ins = ve.scalar_tensor_tensor(
                            xs, pf[bk][:, :], ap("wsdpt")[:, hc:hc + 1], xs,
                            op0=ALU.mult, op1=ALU.add)
                        ev = c.inc(ins, "ve")
                ctx.ev[f"ffn_done{tag}"] = ev

            block.sync(sy_dn)
            ve_bd[0] = ctx.semv["ve"]
            block.tensor(te_dn)
            block.vector(ve_dn)
        return ctx.ev[f"ffn_done{tag}"]

    def phase4(pre_ev, sub=99):
        dsem = "d3"

        def sy_rt(sy):
            c.inc(sy.dma_start(out=ap("ropec")[:, :], in_=ap("rb_cos")[:, :]),
                  "d1", 16)
            c.inc(sy.dma_start(out=ap("ropes")[:, :], in_=ap("rb_sin")[:, :]),
                  "d1", 16)
            ctx.ev["rope_rl"] = c.now("d1")

        block.sync(sy_rt)
        for tb in range(2):
            xn_ev = emit_ln(tb, False, ap("u1")[:, 8192:16384],
                            pre_ev=(pre_ev if tb == 0 else None))

            def rhs(k):
                return ap("u1")[:, 8192 + k * 512: 8192 + (k + 1) * 512]

            def q_slab(gi, k):
                return ap("wb_out")[gi * DM + k * 128: gi * DM + (k + 1) * 128, 0:256]

            se_b4 = ctx.semv["se"]
            emit_stream_mm(f"bq{tb}", dsem, 16, 2, q_slab, 256, rhs,
                           sy_pre=[ctx.ev["cc_wb"]], te_pre=xn_ev,
                           release_fn=lambda oc: ("se", se_b4 + oc + 1))
            items = [(h, ap("qbuf")[:, h * TC + tb * 512: h * TC + (tb + 1) * 512],
                      True) for h in range(16)]
            e1 = emit_rope_evac(f"bq{tb}", tb, items, ap("ropec"), ap("ropes"),
                                ve_pre=ctx.ev["rope_rl"])

            def kv_slab(colbase):
                def f(gi, k):
                    return (ap("wb_out")[(2 * gi) * DM + k * 128:
                                         (2 * gi) * DM + (k + 1) * 128,
                                         colbase: colbase + 64],
                            ap("wb_out")[(2 * gi + 1) * DM + k * 128:
                                         (2 * gi + 1) * DM + (k + 1) * 128,
                                         colbase: colbase + 64])
                return f

            se_b5 = ctx.semv["se"]
            emit_stream_mm(f"bk{tb}", dsem, 4, 1, kv_slab(256), 128, rhs,
                           dual_dma=True,
                           release_fn=lambda oc: ("se", se_b5 + oc + 1))
            items = [(kv, ap("u1")[:, kv * TC + tb * 512: kv * TC + (tb + 1) * 512],
                      True) for kv in range(4)]
            e2 = emit_rope_evac(f"bk{tb}", tb, items, ap("ropec"), ap("ropes"),
                                ve_pre=ctx.ev["rope_rl"])
            se_b6 = ctx.semv["se"]
            emit_stream_mm(f"bv{tb}", dsem, 4, 1, kv_slab(320), 128, rhs,
                           dual_dma=True,
                           release_fn=lambda oc: ("se", se_b6 + oc + 1))
            items = [(kv, ap("u1")[:, 4096 + kv * TC + tb * 512:
                                   4096 + kv * TC + (tb + 1) * 512], False)
                     for kv in range(4)]
            e3 = emit_rope_evac(f"bv{tb}", tb, items, ap("ropec"), ap("ropes"))
            ctx.ev[f"bqkv_ev{tb}"] = [e1, e2, e3]

        if sub < 2:
            return None

        ve_t0 = [None]

        def te_t(te):
            for e in ctx.ev["bqkv_ev0"] + ctx.ev["bqkv_ev1"]:
                c.wait("TE", te, e)
            i = 0
            for src_off in (0, 4096):
                for kv in range(4):
                    for ch2 in range(8):
                        if i >= 2:
                            c.wait("TE", te, ("ve", ve_t0[0] + i - 1))
                        ins = te.transpose(
                            ptp[:, (i % 2) * 128:(i % 2) * 128 + 128],
                            ap("u1")[:, src_off + kv * TC + ch2 * 128:
                                     src_off + kv * TC + (ch2 + 1) * 128],
                            ap("identt")[:, :])
                        ctx.ev[("bt_mm", i)] = c.inc(ins, "pe")
                        i += 1

        def ve_t(ve):
            assert ctx.semv["ve"] == ve_t0[0]
            i = 0
            for dst_off in (16384, 24576):
                for kv in range(4):
                    for ch2 in range(8):
                        c.wait("VE", ve, ctx.ev[("bt_mm", i)])
                        ins = ve.tensor_copy(
                            ap("u1")[:,
                                     dst_off + kv * 1024 + ch2 * 128:
                                     dst_off + kv * 1024 + (ch2 + 1) * 128],
                            ptp[:, (i % 2) * 128:(i % 2) * 128 + 128])
                        ctx.ev[("bt_ve", i)] = c.inc(ins, "ve")
                        i += 1
            ctx.ev["bt_all"] = c.now("ve")

        ve_t0[0] = ctx.semv["ve"]
        block.tensor(te_t)
        block.vector(ve_t)

        if sub < 3:
            return None

        def sy_ct(sy):
            c.wait("SY", sy, ctx.ev["bt_all"])
            for j in range(R):
                for hl in range(2):
                    h = 2 * j + hl
                    c.inc(sy.dma_start(
                        out=ap("qk2_in")[j * 384 + hl * 128: j * 384 + (hl + 1) * 128, :],
                        in_=ap("qbuf")[:, h * TC:(h + 1) * TC]), dsem, 16)
                kv = j // 2
                c.inc(sy.dma_start(
                    out=ap("qk2_in")[j * 384 + 256: j * 384 + 384, :],
                    in_=ap("u1")[:, kv * TC:(kv + 1) * TC]), dsem, 16)
                for ch2 in range(8):
                    c.inc(sy.dma_start(
                        out=ap("ktv_in")[j * TC + ch2 * 128: j * TC + (ch2 + 1) * 128, 0:128],
                        in_=ap("u1")[:, 16384 + kv * 1024 + ch2 * 128:
                                     16384 + kv * 1024 + (ch2 + 1) * 128]), dsem, 16)
                    c.inc(sy.dma_start(
                        out=ap("ktv_in")[j * TC + ch2 * 128: j * TC + (ch2 + 1) * 128, 128:256],
                        in_=ap("u1")[:, 24576 + kv * 1024 + ch2 * 128:
                                     24576 + kv * 1024 + (ch2 + 1) * 128]), dsem, 16)
            ctx.ev["qk2_in_done"] = c.now(dsem)

        block.sync(sy_ct)

        def gp_cc(gp):
            c.wait("GP", gp, ctx.ev["qk2_in_done"])
            with allow_ici_all_to_all():
                ctx.ev["cc_qk2"] = c.inc(gp.collective_compute(
                    "AllToAll", ALU.bypass, replica_groups=[list(range(R))],
                    ins=[cap("qk2_in")], outs=[cap("qk2_out")]), "cc")
                ctx.ev["cc_ktv"] = c.inc(gp.collective_compute(
                    "AllToAll", ALU.bypass, replica_groups=[list(range(R))],
                    ins=[cap("ktv_in")], outs=[cap("ktv_out")]), "cc")

        block.gpsimd(gp_cc)

        def sy_l2(sy):
            c.wait("SY", sy, ctx.ev["cc_qk2"])
            for hl in range(2):
                for r in range(R):
                    c.inc(sy.dma_start(
                        out=ap("qbuf")[:, hl * T + r * TC: hl * T + (r + 1) * TC],
                        in_=ap("qk2_out")[r * 384 + hl * 128:
                                          r * 384 + (hl + 1) * 128, :]), dsem, 16)
            for r in range(R):
                c.inc(sy.dma_start(
                    out=ap("u1")[:, r * TC:(r + 1) * TC],
                    in_=ap("qk2_out")[r * 384 + 256: r * 384 + 384, :]), dsem, 16)
            ctx.ev["b_ld"] = c.now(dsem)

        block.sync(sy_l2)

        if sub < 4:
            return None

        pe0 = ctx.semv["pe"]
        ve0 = ctx.semv["ve"]

        def ve_of_seg(sg):
            return ve0 + sg * 70

        def sy_seg(sy):
            c.wait("SY", sy, ctx.ev["cc_ktv"])
            for sg in range(8):
                if sg >= 2:
                    c.wait("SY", sy, ("pe", pe0 + 48 * (sg - 1)))
                    c.wait("SY", sy, ("ve", ve_of_seg(sg - 2) + 36))
                base = 8192 + (sg % 2) * 4096
                for ci2 in range(8):
                    c.inc(sy.dma_start(
                        out=ap("u1")[0:64, base + (2 * ci2) * 128:
                                     base + (2 * ci2 + 2) * 128]
                        .rearrange("p (a b) -> p a b", a=2),
                        in_=ap("ktv_out")[sg * TC + ci2 * 128:
                                          sg * TC + (ci2 + 1) * 128, 0:128]
                        .rearrange("(a j) d -> j a d", a=2)), f"sg{sg % 2}", 16)
                    c.inc(sy.dma_start(
                        out=ap("u1")[0:64, base + 2048 + (2 * ci2) * 128:
                                     base + 2048 + (2 * ci2 + 2) * 128]
                        .rearrange("p (a b) -> p a b", a=2),
                        in_=ap("ktv_out")[sg * TC + ci2 * 128:
                                          sg * TC + (ci2 + 1) * 128, 128:256]
                        .rearrange("(a j) d -> j a d", a=2)), f"sg{sg % 2}", 16)
                ctx.ev[("seg_ld", sg)] = c.now(f"sg{sg % 2}")

        block.sync(sy_seg)

        def ve_scan(ve):
            assert ctx.semv["ve"] == ve0
            c.wait("VE", ve, ctx.ev["pads_zero"])
            c.wait("VE", ve, ctx.ev["b_ld"])
            for sg in range(8):
                base = 8192 + (sg % 2) * 4096
                for half in range(2):
                    gh = sg * 2 + half
                    c.wait("VE", ve, ctx.ev[("seg_ld", sg)])
                    if gh >= 1:
                        c.wait("VE", ve, ("pe", pe0 + 3 * 8 * gh))
                    ins = None
                    for hl in range(2):
                        for cl in range(8):
                            ci = half * 8 + cl
                            ins = ve.tensor_scalar(
                                ap("ktmd")[0:64, (hl * 8 + cl) * 128:
                                           (hl * 8 + cl + 1) * 128],
                                ap("u1")[0:64, base + ci * 128: base + (ci + 1) * 128],
                                ap("kdect")[0:64, hl:hl + 1], None, op0=ALU.mult)
                    c.inc(ins, "ve")
                    for cl in range(8):
                        ci = half * 8 + cl
                        gc = sg * 16 + ci
                        for hl in range(2):
                            c.wait("VE", ve, ("pe", pe0 + 3 * gc + 2))
                            ve.tensor_tensor(
                                ap("smt")[0:64, hl * 64:(hl + 1) * 64],
                                pf[2 + gc % 2][0:64, hl * 64:(hl + 1) * 64],
                                ap("diagt")[:, hl * 64:(hl + 1) * 64], op=ALU.mult)
                            ins = ve.tensor_tensor(
                                ap("tmq")[:, hl * 64:(hl + 1) * 64],
                                ap("qbuf")[:, hl * T + gc * 64: hl * T + (gc + 1) * 64],
                                ap("qdect")[:, hl * 64:(hl + 1) * 64], op=ALU.mult)
                            c.inc(ins, "ve")
                        for hl in range(2):
                            c.wait("VE", ve, ("pe", pe0 + 3 * gc + 1))
                            kvf = ap("kvf0") if hl == 0 else ap("kvf1")
                            kvb = ap("kvb0") if hl == 0 else ap("kvb1")
                            ve.scalar_tensor_tensor(
                                kvf[:, :], kvf[:, :], ap("blkvt")[:, hl:hl + 1],
                                pf[gc % 2][:, hl * 128:(hl + 1) * 128],
                                op0=ALU.mult, op1=ALU.add)
                            ins = ve.tensor_copy(
                                kvb[:, ((gc + 1) % 2) * 128:((gc + 1) % 2) * 128 + 128],
                                kvf[:, :])
                            c.inc(ins, "ve")
                        if gc % 8 == 7:
                            for hl in range(2):
                                c.wait("VE", ve, ("pe", pe0 + 3 * gc + 3))
                                ins = ve.tensor_copy(
                                    ap("u1")[:, 16384 + hl * T + (gc - 7) * 64:
                                             16384 + hl * T + (gc + 1) * 64],
                                    pf[4 + hl][:, :])
                                c.inc(ins, "ve")
                assert ctx.semv["ve"] == ve_of_seg(sg + 1)
            ctx.ev["scan_done"] = c.now("ve")

        def te_scan(te):
            assert ctx.semv["pe"] == pe0
            c.wait("TE", te, ctx.ev["b_ld"])
            for i in range(7):
                c.wait("TE", te, ctx.ev_bank.get(i))
            for sg in range(8):
                base = 8192 + (sg % 2) * 4096
                for ci in range(16):
                    gc = sg * 16 + ci
                    half = ci // 8
                    gh = sg * 2 + half
                    cl = ci % 8
                    decv = ve_of_seg(sg) + half * 35 + 1
                    c.wait("TE", te, ctx.ev[("seg_ld", sg)])
                    c.wait("TE", te, ("ve", decv))
                    if gc >= 2:
                        c.wait("TE", te, ("ve", _kvb_val(sg, ci, ve0) - 0))
                    for hl in range(2):
                        ins = te.matmul(
                            pf[gc % 2][:, hl * 128:(hl + 1) * 128],
                            ap("ktmd")[:, (hl * 8 + cl) * 128:(hl * 8 + cl + 1) * 128],
                            ap("u1")[:, base + 2048 + ci * 128: base + (ci + 1) * 128 + 2048],
                            start=True, stop=True, skip_group_check=True)
                    c.inc(ins, "pe")
                    for hl in range(2):
                        ins = te.matmul(
                            pf[2 + gc % 2][0:64, hl * 64:(hl + 1) * 64],
                            ap("u1")[:, gc * 64:(gc + 1) * 64],
                            ap("qbuf")[:, hl * T + gc * 64: hl * T + (gc + 1) * 64],
                            start=True, stop=True, skip_group_check=True)
                    c.inc(ins, "pe")
                    slot = gc % 8
                    for hl in range(2):
                        c.wait("TE", te, ("ve", _smq_val(sg, ci, ve0)))
                        if gc > 0:
                            c.wait("TE", te, ("ve", _kvb_val2(sg, ci, ve0)))
                        if slot == 0 and gc >= 8:
                            c.wait("TE", te, ("ve", _oev_val(gc // 8 - 1, ve0)))
                        kvb = ap("kvb0") if hl == 0 else ap("kvb1")
                        te.matmul(pf[4 + hl][:, slot * 64:(slot + 1) * 64],
                                  ap("u1")[:, base + 2048 + ci * 128:
                                           base + 2048 + (ci + 1) * 128],
                                  ap("smt")[:, hl * 64:(hl + 1) * 64],
                                  start=True, stop=False, skip_group_check=True)
                        ins = te.matmul(pf[4 + hl][:, slot * 64:(slot + 1) * 64],
                                        kvb[:, (gc % 2) * 128:(gc % 2) * 128 + 128],
                                        ap("tmq")[:, hl * 64:(hl + 1) * 64],
                                        start=False, stop=True, skip_group_check=True)
                    c.inc(ins, "pe")
                    assert ctx.semv["pe"] == pe0 + 3 * gc + 3
            ctx.seal_banks(ctx.ev["scan_done"])

        def _smq_val(sg, ci, ve0_):
            half = ci // 8
            cl = ci % 8
            return ve_of_seg(sg) + half * 35 + 1 + cl * 4 + 2

        def _kvb_val(sg, ci, ve0_):
            gc = sg * 16 + ci - 2
            sg2, ci2 = divmod(gc, 16)
            half = ci2 // 8
            cl = ci2 % 8
            return ve_of_seg(sg2) + half * 35 + 1 + cl * 4 + 4

        def _kvb_val2(sg, ci, ve0_):
            gc = sg * 16 + ci - 1
            sg2, ci2 = divmod(gc, 16)
            half = ci2 // 8
            cl = ci2 % 8
            return ve_of_seg(sg2) + half * 35 + 1 + cl * 4 + 4

        def _oev_val(slotidx, ve0_):
            gc = slotidx * 8 + 7
            sg2, ci2 = divmod(gc, 16)
            half = ci2 // 8
            return ve_of_seg(sg2) + half * 35 + 1 + 8 * 4 + 2

        block.vector(ve_scan)
        block.tensor(te_scan)

        if sub < 5:
            return None

        def sy_o2(sy):
            c.wait("SY", sy, ctx.ev["scan_done"])
            for j in range(R):
                for hl in range(2):
                    c.inc(sy.dma_start(
                        out=ap("o2_in")[j * 256 + hl * 128: j * 256 + (hl + 1) * 128, :],
                        in_=ap("u1")[:, 16384 + hl * T + j * TC:
                                     16384 + hl * T + (j + 1) * TC]), dsem, 16)
            ctx.ev["o2_in_done"] = c.now(dsem)

        block.sync(sy_o2)

        def gp_cc2(gp):
            c.wait("GP", gp, ctx.ev["o2_in_done"])
            with allow_ici_all_to_all():
                ctx.ev["cc_o2"] = c.inc(gp.collective_compute(
                    "AllToAll", ALU.bypass, replica_groups=[list(range(R))],
                    ins=[cap("o2_in")], outs=[cap("o2_out")]), "cc")

        block.gpsimd(gp_cc2)

        def sy_l3(sy):
            c.wait("SY", sy, ctx.ev["cc_o2"])
            for h in range(16):
                c.inc(sy.dma_start(
                    out=ap("qbuf")[:, h * TC:(h + 1) * TC],
                    in_=ap("o2_out")[(h // 2) * 256 + (h % 2) * 128:
                                     (h // 2) * 256 + (h % 2) * 128 + 128, :]),
                    dsem, 16)
            ctx.ev["bo_ld"] = c.now(dsem)

        block.sync(sy_l3)

        for tb in range(2):
            def rhs(k, _tb=tb):
                return ap("qbuf")[:, k * TC + _tb * 512: k * TC + (_tb + 1) * 512]

            def wo_slab(gi, k):
                return ap("wb_out")[gi * DM + k * 128: gi * DM + (k + 1) * 128,
                                    384:640]

            ve_bw2 = ctx.semv["ve"]
            emit_stream_mm(f"bwo{tb}", dsem, 16, 2, wo_slab, 256, rhs,
                           sy_pre=[ctx.ev["bo_ld"]], te_pre=ctx.ev["bo_ld"],
                           release_fn=lambda oc, b=ve_bw2: ("ve", b + oc + 1))

            def ve_ev(ve, _tb=tb, _vb=ve_bw2):
                assert ctx.semv["ve"] == _vb
                ev = None
                for oc in range(16):
                    bk = ctx.ev[(f"bwo{_tb}", "bank", oc)]
                    c.wait("VE", ve, ctx.ev[(f"bwo{_tb}", "mm", oc)])
                    xs = ap("x32")[:, oc * TC + _tb * 512: oc * TC + (_tb + 1) * 512]
                    ins = ve.tensor_tensor(xs, xs, pf[bk][:, :], op=ALU.add)
                    ev = c.inc(ins, "ve")
                ctx.ev[f"res3_{_tb}"] = ev

            block.vector(ve_ev)

    def phase_store():
        def syb(sy):
            for s_ in list(ctx.semv):
                c.wait("SY", sy, (s_, ctx.semv[s_]))
            for ch in range(NCH):
                c.inc(sy.dma_start(
                    out=ap("out_x")[:, ch * TC:(ch + 1) * TC],
                    in_=ap("x32")[:, ch * TC:(ch + 1) * TC]), "d5", 16)
            c.wait("SY", sy, c.now("d5"))

        block.sync(syb)

    def phase_dbg():
        def gpd(gp):
            c.wait("GP", gp, ctx.ev["cc_ug"])
            c.wait("GP", gp, ctx.ev["cc_wd"])
            c.wait("GP", gp, ctx.ev["cc_wb"])
            c.inc(gp.dma_start(out=ap("dbg_ug")[:, :], in_=ap("ug_out")[:, :]), "g1", 16)
            c.inc(gp.dma_start(out=ap("dbg_wd")[:, :], in_=ap("wd_out")[:, :]), "g1", 16)
            c.inc(gp.dma_start(out=ap("dbg_wb")[:, :], in_=ap("wb_out")[:, :]), "g1", 16)
            c.inc(gp.dma_start(out=ap("dbg_wdi")[:, :], in_=ap("wdi")[:, :]), "g1", 16)
        block.gpsimd(gpd)

    phase1()
    fold_wd()
    phase2()
    if upto >= 2:
        phase_ffn(1, "d2", ctx.ev["o_in_done"])
    if 2 < upto < 3:
        phase4(None, sub=int(round((upto - 2) * 10)))
    if upto >= 3:
        phase4(None)
    if upto >= 4:
        phase_ffn(2, "d4", ctx.ev["o2_in_done"])
    if debug_wts:
        phase_dbg()
    phase_store()


def _hadamard(n):
    h = np.array([[1.0]], dtype=np.float64)
    while h.shape[0] < n:
        h = np.block([[h, h], [h, -h]])
    return h


def _rope_tables(pos, base):
    half = 64
    inv = base ** -(np.arange(half, dtype=np.float64) / half)
    ang = np.outer(inv, pos.astype(np.float64))
    cos = np.cos(ang)
    sin = np.sin(ang)
    return (np.concatenate([cos, cos], axis=0),
            np.concatenate([sin, sin], axis=0))


_NC_CACHE = {}


def _get_nc(upto, debug_wts=False):
    key = (upto, debug_wts)
    if key not in _NC_CACHE:
        _NC_CACHE[key] = build_program(upto, debug_wts)
    return _NC_CACHE[key]


def build_in_maps(L0_attn_a_x, L0_attn_a_Wq, L0_attn_a_bq, L0_attn_a_Wk,
                  L0_attn_a_bk, L0_attn_a_Wv, L0_attn_a_bv, L0_attn_a_Wo,
                  L0_attn_b_wq, L0_attn_b_wk, L0_attn_b_wv, L0_attn_b_wo,
                  L0_ffn_SU_ug, L0_ffn_SU_dp, L0_ffn_Wscale_ug,
                  L0_ffn_Wscale_dp, L0_ffn_W_up, L0_ffn_W_gate, L0_ffn_W_down):
    f32 = np.float32
    asn = np.ascontiguousarray
    x = np.asarray(L0_attn_a_x)
    Wq = np.asarray(L0_attn_a_Wq)
    Wk = np.asarray(L0_attn_a_Wk)
    Wv = np.asarray(L0_attn_a_Wv)
    Wo = np.asarray(L0_attn_a_Wo)
    bq = np.asarray(L0_attn_a_bq)
    bk = np.asarray(L0_attn_a_bk)
    bv = np.asarray(L0_attn_a_bv)
    wqT = asn(np.asarray(L0_attn_b_wq, f32).T)
    wkT = asn(np.asarray(L0_attn_b_wk, f32).T)
    wvT = asn(np.asarray(L0_attn_b_wv, f32).T)
    woT = asn(np.asarray(L0_attn_b_wo, f32).T)
    wupT = asn(np.asarray(L0_ffn_W_up, f32).T)
    wgT = asn(np.asarray(L0_ffn_W_gate, f32).T)
    wdT = asn(np.asarray(L0_ffn_W_down, f32).T)
    su_ug = np.asarray(L0_ffn_SU_ug, f32)
    su_dp = np.asarray(L0_ffn_SU_dp, f32)
    ws_ug = np.asarray(L0_ffn_Wscale_ug, f32)
    ws_dp = np.asarray(L0_ffn_Wscale_dp, f32)

    Hpm = _hadamard(2048).astype(bfdt)
    H64 = _hadamard(64).astype(bfdt)
    H128 = _hadamard(128).astype(bfdt)
    ident = np.eye(128, dtype=bfdt)
    ones = np.ones((128, 128), dtype=bfdt)
    Rm = np.zeros((128, 128), np.float64)
    for d in range(64):
        Rm[d, d + 64] = -1.0
        Rm[d + 64, d] = 1.0
    Rpm = np.ascontiguousarray(Rm.T).astype(bfdt)
    mask = (np.arange(128)[:, None] <= (np.arange(896)[None, :] - 384)).astype(bfdt)
    lam = np.exp2(-8.0 * np.arange(1, 17, dtype=np.float64) / 16.0)

    in_maps = []
    for cid in range(R):
        sl = slice(cid * TC, (cid + 1) * TC)
        pos = np.arange(cid * TC, (cid + 1) * TC)
        ca, sa = _rope_tables(pos, 1e6)
        cb, sb2 = _rope_tables(pos, 1e4)
        heads = [2 * cid, 2 * cid + 1]
        diagT = np.zeros((64, 128), dtype=f32)
        qdec = np.zeros((128, 128), dtype=f32)
        kdec = np.zeros((128, 2), dtype=f32)
        blkv = np.zeros((128, 2), dtype=f32)
        rr = np.arange(64, dtype=np.float64)
        for hi_, h in enumerate(heads):
            diff = rr[None, :] - rr[:, None]
            diagT[:, hi_ * 64:(hi_ + 1) * 64] = np.where(
                diff >= 0, np.exp(-lam[h] * diff), 0.0)
            qdec[:, hi_ * 64:(hi_ + 1) * 64] = np.exp(-lam[h] * (rr + 1.0))[None, :]
            kdec[0:64, hi_] = np.exp(-lam[h] * (63.0 - rr))
            kdec[64:128, hi_] = kdec[0:64, hi_]
            blkv[:, hi_] = math.exp(-lam[h] * 64.0)
        in_maps.append({
            "x_fm": asn(x[0, sl, :].T),
            "Wq": Wq, "Wk": Wk, "Wv": Wv, "Wo": Wo,
            "bqr": asn(bq.reshape(1, 2048)),
            "bkr": asn(bk.reshape(1, 512)),
            "bvr": asn(bv.reshape(1, 512)),
            "rpmc": Rpm,
            "wqT_sh": asn(wqT[:, cid * 256:(cid + 1) * 256]),
            "wkT_sh": asn(wkT[:, cid * 64:(cid + 1) * 64]),
            "wvT_sh": asn(wvT[:, cid * 64:(cid + 1) * 64]),
            "woT_sh": asn(woT[:, cid * 256:(cid + 1) * 256]),
            "wupT_sh": asn(wupT[:, cid * 1024:(cid + 1) * 1024]),
            "wgT_sh": asn(wgT[:, cid * 1024:(cid + 1) * 1024]),
            "wdT_sh": asn(wdT[:, cid * 256:(cid + 1) * 256]),
            "su_ug2": asn(su_ug.reshape(16, 128).T),
            "su_dp2": asn(su_dp.reshape(64, 128).T),
            "wsup2": asn(ws_ug[:INTER].reshape(64, 128).T),
            "wsg2": asn(ws_ug[INTER:].reshape(64, 128).T),
            "wsdp2": asn(ws_dp.reshape(16, 128).T),
            "Hpm": Hpm, "H64c": H64, "H128c": H128,
            "identc": ident, "onesc": ones, "maskcc": mask,
            "ra_cos": asn(ca.astype(bfdt)), "ra_sin": asn(sa.astype(bfdt)),
            "rb_cos": asn(cb.astype(bfdt)), "rb_sin": asn(sb2.astype(bfdt)),
            "diagTc": diagT, "qdecc": asn(qdec.astype(bfdt)),
            "kdecc": kdec, "blkvc": blkv,
        })
    return in_maps


def kernel(_upto=4, _debug_wts=False, **inputs):
    in_maps = build_in_maps(**inputs)
    nc = _get_nc(_upto, _debug_wts)
    res = run_bass_kernel_spmd(nc, in_maps, core_ids=list(range(R)))
    if _debug_wts:
        kernel._dbg = {k: np.asarray(res.results[0][k]) for k in ("dbg_ug", "dbg_wd", "dbg_wb", "dbg_wdi", "dbg_xh", "dbg_z1")}
        kernel._dbg_all = [{k: np.asarray(r[k]) for k in ("dbg_ug", "dbg_wd", "dbg_xh", "dbg_z1")} for r in res.results]
    out = np.empty((1, T, DM), np.float32)
    for cid in range(R):
        arr = np.asarray(res.results[cid]["out_x"])
        out[0, cid * TC:(cid + 1) * TC, :] = (
            arr.reshape(128, NCH, TC).transpose(2, 1, 0).reshape(TC, DM))
    return out
